# version 08
# speedup vs baseline: 1.0873x; 1.0283x over previous
import numpy as np
import ml_dtypes
from collections import defaultdict
from contextlib import ExitStack

import concourse.bass as bass
import concourse.mybir as mybir
from concourse.bass_utils import run_bass_kernel_spmd

F32 = mybir.dt.float32
BF16 = mybir.dt.bfloat16
AF = mybir.ActivationFunctionType
ALU = mybir.AluOpType
AX = mybir.AxisListType.X

T = 2048
NT = T // 512
TC = T // 128
DM = 1024
DC = DM // 128
NCORES = 8
MAGIC = float(np.float32(1.5 * 2 ** 23))

ENGS = ("tensor", "vector", "scalar", "gpsimd", "sync")


class Prog:

    NSLOT = 20

    def __init__(self):
        self.items = {e: [] for e in ENGS}
        self.counts = defaultdict(int)
        self.last_wait = {e: defaultdict(int) for e in ENGS}
        self.sem_names = set()
        self.dma_idx = defaultdict(int)

    def op(self, eng, fn, deps=(), sig=None, n=1):
        waits = []
        for d in deps:
            if d is None:
                continue
            s, v = d
            if self.last_wait[eng][s] >= v:
                continue
            self.last_wait[eng][s] = v
            self.sem_names.add(s)
            waits.append((s, v))
        incs = []
        tok = None
        if sig is not None:
            self.sem_names.add(sig)
            self.counts[sig] += n
            incs.append((sig, n))
            tok = (sig, self.counts[sig])
        self.items[eng].append((fn, waits, incs))
        return tok

    def dma(self, eng, out, in_, deps=(), **kw):
        i = self.dma_idx[eng]
        self.dma_idx[eng] += 1
        sem = f"D_{eng}{i % self.NSLOT}"
        prev = 16 * (i // self.NSLOT)
        deps = list(deps)
        if prev > 0:
            deps.append((sem, prev))
        return self.op(eng, lambda e: e.dma_start(out=out, in_=in_, **kw),
                       deps=deps, sig=sem, n=16)

    def barrier(self):
        toks = []
        for e in ENGS:
            if e in ("vector", "scalar", "gpsimd"):
                toks.append(self.op(e, lambda eng: eng.drain(), sig="B_" + e))
            else:
                toks.append(self.op(e, lambda eng: eng.nop(), sig="B_" + e))
        for e in ENGS:
            self.op(e, None, deps=[t for t in toks if t[0] != "B_" + e])

    def replay(self, block, sems):
        def runner(name):
            def run(engine):
                for fn, waits, incs in self.items[name]:
                    for s, v in waits:
                        engine.wait_ge(sems[s], v)
                    if fn is None:
                        continue
                    inst = fn(engine)
                    for s, nn in incs:
                        inst.then_inc(sems[s], nn)
            return run
        block.tensor(runner("tensor"))
        block.vector(runner("vector"))
        block.scalar(runner("scalar"))
        block.gpsimd(runner("gpsimd"))
        block.sync(runner("sync"))


class Psum:
    def __init__(self, ap):
        self.ap = ap
        self.consumed = None
        self.written = None


class Arena:

    def __init__(self, ap, n):
        self.ap, self.n, self.ptr = ap, n, 0
        self.hi = 0
        self.mark = 0

    def alloc(self, elems, dtype=BF16, align=2):
        nb = elems if dtype == BF16 else 2 * elems
        self.ptr = (self.ptr + align - 1) // align * align
        assert self.ptr + nb <= self.n, f"arena overflow {self.ptr}+{nb}>{self.n}"
        v = self.ap[:, self.ptr:self.ptr + nb]
        if dtype != BF16:
            v = v.bitcast(dtype)
        self.ptr += nb
        self.hi = max(self.hi, self.ptr)
        return v

    def set_mark(self):
        self.mark = self.ptr

    def reset(self):
        self.ptr = self.mark


def mmgroup(P, ps, mms, deps=(), sub=None):
    tok = None
    nmm = len(mms)
    for i, (l, r, o, d) in enumerate(mms):
        first, last = (i == 0), (i == nmm - 1)
        o = (ps.ap if sub is None else sub) if o is None else o
        dd = list(deps) + [d] + ([ps.consumed] if first else [])
        tok = P.op("tensor",
                   lambda t, oo=o, ll=l, rr=r, st=first, sp=last:
                   t.matmul(oo, ll, rr, start=st, stop=sp),
                   deps=dd, sig="S_tensor" if last else None)
    ps.written = tok
    return tok


def emit_layernorm(P, g, xdep=()):
    XT, XN = g["XT"], g["XN"]
    ones = g["ones_sb"]
    SQ = [g["SQ0"], g["SQ1"]]
    ps_s, ps_q = g["psum"][6], g["psum"][7]
    M, MSQ, VT, SD, R, MR = g["M"], g["MSQ"], g["VT"], g["SD"], g["R"], g["MR"]
    stats_prev = None
    done = []
    for tb in range(NT):
        cp_toks = []
        sq_mm = [None, None]
        t1 = t2 = None
        for c in range(DC):
            xt_c = XT[:, c * T + tb * 512: c * T + (tb + 1) * 512]
            xn_c = XN[:, c * T + tb * 512: c * T + (tb + 1) * 512]
            tcp = P.op("vector", lambda v, o=xn_c, i=xt_c: v.tensor_copy(o, i),
                       deps=list(xdep), sig="S_vector")
            cp_toks.append(tcp)
            sq = SQ[c % 2]
            tsq = P.op("scalar", lambda s, o=sq, i=xn_c: s.activation(o, i, AF.Square),
                       deps=[tcp, sq_mm[c % 2]], sig="S_scalar")
            t1 = P.op("tensor", lambda t, i=xn_c, st=(c == 0), sp=(c == DC - 1):
                      t.matmul(ps_s.ap, ones, i, start=st, stop=sp),
                      deps=[tcp] + ([ps_s.consumed, stats_prev] if c == 0 else []),
                      sig="S_tensor" if c == DC - 1 else None)
            t2 = P.op("tensor", lambda t, i=sq, st=(c == 0), sp=(c == DC - 1):
                      t.matmul(ps_q.ap, ones, i, start=st, stop=sp),
                      deps=[tsq] + ([ps_q.consumed] if c == 0 else []),
                      sig="S_tensor")
            sq_mm[c % 2] = t2
        tM = P.op("vector", lambda v: v.tensor_scalar(M, ps_s.ap, 1.0 / DM, None, ALU.mult),
                  deps=[t1, stats_prev], sig="S_vector")
        ps_s.consumed = tM
        tM2 = P.op("scalar", lambda s: s.activation(MSQ, M, AF.Square),
                   deps=[tM], sig="S_scalar")
        tV = P.op("vector", lambda v: v.scalar_tensor_tensor(
            VT, ps_q.ap, 1.0 / DM, MSQ, ALU.mult, ALU.subtract),
            deps=[t2, tM2], sig="S_vector")
        ps_q.consumed = tV
        tSD = P.op("scalar", lambda s: s.activation(SD, VT, AF.Sqrt, bias=g["cc"](1e-5)),
                   deps=[tV], sig="S_scalar")
        tR = P.op("vector", lambda v: v.reciprocal(R, SD), deps=[tSD], sig="S_vector")
        tMR = P.op("vector", lambda v: v.tensor_tensor(MR, M, R, ALU.mult),
                   deps=[tR], sig="S_vector")
        tk = None
        for c in range(DC):
            xn_c = XN[:, c * T + tb * 512: c * T + (tb + 1) * 512]
            eng = "vector" if c % 2 == 0 else "gpsimd"
            tn1 = P.op(eng, lambda v, o=xn_c: v.tensor_tensor(o, o, R, ALU.mult),
                       deps=[tR, t1, t2, cp_toks[c]], sig="S_" + eng)
            tk = P.op(eng, lambda v, o=xn_c: v.tensor_tensor(o, o, MR, ALU.subtract),
                      deps=[tMR, tn1], sig="S_" + eng)
            if tb == NT - 1 and c >= DC - 2:
                done.append(tk)
        stats_prev = tk
    return done


def load_wslabs(P, g, dram_ap, in_dim, out_dim, deps=()):
    KC = (in_dim + 127) // 128
    views, toks = [], []
    for kc in range(KC):
        kw = min(128, in_dim - kc * 128)
        v = g["BA"].alloc(out_dim)
        tk = P.dma("gpsimd", v[0:kw, :], dram_ap[kc * 128: kc * 128 + kw, :], deps=deps)
        views.append(v)
        toks.append(tk)
    return views, toks


def residual_evac(P, g, oc, t0, n, ps, deps, bias_col=None):
    xt = g["XT"][:, oc * T + t0: oc * T + t0 + n]
    if bias_col is not None:
        tk = P.op("vector", lambda v: v.scalar_tensor_tensor(
            xt, ps.ap[:, 0:n], bias_col, xt, ALU.add, ALU.add),
            deps=deps, sig="S_vector")
    else:
        tk = P.op("vector", lambda v: v.tensor_tensor(xt, xt, ps.ap[:, 0:n], ALU.add),
                  deps=deps, sig="S_vector")
    ps.consumed = tk
    return tk


def bcast_cols(P, g, row_ap, n, deps):
    pb = g["psum"][4]
    t1 = P.op("tensor", lambda t: t.matmul(pb.ap[:, 0:n], g["ones1f_sb"], row_ap,
                                           start=True, stop=True),
              deps=list(deps) + [pb.consumed], sig="S_tensor")
    out = g["FA"].alloc(n, F32)
    t2 = P.op("vector", lambda v: v.tensor_copy(out, pb.ap[:, 0:n]),
              deps=[t1], sig="S_vector")
    pb.consumed = t2
    return out, t2


def emit_L1_ffn(P, g):
    W = g["W"]
    xdeps = emit_layernorm(P, g)
    BA, FA = g["BA"], g["FA"]
    XN = g["XN"]
    psum = g["psum"]

    kslab = [FA.alloc(1024, F32) for _ in range(DC)]
    ktok = [P.dma("sync", kslab[kc], W("L1_ffn_kernel")[kc * 128:(kc + 1) * 128, :])
            for kc in range(DC)]
    CM = g["colmax"]
    cm_toks = []
    for kc in range(DC):
        tk = P.op("vector", lambda v, o=CM[:, kc:kc + 1], i=kslab[kc]:
                  v.tensor_reduce(o, i, AX, ALU.max, apply_absolute_value=True),
                  deps=[ktok[kc]], sig="S_vector")
        cm_toks.append(tk)
    t1 = P.op("vector", lambda v: v.tensor_reduce(g["cmax1"], CM, AX, ALU.max),
              deps=cm_toks, sig="S_vector")
    pt = psum[5]
    t2 = P.op("tensor", lambda t: t.transpose(pt.ap[0:1, 0:128], g["cmax1"], g["eye_sb"]),
              deps=[t1, pt.consumed], sig="S_tensor")
    t3 = P.op("vector", lambda v: v.tensor_reduce(
        g["row2"][0:1, 0:1], pt.ap[0:1, 0:128], AX, ALU.max), deps=[t2], sig="S_vector")
    pt.consumed = t3
    t4 = P.op("vector", lambda v: v.reciprocal(g["row2"][0:1, 1:2], g["row2"][0:1, 0:1]),
              deps=[t3], sig="S_vector")
    t5a = P.op("vector", lambda v: v.tensor_scalar(
        g["row2b"][0:1, 0:1], g["row2"][0:1, 1:2], 7.0, None, ALU.mult),
        deps=[t4], sig="S_vector")
    t5b = P.op("vector", lambda v: v.tensor_scalar(
        g["row2b"][0:1, 1:2], g["row2"][0:1, 0:1], 1.0 / 7.0, None, ALU.mult),
        deps=[t3], sig="S_vector")
    qs, tbc = bcast_cols(P, g, g["row2b"][0:1, 0:2], 2, [t5a, t5b])
    ic, sc = qs[:, 0:1], qs[:, 1:2]
    wq_views, wq_toks = [], []
    for kc in range(DC):
        wv = BA.alloc(1024)
        a = kslab[kc]
        tq1 = P.op("vector", lambda v, a=a: v.tensor_scalar(a, a, ic, MAGIC, ALU.mult, ALU.add),
                   deps=[tbc, cm_toks[kc]], sig="S_vector")
        tq2 = P.op("vector", lambda v, a=a: v.tensor_scalar(a, a, MAGIC, -8.0, ALU.subtract, ALU.max),
                   deps=[tq1], sig="S_vector")
        tq3 = P.op("vector", lambda v, a=a, o=wv: v.tensor_scalar(o, a, 7.0, sc, ALU.min, ALU.mult),
                   deps=[tq2], sig="S_vector")
        wq_views.append(wv)
        wq_toks.append(tq3)

    aslab, atoks = load_wslabs(P, g, W("L1_ffn_a"), DM, 16)
    bslab = BA.alloc(1024)
    btok = P.dma("gpsimd", bslab[0:16, :], W("L1_ffn_b"))
    bias_sb = g["FAcol"].alloc(8, F32)
    btok2 = P.dma("gpsimd", bias_sb,
                  W("L1_ffn_bias").rearrange("one (oc p) -> p (one oc)", p=128),
                  allow_slow_non_contiguous=True)

    XA = BA.alloc(T)
    ps_xa = psum[3]
    xa_toks = []
    for tb in range(NT):
        mms = [(aslab[kc][:, 0:16],
                XN[:, kc * T + tb * 512: kc * T + (tb + 1) * 512],
                ps_xa.ap[0:16, 0:512], atoks[kc]) for kc in range(DC)]
        gt = mmgroup(P, ps_xa, mms, deps=xdeps)
        ev = P.op("vector", lambda v, o=XA[0:16, tb * 512:(tb + 1) * 512]:
                  v.tensor_copy(o, ps_xa.ap[0:16, 0:512]), deps=[gt], sig="S_vector")
        ps_xa.consumed = ev
        xa_toks.append(ev)

    for oc in range(DC):
        for tb in range(NT):
            ps = psum[(oc * NT + tb) % 3]
            mms = [(wq_views[kc][:, oc * 128:(oc + 1) * 128],
                    XN[:, kc * T + tb * 512: kc * T + (tb + 1) * 512],
                    None, wq_toks[kc]) for kc in range(DC)]
            mms.append((bslab[0:16, oc * 128:(oc + 1) * 128],
                        XA[0:16, tb * 512:(tb + 1) * 512], None, btok))
            gt = mmgroup(P, ps, mms, deps=list(xdeps) + [xa_toks[tb], btok2])
            residual_evac(P, g, oc, tb * 512, 512, ps, [gt],
                          bias_col=bias_sb[:, oc:oc + 1])


def gating_topk2(P, g, LG8, lg_tok, GW, GWT, E=8):
    fa = g["FAcol"]
    scr = fa.alloc(E, F32)
    scr2 = fa.alloc(E, F32)
    E8 = fa.alloc(E, F32)
    SEL = fa.alloc(E, F32)
    m1 = fa.alloc(1, F32)
    nm1 = fa.alloc(1, F32)
    m2 = fa.alloc(1, F32)
    zz = fa.alloc(1, F32)
    rz = fa.alloc(1, F32)
    ps_g = g["psum"][5]
    gw_toks = []
    gwt_toks = []
    for tci in range(TC):
        tt = P.op("tensor", lambda t, i=LG8[0:8, tci * 128:(tci + 1) * 128]:
                  t.transpose(ps_g.ap[0:128, 0:8], i, g["eye_sb"][0:8, 0:8]),
                  deps=[lg_tok, ps_g.consumed], sig="S_tensor")
        t0 = P.op("vector", lambda v: v.tensor_copy(scr, ps_g.ap[0:128, 0:8]),
                  deps=[tt], sig="S_vector")
        ps_g.consumed = t0
        tm1 = P.op("vector", lambda v: v.tensor_reduce(m1, scr, AX, ALU.max),
                   deps=[t0], sig="S_vector")
        tn = P.op("vector", lambda v: v.tensor_scalar(nm1, m1, -1.0, None, ALU.mult),
                  deps=[tm1], sig="S_vector")
        te = P.op("scalar", lambda s: s.activation(E8, scr, AF.Exp, bias=nm1),
                  deps=[t0, tn], sig="S_scalar")
        teq = P.op("vector", lambda v: v.tensor_scalar(SEL, scr, m1, None, ALU.is_ge),
                   deps=[tm1], sig="S_vector")
        tl2 = P.op("vector", lambda v: v.scalar_tensor_tensor(
            scr2, SEL, g["cc"](-1e30), scr, ALU.mult, ALU.add),
            deps=[teq], sig="S_vector")
        tm2 = P.op("vector", lambda v: v.tensor_reduce(m2, scr2, AX, ALU.max),
                   deps=[tl2], sig="S_vector")
        tsel = P.op("vector", lambda v: v.tensor_scalar(SEL, scr, m2, None, ALU.is_ge),
                    deps=[tm2], sig="S_vector")
        tes = P.op("vector", lambda v: v.tensor_tensor(E8, E8, SEL, ALU.mult),
                   deps=[te, tsel], sig="S_vector")
        tz = P.op("vector", lambda v: v.tensor_reduce(zz, E8, AX, ALU.add),
                  deps=[tes], sig="S_vector")
        trz = P.op("vector", lambda v: v.reciprocal(rz, zz), deps=[tz], sig="S_vector")
        gwc = GW[:, tci * E:(tci + 1) * E]
        tw = P.op("vector", lambda v, o=gwc: v.tensor_scalar(o, E8, rz, None, ALU.mult),
                  deps=[trz], sig="S_vector")
        gw_toks.append(tw)
        tt2 = P.op("tensor", lambda t, i=gwc: t.transpose(
            ps_g.ap[0:8, 0:128], i, g["eye_sb"]),
            deps=[tw, ps_g.consumed], sig="S_tensor")
        tc2 = P.op("vector", lambda v, o=GWT[0:8, tci * 128:(tci + 1) * 128]:
                   v.tensor_copy(o, ps_g.ap[0:8, 0:128]), deps=[tt2], sig="S_vector")
        ps_g.consumed = tc2
        gwt_toks.append(tc2)
    return gw_toks, gwt_toks


def emit_moe(P, g, pref, HID, router_fn, add_xn, TBW):
    E = 8
    HC = HID // 128
    W, GA, XN, psum = g["W"], g["GA"], g["XN"], g["psum"]
    xdeps = emit_layernorm(P, g)
    xres = g["dram"](pref + "_xres", [DM, T])
    acc = g["dram"](pref + "_acc", [T, DM])
    sp_toks = [P.dma("sync", xres[c * 128:(c + 1) * 128, :], g["XT"][:, c * T:(c + 1) * T],
                     deps=list(xdeps)) for c in range(DC)]
    GW = GA.alloc(TC * E, F32)
    GWT = GA.alloc(T)
    ga_save = GA.ptr
    LG8, lg_tok = router_fn(P, g, xdeps)
    gw_toks, gwt_toks = gating_topk2(P, g, LG8, lg_tok, GW, GWT)
    GA.ptr = ga_save
    gate_done = [gw_toks[-1], gwt_toks[-1]]
    b2slab = GA.alloc(DM)
    b2tok = P.dma("gpsimd", b2slab[0:8, :],
                  W(pref + ("_b2" if pref == "L3_ffn" else "_eb2")),
                  deps=gate_done)
    b1col = g["FAcol"].alloc(HC, F32)
    b1name = pref + ("_b1" if pref == "L3_ffn" else "_eb1")
    w1name = pref + ("_W1" if pref == "L3_ffn" else "_ew1")
    w2name = pref + ("_W2" if pref == "L3_ffn" else "_ew2")
    ga_w2 = GA.ptr
    W2space = GA.alloc(HC * 1024)
    h_tb = GA.alloc(HC * TBW)
    stage = [g["M"], g["VT"]]

    NTB = T // TBW
    ntc2 = TBW // 128
    prev_acc = [None] * (2 * TC)
    prev_h_last = None
    prev_out_last = None
    w1_views = [g["XTspace"][:, kc * HID:(kc + 1) * HID] for kc in range(DC)]
    for e in range(E):
        w1_toks = []
        for kc in range(DC):
            tk = P.dma("gpsimd", w1_views[kc],
                       W(w1name)[e:e + 1, kc * 128:(kc + 1) * 128, :],
                       deps=(sp_toks + list(xdeps) if e == 0 else [prev_h_last]))
            w1_toks.append(tk)
        w2_toks = []
        for hc in range(HC):
            tk = P.dma("gpsimd", W2space[:, hc * 1024:(hc + 1) * 1024],
                       W(w2name)[e:e + 1, hc * 128:(hc + 1) * 128, :],
                       deps=(gate_done if e == 0 else [prev_out_last]))
            w2_toks.append(tk)
        tb1 = P.dma("gpsimd", b1col,
                    W(b1name)[e:e + 1, :].rearrange("one (hc p) -> p (one hc)", p=128),
                    deps=[prev_h_last] if e > 0 else [],
                    allow_slow_non_contiguous=True)
        h_toks = {}
        out_last = None
        for tw in range(NTB):
            for hc in range(HC):
                ps_h = psum[hc % 2]
                mms = [(w1_views[kc][:, hc * 128:(hc + 1) * 128],
                        XN[:, kc * T + tw * TBW: kc * T + (tw + 1) * TBW],
                        None, w1_toks[kc]) for kc in range(DC)]
                gt = mmgroup(P, ps_h, mms, deps=list(xdeps), sub=ps_h.ap[:, 0:TBW])
                hv = h_tb[:, hc * TBW:(hc + 1) * TBW]
                ev = P.op("scalar", lambda s, o=hv, i=ps_h.ap[:, 0:TBW], b=b1col[:, hc:hc + 1]:
                          s.activation(o, i, AF.Silu, bias=b), deps=[gt, tb1, out_last],
                          sig="S_scalar")
                ps_h.consumed = ev
                h_toks[hc] = ev
                prev_h_last = gt
            for tc2 in range(ntc2):
                tci = tw * ntc2 + tc2
                for half in range(2):
                    ps_o = psum[2 + half]
                    mms = [(h_tb[:, hc * TBW + tc2 * 128: hc * TBW + (tc2 + 1) * 128],
                            W2space[:, hc * 1024 + half * 512: hc * 1024 + (half + 1) * 512],
                            None, w2_toks[hc]) for hc in range(HC)]
                    gt = mmgroup(P, ps_o, mms,
                                 deps=[h_toks[hc] for hc in range(HC)][-1:] + [h_toks[0]])
                    out_last = gt
                    wcol = GW[:, tci * E + e: tci * E + e + 1]
                    ev = P.op("vector", lambda v, o=stage[half], i=ps_o.ap, w=wcol:
                              v.tensor_scalar(o, i, w, None, ALU.mult),
                              deps=[gt, gw_toks[tci],
                                    prev_acc[2 * tci + half]], sig="S_vector")
                    dk = P.dma("gpsimd",
                               acc[tci * 128:(tci + 1) * 128, half * 512:(half + 1) * 512],
                               stage[half],
                               deps=[ev] + ([prev_acc[2 * tci + half]] if prev_acc[2 * tci + half] else []),
                               **({} if e == 0 else {"accum_op": ALU.add}))
                    prev_acc[2 * tci + half] = dk
                    ps_o.consumed = dk
        prev_out_last = out_last
    rs_toks = [P.dma("sync", g["XT"][:, c * T:(c + 1) * T], xres[c * 128:(c + 1) * 128, :],
                     deps=[prev_h_last]) for c in range(DC)]
    GA.ptr = ga_w2
    ACCS = [GA.alloc(DM, F32), GA.alloc(DM, F32)]
    acc_ld = [None, None]
    for tci in range(TC):
        buf = ACCS[tci % 2]
        lt = P.dma("sync", buf, acc[tci * 128:(tci + 1) * 128, :],
                   deps=[prev_acc[2 * tci], prev_acc[2 * tci + 1], acc_ld[tci % 2],
                         prev_out_last])
        for fc in range(DC):
            ps_f = psum[4 + (fc % 2)]
            t1 = P.op("tensor", lambda t, i=buf[:, fc * 128:(fc + 1) * 128],
                      pf=ps_f.ap[:, 0:128]:
                      t.transpose(pf, i, g["eye_sb"]),
                      deps=[lt, ps_f.consumed], sig="S_tensor")
            t2 = P.op("tensor", lambda t, l=b2slab[0:8, fc * 128:(fc + 1) * 128],
                      r=GWT[0:8, tci * 128:(tci + 1) * 128], pf=ps_f.ap[:, 0:128]:
                      t.matmul(pf, l, r, start=False, stop=True,
                               skip_group_check=True),
                      deps=[t1, b2tok, gwt_toks[tci]], sig="S_tensor")
            xt = g["XT"][:, fc * T + tci * 128: fc * T + (tci + 1) * 128]
            t3 = P.op("vector", lambda v, o=xt, pf=ps_f.ap[:, 0:128]:
                      v.tensor_tensor(o, o, pf, ALU.add),
                      deps=[t2, rs_toks[fc]], sig="S_vector")
            ps_f.consumed = t3
            if add_xn:
                xnv = XN[:, fc * T + tci * 128: fc * T + (tci + 1) * 128]
                t3 = P.op("vector", lambda v, o=xt, i=xnv: v.tensor_tensor(o, o, i, ALU.add),
                          deps=[t3], sig="S_vector")
            acc_ld[tci % 2] = t3 if fc == DC - 1 else acc_ld[tci % 2]


def router_L3(P, g, xdeps):
    W, GA, XN, psum = g["W"], g["GA"], g["XN"], g["psum"]
    slabs, toks = load_wslabs(P, g, W("L3_ffn_gate_w"), DM, 8)
    gbcol = g["FAcol"].alloc(1, F32)
    gbt = P.dma("gpsimd", gbcol[0:8, :],
                W("L3_ffn_gate_b").rearrange("one e -> e one"),
                allow_slow_non_contiguous=True)
    LG8 = GA.alloc(T, F32)
    ps = psum[4]
    tk = None
    for tb in range(NT):
        mms = [(slabs[kc], XN[:, kc * T + tb * 512: kc * T + (tb + 1) * 512],
                ps.ap[0:8, 0:512], toks[kc]) for kc in range(DC)]
        gt = mmgroup(P, ps, mms, deps=list(xdeps))
        tk = P.op("vector", lambda v, o=LG8[0:8, tb * 512:(tb + 1) * 512]:
                  v.tensor_scalar(o, ps.ap[0:8, 0:512], gbcol[0:8, :], None, ALU.add),
                  deps=[gt, gbt], sig="S_vector")
        ps.consumed = tk
    return LG8, tk


def router_L0(P, g, xdeps):
    W, GA, XN, psum = g["W"], g["GA"], g["XN"], g["psum"]
    r1slabs, r1toks = load_wslabs(P, g, W("L0_ffn_rw1"), DM, DM)
    rb1col = g["FAcol"].alloc(DC, F32)
    t_rb1 = P.dma("gpsimd", rb1col,
                  W("L0_ffn_rb1").rearrange("one (oc p) -> p (one oc)", p=128),
                  allow_slow_non_contiguous=True)
    R1 = GA.alloc(DC * T)
    r1_done = []
    for oc in range(DC):
        for tb in range(NT):
            ps = psum[2 + (tb % 2)]
            mms = [(r1slabs[kc][:, oc * 128:(oc + 1) * 128],
                    XN[:, kc * T + tb * 512: kc * T + (tb + 1) * 512],
                    None, r1toks[kc]) for kc in range(DC)]
            gt = mmgroup(P, ps, mms, deps=list(xdeps))
            ev = P.op("scalar", lambda s, o=R1[:, oc * T + tb * 512: oc * T + (tb + 1) * 512],
                      i=ps.ap, b=rb1col[:, oc:oc + 1]:
                      s.activation(o, i, AF.Silu, bias=b), deps=[gt, t_rb1], sig="S_scalar")
            ps.consumed = ev
            r1_done.append(ev)
    r2slabs, r2toks = load_wslabs(P, g, W("L0_ffn_rw2"), DM, 8)
    rb2col = g["FAcol"].alloc(1, F32)
    t_rb2 = P.dma("gpsimd", rb2col[0:8, :],
                  W("L0_ffn_rb2").rearrange("one e -> e one"),
                  allow_slow_non_contiguous=True)
    LG8 = GA.alloc(T, F32)
    ps = psum[4]
    tk = None
    for tb in range(NT):
        mms = [(r2slabs[kc], R1[:, kc * T + tb * 512: kc * T + (tb + 1) * 512],
                ps.ap[0:8, 0:512], r2toks[kc]) for kc in range(DC)]
        gt = mmgroup(P, ps, mms, deps=r1_done[-1:] + [r1_done[tb]])
        tk = P.op("vector", lambda v, o=LG8[0:8, tb * 512:(tb + 1) * 512]:
                  v.tensor_scalar(o, ps.ap[0:8, 0:512], rb2col[0:8, :], None, ALU.add),
                  deps=[gt, t_rb2], sig="S_vector")
        ps.consumed = tk
    return LG8, tk


def transpose_weight(P, g, dram_ap, n_rows, n_cols, deps=()):
    GA, psum = g["GA"], g["psum"]
    RC, CC = n_rows // 128, n_cols // 128
    out = [GA.alloc(n_rows) for _ in range(CC)]
    stg = [GA.alloc(128, F32), GA.alloc(128, F32)]
    stok = [None, None]
    ps = psum[5]
    k = 0
    for dc in range(CC):
        for rc in range(RC):
            sb = stg[k % 2]
            lt = P.dma("sync", sb, dram_ap[rc * 128:(rc + 1) * 128, dc * 128:(dc + 1) * 128],
                       deps=[stok[k % 2]] + list(deps))
            off = (rc % 4) * 128
            tt = P.op("tensor", lambda t, i=sb, pf=ps.ap[:, off:off + 128]:
                      t.transpose(pf, i, g["eye_sb"]),
                      deps=[lt, ps.consumed if rc % 4 == 0 else None], sig="S_tensor")
            stok[k % 2] = tt
            k += 1
            if rc % 4 == 3 or rc == RC - 1:
                lo = (rc // 4) * 4 * 128
                w = (rc % 4 + 1) * 128
                ev = P.op("vector", lambda v, o=out[dc][:, lo:lo + w],
                          pf=ps.ap[:, 0:w]: v.tensor_copy(o, pf),
                          deps=[tt], sig="S_vector")
                ps.consumed = ev
    return out, ps.consumed


def elu1_evac(P, g, dst, ps, n, deps, scr):
    prev = g.get("_elu_prev")
    t1 = P.op("vector", lambda v, o=scr[0][:, 0:n], i=ps.ap[:, 0:n]:
              v.tensor_scalar(o, i, 0.0, None, ALU.min),
              deps=list(deps) + [prev], sig="S_vector")
    t2 = P.op("scalar", lambda s, o=scr[0][:, 0:n]: s.activation(o, o, AF.Exp),
              deps=[t1, prev], sig="S_scalar")
    t3 = P.op("vector", lambda v, o=scr[1][:, 0:n], i=ps.ap[:, 0:n]:
              v.tensor_scalar(o, i, 0.0, None, ALU.max), deps=deps, sig="S_vector")
    t4 = P.op("vector", lambda v, o=dst, a=scr[0][:, 0:n], b=scr[1][:, 0:n]:
              v.tensor_tensor(o, a, b, ALU.add), deps=[t2, t3], sig="S_vector")
    ps.consumed = t4
    g["_elu_prev"] = t4
    return t4


def emit_L2_attn(P, g):
    W, GA, XN, psum = g["W"], g["GA"], g["XN"], g["psum"]
    H, HD = 16, 64
    xdeps = emit_layernorm(P, g)
    xres = g["dram"]("l2a_xres", [DM, T])
    sp_toks = [P.dma("sync", xres[c * 128:(c + 1) * 128, :], g["XT"][:, c * T:(c + 1) * T],
                     deps=list(xdeps)) for c in range(DC)]
    KTOK = g["XTspace"][:, 0:TC * 1024]
    VTOK = g["XTspace"][:, TC * 1024:2 * TC * 1024]
    scr = [g["SQ0"], g["SQ1"]]
    OUTC = GA.alloc(DC * T)
    ga_post_outc = GA.ptr
    QF = GA.alloc(DC * T)
    ga_save = GA.ptr

    def proj_tokmajor(wname, dst, elu, first_deps):
        WT, wt_done = transpose_weight(P, g, W(wname), DM, DM, deps=first_deps)
        toks = []
        for tc in range(TC):
            for fb in range(2):
                ps = psum[fb]
                mms = [(XN[:, kc * T + tc * 128: kc * T + (tc + 1) * 128],
                        WT[kc][:, fb * 512:(fb + 1) * 512], None, wt_done)
                       for kc in range(DC)]
                gt = mmgroup(P, ps, mms, deps=list(xdeps) + (sp_toks if wname == "L2_attn_Wk" else []))
                dv = dst[:, tc * 1024 + fb * 512: tc * 1024 + (fb + 1) * 512]
                if elu:
                    tk = elu1_evac(P, g, dv, ps, 512, [gt], scr)
                else:
                    tk = P.op("vector", lambda v, o=dv, i=ps.ap: v.tensor_copy(o, i),
                              deps=[gt], sig="S_vector")
                    ps.consumed = tk
                toks.append(tk)
        return toks

    k_toks = proj_tokmajor("L2_attn_Wk", KTOK, True, [])
    GA.ptr = ga_save
    v_toks = proj_tokmajor("L2_attn_Wv", VTOK, False, [k_toks[-1]])
    GA.ptr = ga_save
    WTq, wtq_done = transpose_weight(P, g, W("L2_attn_Wq"), DM, DM, deps=[v_toks[-1]])
    q_toks = []
    for oc in range(DC):
        for tb in range(NT):
            ps = psum[2 + ((oc * NT + tb) % 2)]
            mms = [(WTq[kc][:, oc * 128:(oc + 1) * 128],
                    XN[:, kc * T + tb * 512: kc * T + (tb + 1) * 512],
                    None, wtq_done) for kc in range(DC)]
            gt = mmgroup(P, ps, mms, deps=list(xdeps))
            dv = QF[:, oc * T + tb * 512: oc * T + (tb + 1) * 512]
            q_toks.append(elu1_evac(P, g, dv, ps, 512, [gt], scr))
    GA.ptr = ga_save

    KVBD = GA.alloc(DC * 128)
    kdep = [k_toks[-1], v_toks[-1]]
    kv_toks = []
    zt = P.op("vector", lambda v: v.memset(KVBD, 0.0), deps=[], sig="S_vector")
    for c in range(DC):
        toks2 = []
        banks = [psum[c % 2], psum[2 + c % 2]]
        for par in range(2):
            h = 2 * c + par
            ps = banks[par]
            mms = [(KTOK[:, tc * 1024 + c * 128: tc * 1024 + (c + 1) * 128],
                    VTOK[:, tc * 1024 + h * 64: tc * 1024 + (h + 1) * 64],
                    ps.ap[:, 0:64], None) for tc in range(TC)]
            toks2.append(mmgroup(P, ps, mms, deps=kdep))
        c1 = P.op("vector", lambda v, o=KVBD[0:64, c * 128: c * 128 + 64],
                  pf=banks[0].ap[0:64, 0:64]: v.tensor_copy(o, pf),
                  deps=[toks2[0], zt], sig="S_vector")
        c2 = P.op("vector", lambda v, o=KVBD[64:128, c * 128 + 64: c * 128 + 128],
                  pf=banks[1].ap[64:128, 0:64]: v.tensor_copy(o, pf),
                  deps=[toks2[1]], sig="S_vector")
        banks[0].consumed = c1
        banks[1].consumed = c2
        kv_toks.append(c2)

    KS = GA.alloc(1024, F32)
    ks_toks = []
    for fb in range(2):
        ps = psum[2 + fb]
        mms = [(g["ones_sb"], KTOK[:, tc * 1024 + fb * 512: tc * 1024 + (fb + 1) * 512],
                None, None) for tc in range(TC)]
        gt = mmgroup(P, ps, mms, deps=kdep)
        ev = P.op("vector", lambda v, o=KS[:, fb * 512:(fb + 1) * 512], i=ps.ap:
                  v.tensor_copy(o, i), deps=[gt], sig="S_vector")
        ps.consumed = ev
        ks_toks.append(ev)
    DKL = GA.alloc(16 * DC)
    tz = P.op("vector", lambda v: v.memset(DKL, 0.0), deps=[], sig="S_vector")
    dkl_toks = []
    ps5 = psum[5]
    for c in range(DC):
        tt = P.op("tensor", lambda t, i=KS[:, c * 128:(c + 1) * 128],
                  pf=ps5.ap[:, 0:128]: t.transpose(pf, i, g["eye_sb"]),
                  deps=[ks_toks[-1], ps5.consumed], sig="S_tensor")
        h0, h1 = 2 * c, 2 * c + 1
        c1 = P.op("vector", lambda v, o=DKL[0:64, c * 16 + h0: c * 16 + h0 + 1],
                  pf=ps5.ap[0:64, 0:1]: v.tensor_copy(o, pf), deps=[tt, tz], sig="S_vector")
        c2 = P.op("vector", lambda v, o=DKL[64:128, c * 16 + h1: c * 16 + h1 + 1],
                  pf=ps5.ap[64:128, 0:1]: v.tensor_copy(o, pf), deps=[tt], sig="S_vector")
        ps5.consumed = c2
        dkl_toks.append(c2)
    DRCPB = GA.alloc(T)
    den_toks = []
    for tb in range(NT):
        ps = psum[4]
        mms = [(DKL[:, c * 16:(c + 1) * 16],
                QF[:, c * T + tb * 512: c * T + (tb + 1) * 512],
                ps.ap[0:16, 0:512], dkl_toks[c]) for c in range(DC)]
        gt = mmgroup(P, ps, mms, deps=[q_toks[-1]])
        t1 = P.op("vector", lambda v, pf=ps.ap[0:16, 0:512]:
                  v.tensor_scalar(pf, pf, 1e-6, None, ALU.add), deps=[gt], sig="S_vector")
        t2 = P.op("vector", lambda v, o=g["M"][0:16, 0:512], pf=ps.ap[0:16, 0:512]:
                  v.reciprocal(o, pf), deps=[t1], sig="S_vector")
        t3 = P.op("vector", lambda v, o=DRCPB[0:16, tb * 512:(tb + 1) * 512],
                  i=g["M"][0:16, 0:512]: v.tensor_copy(o, i), deps=[t2], sig="S_vector")
        ps.consumed = t3
        den_toks.append(t3)
    HM = g["hmask_sb"]
    RB = GA.alloc(512, F32)
    num_toks = []
    prev_ev = None
    for c in range(DC):
        for tb in range(NT):
            ps_rb = psum[4]
            tb1 = P.op("tensor", lambda t, l=HM[0:16, c * 128:(c + 1) * 128],
                       r=DRCPB[0:16, tb * 512:(tb + 1) * 512], pf=ps_rb.ap:
                       t.matmul(pf, l, r, start=True, stop=True),
                       deps=[den_toks[tb], ps_rb.consumed], sig="S_tensor")
            tb2 = P.op("vector", lambda v, o=RB, pf=ps_rb.ap: v.tensor_copy(o, pf),
                       deps=[tb1, prev_ev], sig="S_vector")
            ps_rb.consumed = tb2
            ps_n = psum[(c * NT + tb) % 2]
            t2 = P.op("tensor", lambda t, ll=KVBD[:, c * 128:(c + 1) * 128],
                      rr=QF[:, c * T + tb * 512: c * T + (tb + 1) * 512],
                      pf=ps_n.ap: t.matmul(pf, ll, rr, start=True, stop=True),
                      deps=[kv_toks[c], q_toks[-1], ps_n.consumed], sig="S_tensor")
            ev = P.op("vector", lambda v, o=OUTC[:, c * T + tb * 512: c * T + (tb + 1) * 512],
                      a=RB, pf=ps_n.ap: v.tensor_tensor(o, pf, a, ALU.mult),
                      deps=[t2, tb2], sig="S_vector")
            ps_n.consumed = ev
            prev_ev = ev
            num_toks.append(ev)
    rs_toks = [P.dma("sync", g["XT"][:, c * T:(c + 1) * T], xres[c * 128:(c + 1) * 128, :],
                     deps=[num_toks[-1]]) for c in range(DC)]
    GA.ptr = ga_post_outc
    WoT, wo_done = transpose_weight(P, g, W("L2_attn_Wo"), DM, DM,
                                    deps=[num_toks[-1]])
    for oc in range(DC):
        for tb in range(NT):
            ps = psum[2 + ((oc * NT + tb) % 2)]
            mms = [(WoT[kc][:, oc * 128:(oc + 1) * 128],
                    OUTC[:, kc * T + tb * 512: kc * T + (tb + 1) * 512],
                    None, wo_done) for kc in range(DC)]
            gt = mmgroup(P, ps, mms, deps=[num_toks[-1]])
            residual_evac(P, g, oc, tb * 512, 512, ps, [gt, rs_toks[oc]])


def emit_L3_attn(P, g):
    W, GA, XN, psum = g["W"], g["GA"], g["XN"], g["psum"]
    H = 16
    xdeps = emit_layernorm(P, g)
    xres = g["dram"]("l3a_xres", [DM, T])
    ydel = g["dram"]("l3a_delta", [DM, T], BF16)
    sp_toks = [P.dma("sync", xres[c * 128:(c + 1) * 128, :], g["XT"][:, c * T:(c + 1) * T],
                     deps=list(xdeps)) for c in range(DC)]
    KCOM = g["XTspace"]
    VTOK = g["XNspace"]
    kz = P.op("vector", lambda v: v.memset(KCOM, 0.0), deps=sp_toks + list(xdeps),
              sig="S_vector")
    HT = T // 2
    g["cos4_sb"] = GA.alloc(T)
    g["sin4_sb"] = GA.alloc(T)
    g["_cs_tok"] = P.dma("sync", g["cos4_sb"], g["c_cos4"])
    g["_cs_tok2"] = P.dma("sync", g["sin4_sb"], g["c_sin4"])
    QCOM = GA.alloc(H * HT)
    qz = P.op("gpsimd", lambda v: v.memset(QCOM, 0.0), deps=[], sig="S_gpsimd")
    ga0 = GA.ptr
    QC = GA.alloc(4 * T)
    ga_qc = GA.ptr
    KVN = GA.alloc(2 * T)
    ga1 = GA.ptr
    wkv, wkv_t = load_wslabs(P, g, W("L3_attn_W_kv"), DM, 256)
    gcol = g["FAcol"].alloc(2, F32)
    gct = P.dma("gpsimd", gcol, W("L3_attn_g_kv").rearrange("one (c p) -> p (one c)", p=128),
                allow_slow_non_contiguous=True)
    M, VT, SD, R = g["M"], g["VT"], g["SD"], g["R"]
    SQ = [g["SQ0"], g["SQ1"]]
    for tb in range(NT):
        sq_mm = None
        raw_toks = []
        for oc in range(2):
            ps = psum[oc]
            kvslot = KVN[:, oc * T + tb * 512: oc * T + (tb + 1) * 512]
            mms = [(wkv[kc][:, oc * 128:(oc + 1) * 128],
                    XN[:, kc * T + tb * 512: kc * T + (tb + 1) * 512],
                    None, wkv_t[kc]) for kc in range(DC)]
            gt = mmgroup(P, ps, mms, deps=list(xdeps))
            t1 = P.op("vector", lambda v, o=SQ[oc], pf=ps.ap: v.tensor_copy(o, pf),
                      deps=[gt, g.get("_kvn_prev")], sig="S_vector")
            ps.consumed = t1
            raw_toks.append(t1)
            t2 = P.op("scalar", lambda s, o=kvslot, i=SQ[oc]: s.activation(o, i, AF.Square),
                      deps=[t1], sig="S_scalar")
            sq_mm = P.op("tensor", lambda t, i=kvslot, st=(oc == 0), sp=(oc == 1):
                         t.matmul(psum[2].ap, g["ones_sb"], i, start=st, stop=sp),
                         deps=[t2] + ([psum[2].consumed] if oc == 0 else []),
                         sig="S_tensor")
        t4 = P.op("vector", lambda v: v.tensor_scalar(VT, psum[2].ap, 1.0 / 256, None, ALU.mult),
                  deps=[sq_mm], sig="S_vector")
        psum[2].consumed = t4
        t5 = P.op("scalar", lambda s: s.activation(SD, VT, AF.Sqrt, bias=g["cc"](1e-6)),
                  deps=[t4], sig="S_scalar")
        t6 = P.op("vector", lambda v: v.reciprocal(R, SD), deps=[t5], sig="S_vector")
        for oc in range(2):
            kvslot = KVN[:, oc * T + tb * 512: oc * T + (tb + 1) * 512]
            tk = P.op("vector", lambda v, o=kvslot, i=SQ[oc], gc=gcol[:, oc:oc + 1]:
                      v.scalar_tensor_tensor(o, i, gc, R, ALU.mult, ALU.mult),
                      deps=[t6, gct, sq_mm], sig="S_vector")
            g["_kvn_prev"] = tk
    kvn_done = g["_kvn_prev"]
    GA.ptr = ga1
    wq, wq_t = load_wslabs(P, g, W("L3_attn_W_q"), DM, 512, deps=[kvn_done])
    qc_done = None
    for oc in range(4):
        for tb in range(NT):
            ps = psum[3 + (oc % 2)]
            mms = [(wq[kc][:, oc * 128:(oc + 1) * 128],
                    XN[:, kc * T + tb * 512: kc * T + (tb + 1) * 512],
                    None, wq_t[kc]) for kc in range(DC)]
            gt = mmgroup(P, ps, mms, deps=list(xdeps))
            qc_done = P.op("vector", lambda v, o=QC[:, oc * T + tb * 512: oc * T + (tb + 1) * 512],
                           pf=ps.ap: v.tensor_copy(o, pf), deps=[gt], sig="S_vector")
            ps.consumed = qc_done
    GA.ptr = ga1
    KR = GA.alloc(4 * T)
    wkr, wkr_t = load_wslabs(P, g, W("L3_attn_W_krope"), DM, 512, deps=[qc_done])
    kr_done = None
    for oc in range(4):
        for tb in range(NT):
            ps = psum[3 + (oc % 2)]
            mms = [(wkr[kc][:, oc * 128:(oc + 1) * 128],
                    XN[:, kc * T + tb * 512: kc * T + (tb + 1) * 512],
                    None, wkr_t[kc]) for kc in range(DC)]
            gt = mmgroup(P, ps, mms, deps=list(xdeps))
            kr_done = P.op("vector", lambda v, o=KR[:, oc * T + tb * 512: oc * T + (tb + 1) * 512],
                           pf=ps.ap: v.tensor_copy(o, pf), deps=[gt], sig="S_vector")
            ps.consumed = kr_done
    kr_rope = emit_rope(P, g, KR, 4 * NT, None, kr_done)
    sc_done = []
    for h in range(H):
        src = KR[:, (h // 4) * T: (h // 4 + 1) * T][(h % 4) * 32:(h % 4) * 32 + 32, :]
        dst = KCOM[:, h * T:(h + 1) * T][64:96, :]
        sc_done.append(P.op("vector", lambda v, o=dst, i=src: v.tensor_copy(o, i),
                            deps=[kr_rope, kz], sig="S_vector"))
    GA.ptr = ga1
    wkd, wkd_t = load_wslabs(P, g, W("L3_attn_W_kdec"), 256, 1024,
                             deps=[sc_done[-1]])
    for oc in range(DC):
        for tb in range(NT):
            ps = psum[3 + (oc % 2)]
            mms = [(wkd[kc][:, oc * 128:(oc + 1) * 128],
                    KVN[:, kc * T + tb * 512: kc * T + (tb + 1) * 512],
                    None, wkd_t[kc]) for kc in range(2)]
            gt = mmgroup(P, ps, mms, deps=[kvn_done])
            for par in range(2):
                h = 2 * oc + par
                dst = KCOM[:, h * T:(h + 1) * T][0:64, tb * 512:(tb + 1) * 512]
                tk = P.op("vector", lambda v, o=dst, pf=ps.ap[par * 64:(par + 1) * 64, :]:
                          v.tensor_copy(o, pf), deps=[gt, kz], sig="S_vector")
            ps.consumed = tk
            sc_done.append(tk)
    wvd, wvd_t = load_wslabs(P, g, W("L3_attn_W_vdec"), 256, 1024,
                             deps=[sc_done[-1]])
    v_done = []
    for tc in range(TC):
        for fb in range(2):
            ps = psum[3 + fb]
            mms = [(KVN[:, kc * T + tc * 128: kc * T + (tc + 1) * 128],
                    wvd[kc][:, fb * 512:(fb + 1) * 512], None, wvd_t[kc])
                   for kc in range(2)]
            gt = mmgroup(P, ps, mms, deps=[kvn_done])
            tk = P.op("vector", lambda v, o=VTOK[:, tc * 1024 + fb * 512: tc * 1024 + (fb + 1) * 512],
                      pf=ps.ap: v.tensor_copy(o, pf), deps=[gt], sig="S_vector")
            ps.consumed = tk
            v_done.append(tk)
    GA.ptr = ga_qc
    QSCALE = 1.0 / np.sqrt(96.0)
    stg = GA.alloc(512)
    EXPB = GA.alloc(T)
    ATB = GA.alloc(T)
    YH = GA.alloc(DM)
    cols = GA.alloc(16, F32)
    g["l3racc"] = g["FAcol"].alloc(8, F32)
    ga2 = GA.ptr
    for half in range(2):
        hsl = slice(half * HT, (half + 1) * HT)
        GA.ptr = ga2
        wo_prev = [g.get("_wo_last"), v_done[-1], sc_done[-1]]
        wqd, wqd_t = load_wslabs(P, g, W("L3_attn_W_qdec"), 512, 1024, deps=wo_prev)
        wqr, wqr_t = load_wslabs(P, g, W("L3_attn_W_qrope"), 512, 512, deps=wo_prev)
        QR = GA.alloc(2 * T)
        qcd = []
        for oc in range(DC):
            for tb in range(2):
                ps = psum[3 + (tb % 2)]
                mms = [(wqd[kc][:, oc * 128:(oc + 1) * 128],
                        QC[:, kc * T + half * HT + tb * 512: kc * T + half * HT + (tb + 1) * 512],
                        None, wqd_t[kc]) for kc in range(4)]
                gt = mmgroup(P, ps, mms, deps=[qc_done])
                for par in range(2):
                    h = 2 * oc + par
                    dst = QCOM[:, h * HT:(h + 1) * HT][0:64, tb * 512:(tb + 1) * 512]
                    tk = P.op("vector", lambda v, o=dst, pf=ps.ap[par * 64:(par + 1) * 64, :]:
                              v.tensor_scalar(o, pf, QSCALE, None, ALU.mult),
                              deps=[gt, qz], sig="S_vector")
                ps.consumed = tk
                qcd.append(tk)
        qr_d = None
        for oc in range(4):
            for tb in range(2):
                ps = psum[3 + (tb % 2)]
                mms = [(wqr[kc][:, oc * 128:(oc + 1) * 128],
                        QC[:, kc * T + half * HT + tb * 512: kc * T + half * HT + (tb + 1) * 512],
                        None, wqr_t[kc]) for kc in range(4)]
                gt = mmgroup(P, ps, mms, deps=[qc_done])
                qr_d = P.op("vector", lambda v, o=QR[:, oc * HT + tb * 512: oc * HT + (tb + 1) * 512],
                            pf=ps.ap: v.tensor_copy(o, pf), deps=[gt], sig="S_vector")
                ps.consumed = qr_d
        qr_rope = emit_rope(P, g, QR, 4 * 2, half * HT, qr_d)
        qs_done = []
        for h in range(H):
            src = QR[:, (h // 4) * HT: (h // 4 + 1) * HT][(h % 4) * 32:(h % 4) * 32 + 32, :]
            dst = QCOM[:, h * HT:(h + 1) * HT][64:96, :]
            qs_done.append(P.op("vector", lambda v, o=dst, i=src:
                                v.tensor_scalar(o, i, QSCALE, None, ALU.mult),
                                deps=[qr_rope, qz], sig="S_vector"))
        qready = [qcd[-1], qs_done[-1]]
        GA.ptr = ga2
        wo, wo_t = load_wslabs(P, g, W("L3_attn_W_o"), DM, DM, deps=qready)
        EXPB2 = [EXPB, g["scr2k"]]
        itersA = [(tcl, h) for tcl in range(TC // 2) for h in range(H)]
        stA = {}

        def frontA(k):
            tcl, h = itersA[k]
            pp = k % 2
            tci = half * (TC // 2) + tcl
            s_end = (tci + 1) * 128
            NB = (s_end + 511) // 512
            bank_toks = []
            for bb in range(NB):
                w = min(512, s_end - bb * 512)
                ps = psum[bb % 4]
                mms = [(QCOM[:, h * HT + tcl * 128: h * HT + (tcl + 1) * 128],
                        KCOM[:, h * T + bb * 512: h * T + bb * 512 + w],
                        ps.ap[:, 0:w], None)]
                gt = mmgroup(P, ps, mms, deps=qready + [sc_done[-1], kz])
                if bb == NB - 1:
                    off = tci * 128 - bb * 512
                    gt = P.op("vector", lambda v, pf=ps.ap[:, off:off + 128]:
                              v.tensor_tensor(pf, pf, g["maskf_sb"], ALU.add),
                              deps=[gt], sig="S_vector")
                bank_toks.append((ps, gt, w))
            mx = []
            for bb, (ps, gt, w) in enumerate(bank_toks):
                tk = P.op("vector", lambda v, o=cols[:, pp * 8 + bb:pp * 8 + bb + 1],
                          pf=ps.ap[:, 0:w]:
                          v.tensor_reduce(o, pf, AX, ALU.max), deps=[gt], sig="S_vector")
                mx.append(tk)
            tm = P.op("vector", lambda v, o=cols[:, pp * 8 + 4:pp * 8 + 5],
                      i=cols[:, pp * 8:pp * 8 + NB]:
                      v.tensor_reduce(o, i, AX, ALU.max), deps=mx, sig="S_vector")
            tnm = P.op("vector", lambda v, o=cols[:, pp * 8 + 5:pp * 8 + 6],
                       i=cols[:, pp * 8 + 4:pp * 8 + 5]:
                       v.tensor_scalar(o, i, -1.0, None, ALU.mult), deps=[tm], sig="S_vector")
            stA[k] = (bank_toks, tnm)

        def middleA(k):
            tcl, h = itersA[k]
            pp = k % 2
            tci = half * (TC // 2) + tcl
            s_end = (tci + 1) * 128
            NB = (s_end + 511) // 512
            EXPB_ = EXPB2[pp]
            bank_toks, tnm = stA.pop(k)
            rs = []
            for bb, (ps, gt, w) in enumerate(bank_toks):
                tk = P.op("scalar", lambda s, o=EXPB_[:, bb * 512: bb * 512 + w],
                          pf=ps.ap[:, 0:w], b=cols[:, pp * 8 + 5:pp * 8 + 6],
                          a=g["l3racc"][:, pp * 4 + bb:pp * 4 + bb + 1]:
                          s.activation(o, pf, AF.Exp, bias=b, accum_out=a),
                          deps=[tnm, gt, g.get("_l3_eb%d" % pp)], sig="S_scalar")
                ps.consumed = tk
                rs.append(tk)
            tz2 = P.op("vector", lambda v, o=cols[:, pp * 8 + 6:pp * 8 + 7],
                       i=g["l3racc"][:, pp * 4:pp * 4 + NB]:
                       v.tensor_reduce(o, i, AX, ALU.add), deps=rs, sig="S_vector")
            trr = P.op("vector", lambda v, o=cols[:, pp * 8 + 7:pp * 8 + 8],
                       i=cols[:, pp * 8 + 6:pp * 8 + 7]:
                       v.reciprocal(o, i), deps=[tz2], sig="S_vector")
            tn = P.op("vector", lambda v, o=EXPB_[:, 0:s_end],
                      rr=cols[:, pp * 8 + 7:pp * 8 + 8]:
                      v.tensor_scalar(o, o, rr, None, ALU.mult), deps=[trr] + rs,
                      sig="S_vector")
            stA[("n", k)] = tn

        def backA(k):
            tcl, h = itersA[k]
            pp = k % 2
            tci = half * (TC // 2) + tcl
            s_end = (tci + 1) * 128
            EXPB_ = EXPB2[pp]
            tn = stA.pop(("n", k))
            ps_t = psum[4]
            ps_y = psum[5]
            nsc = (s_end + 127) // 128
            ev = None
            for scb in range(0, nsc, 4):
                nn = min(4, nsc - scb)
                tts = []
                for j in range(nn):
                    tt = P.op("tensor", lambda t, i=EXPB_[:, (scb + j) * 128:(scb + j + 1) * 128],
                              pf=ps_t.ap.bitcast(BF16)[:, j * 128:(j + 1) * 128]:
                              t.transpose(pf, i, g["eyeb_sb"]),
                              deps=[tn, ps_t.consumed if j == 0 else None], sig="S_tensor")
                    tts.append(tt)
                ev = P.op("vector", lambda v, o=ATB[:, scb * 128: scb * 128 + nn * 128],
                          pf=ps_t.ap.bitcast(BF16)[:, 0:nn * 128]: v.tensor_copy(o, pf),
                          deps=[tts[-1]], sig="S_vector")
                ps_t.consumed = ev
            g["_l3_eb%d" % pp] = ev
            mms = [(VTOK[:, sc * 1024 + h * 64: sc * 1024 + (h + 1) * 64],
                    ATB[:, sc * 128:(sc + 1) * 128],
                    ps_y.ap[0:64, 0:128], ev) for sc in range(nsc)]
            gt = mmgroup(P, ps_y, mms, deps=[ev, v_done[-1]])
            yev = P.op("vector", lambda v, o=YH[(h % 2) * 64:(h % 2) * 64 + 64,
                                                (h // 2) * 128:(h // 2) * 128 + 128],
                       pf=ps_y.ap[0:64, 0:128]: v.tensor_copy(o, pf),
                       deps=[gt, g.get("_l3_yh")], sig="S_vector")
            ps_y.consumed = yev
            g["_yh%d" % h] = yev
            if h == H - 1:
                wo_emit(tci, YH)

        def wo_emit(tci, YH):
            ydeps = [g["_yh%d" % hh] for hh in range(H)]
            for oc in range(DC):
                ps = psum[6]
                mms2 = [(wo[kc][:, oc * 128:(oc + 1) * 128],
                         YH[:, kc * 128:(kc + 1) * 128], None, wo_t[kc]) for kc in range(DC)]
                gt2 = mmgroup(P, ps, mms2, deps=ydeps, sub=ps.ap[:, 0:128])
                ev2 = P.op("vector", lambda v, o=stg[:, 0:128], pf=ps.ap[:, 0:128]:
                           v.tensor_copy(o, pf), deps=[gt2, g.get("_stg_guard")],
                           sig="S_vector")
                dk = P.dma("sync", ydel[oc * 128:(oc + 1) * 128, tci * 128:(tci + 1) * 128],
                           stg[:, 0:128], deps=[ev2])
                ps.consumed = dk
                g["_stg_guard"] = dk
                g["_wo_last"] = gt2
                g["_l3_yh"] = gt2

        NA = len(itersA)
        for k in range(NA + 1):
            if k < NA:
                frontA(k)
            if k >= 1:
                backA(k - 1)
            if k < NA:
                middleA(k)
    last_h = [g["_yh%d" % h] for h in range(H)]
    rs_toks = [P.dma("sync", g["XT"][:, c * T:(c + 1) * T], xres[c * 128:(c + 1) * 128, :],
                     deps=last_h + [g["_stg_guard"]]) for c in range(DC)]
    GA.ptr = ga2
    DB = GA.alloc(2 * T)
    db_ld = [None, None]
    for c in range(DC):
        buf = DB[:, (c % 2) * T:(c % 2 + 1) * T]
        lt = P.dma("sync", buf, ydel[c * 128:(c + 1) * 128, :],
                   deps=[g["_stg_guard"], db_ld[c % 2]])
        tk = P.op("vector", lambda v, o=g["XT"][:, c * T:(c + 1) * T], i=buf:
                  v.tensor_tensor(o, o, i, ALU.add), deps=[lt, rs_toks[c]], sig="S_vector")
        db_ld[c % 2] = tk


def _bfly_coeff(P, g, wap, stage, kind, dst, deps=()):
    s = 2 ** stage
    toks = []
    pitch = dst.tensor.shape[1]
    doff = dst.offset
    if stage <= 6:
        q_cnt = 128 // (2 * s)
        base = stage * 8192
        if kind == "diag":
            bdim, boff = [3, 2], 0
        elif kind == "part":
            bdim, boff = [1, 2], 1
        else:
            bdim, boff = [-1, 2], 2
        dims = ([[4 * s, q_cnt]] if q_cnt > 1 else []) + [bdim] + \
            ([[4, s]] if s > 1 else []) + [[256, 32]]
        odims = ([[pitch * 2 * s, q_cnt]] if q_cnt > 1 else []) + \
            [[pitch * s, 2]] + ([[pitch, s]] if s > 1 else []) + [[1, 32]]
        _peel_dma(P, toks, wap.tensor, base + boff, dims,
                  dst.tensor, doff, odims, deps)
    else:
        sb = stage - 7
        lo_cnt = 2 ** sb
        hi_cnt = 32 // (2 * lo_cnt)
        base = stage * 8192
        boff, bstep = {"diag": (0, 3), "part": (1, 1), "partT": (2, -1)}[kind]
        dims = ([[s * 4, hi_cnt]] if hi_cnt > 1 else []) + [[bstep, 2]] + \
            ([[512, lo_cnt]] if lo_cnt > 1 else [])
        odims = ([[2 * lo_cnt, hi_cnt]] if hi_cnt > 1 else []) + [[lo_cnt, 2]] + \
            ([[1, lo_cnt]] if lo_cnt > 1 else [])
        _peel_dma(P, toks, wap.tensor, base + boff, dims,
                  dst.tensor, doff, odims, deps, lead=([4, 128], [pitch, 128]))
    return toks


def _peel_dma(P, toks, it, ioff, idims, ot, ooff, odims, deps, lead=None):
    maxd = 1 if lead else 2
    if len(idims) <= maxd:
        ii = ([list(lead[0])] if lead else []) + [list(d) for d in idims]
        oo = ([list(lead[1])] if lead else []) + [list(d) for d in odims]
        eng = ("gpsimd", "sync", "scalar")[len(toks) % 3]
        toks.append(P.dma(eng, bass.AP(ot, ooff, oo), bass.AP(it, ioff, ii),
                          deps=deps, allow_slow_non_contiguous=True))
        return
    (istep, cnt), (ostep, cnt2) = idims[0], odims[0]
    assert cnt == cnt2
    for k in range(cnt):
        _peel_dma(P, toks, it, ioff + k * istep, idims[1:],
                  ot, ooff + k * ostep, odims[1:], deps, lead)


def emit_L2_ffn(P, g):
    W, GA, XN, psum = g["W"], g["GA"], g["XN"], g["psum"]
    xdeps = emit_layernorm(P, g)
    xres = g["dram"]("l2f_xres", [DM, T])
    wexp2 = g["dram"]("l2f_wexp2", [DM, 4096], BF16)
    wcon2 = g["dram"]("l2f_wcon2", [4096, DM], BF16)
    sp_toks = [P.dma("sync", xres[c * 128:(c + 1) * 128, :], g["XT"][:, c * T:(c + 1) * T],
                     deps=list(xdeps)) for c in range(DC)]
    upw, dww = W("L2_ffn_up_weights"), W("L2_ffn_down_weights")
    CO = {}
    co_t = {}
    for wap, tag in ((upw, "u"), (dww, "d")):
        for st in range(12):
            if st < 6:
                kinds = ("diag", "part")
            elif st == 6:
                kinds = ("diag", "part", "partT")
            elif tag == "d":
                kinds = ("diag", "partT")
            else:
                kinds = ("diag", "part")
            for kind in kinds:
                buf = GA.alloc(32, F32)
                co_t[(tag, st, kind)] = _bfly_coeff(P, g, wap, st, kind, buf)
                CO[(tag, st, kind)] = buf
    LT = {"u": GA.alloc(32 * 128), "d": GA.alloc(32 * 128)}
    becol = GA.alloc(32, F32)
    B1 = GA.alloc(32, F32)
    GB = GA.alloc(32, F32)
    bcol = GA.alloc(8, F32)
    bcol2 = GA.alloc(8, F32)
    ga_fw = GA.ptr
    St2 = [GA.alloc(128), GA.alloc(128)]
    Rt_ps2 = [psum[5], psum[6]]
    chain_done = {}
    prev_ev = None
    ccnt = 0
    for tag in ("u", "d"):
        for c in range(32):
            St = St2[ccnt % 2]
            Rt_ps = Rt_ps2[ccnt % 2]
            ccnt += 1
            rt = LT[tag][:, c * 128:(c + 1) * 128]
            t1 = P.op("vector", lambda v, o=rt, a=CO[(tag, 6, "diag")][:, c:c + 1]:
                      v.tensor_scalar(o, g["eyeb_sb"], a, None, ALU.mult),
                      deps=co_t[(tag, 6, "diag")] + [prev_ev], sig="S_vector")
            t2 = P.op("vector", lambda v, o=rt, pm=g["pmask_sb"][:, 6 * 128:7 * 128],
                      a=CO[(tag, 6, "partT")][:, c:c + 1]:
                      v.scalar_tensor_tensor(o, pm, a, o, ALU.mult, ALU.add),
                      deps=[t1] + co_t[(tag, 6, "partT")], sig="S_vector")
            prev = t2
            for k in range(5, -1, -1):
                t3 = P.op("vector", lambda v, a=CO[(tag, k, "diag")][:, c:c + 1]:
                          v.tensor_scalar(St, g["eyeb_sb"], a, None, ALU.mult),
                          deps=co_t[(tag, k, "diag")] + [g.get("_l2_st%d" % (ccnt % 2))], sig="S_vector")
                t4 = P.op("vector", lambda v, pm=g["pmask_sb"][:, k * 128:(k + 1) * 128],
                          a=CO[(tag, k, "part")][:, c:c + 1]:
                          v.scalar_tensor_tensor(St, pm, a, St, ALU.mult, ALU.add),
                          deps=[t3] + co_t[(tag, k, "part")], sig="S_vector")
                t5 = P.op("tensor", lambda t, r=rt, pf=Rt_ps.ap[:, 0:128]:
                          t.matmul(pf, St, r, start=True, stop=True),
                          deps=[t4, prev, Rt_ps.consumed], sig="S_tensor")
                g["_l2_st%d" % (ccnt % 2)] = t5
                prev = P.op("vector", lambda v, o=rt, pf=Rt_ps.ap[:, 0:128]:
                            v.tensor_copy(o, pf), deps=[t5], sig="S_vector")
                Rt_ps.consumed = prev
            chain_done[(tag, c)] = prev
            prev_ev = prev
    W1T = g["XTspace"]
    wstage2 = [GA.alloc(4096, F32), GA.alloc(4096, F32)]
    TMPW = GA.alloc(2048)
    TMP2W = GA.alloc(2048)
    tile2 = [GA.alloc(128), GA.alloc(128)]
    last_mm = None
    mm_tok = {}
    st_tok = [None, None]
    for kc in range(DC):
        wstage = wstage2[kc % 2]
        st_tok[kc % 2] = P.dma("sync", wstage, W("L2_ffn_W_exp")[kc * 128:(kc + 1) * 128, :],
                               deps=sp_toks + [st_tok[kc % 2], mm_tok.get(kc - 2)])
        for c in range(32):
            cp = c % 2
            tile = tile2[cp]
            ps_t = psum[0 + cp]
            tt = P.op("tensor", lambda t, i=wstage[:, c * 128:(c + 1) * 128],
                      pf=ps_t.ap[:, 0:128]: t.transpose(pf, i, g["eye_sb"]),
                      deps=[st_tok[kc % 2], ps_t.consumed], sig="S_tensor")
            te = P.op("vector", lambda v, o=tile, pf=ps_t.ap[:, 0:128]: v.tensor_copy(o, pf),
                      deps=[tt, g.get("_l2_tile%d" % cp)], sig="S_vector")
            ps_t.consumed = te
            ps_w = psum[2 + cp]
            tm = P.op("tensor", lambda t, l=LT["u"][:, c * 128:(c + 1) * 128], r=tile,
                      pf=ps_w.ap[:, 0:128]:
                      t.matmul(pf, l, r, start=True, stop=True),
                      deps=[te, chain_done[("u", c)], ps_w.consumed], sig="S_tensor")
            g["_l2_tile%d" % cp] = tm
            last_mm = P.op("vector", lambda v, o=W1T[:, c * 1024 + kc * 128: c * 1024 + (kc + 1) * 128],
                           pf=ps_w.ap[:, 0:128]: v.tensor_copy(o, pf),
                           deps=[tm], sig="S_vector")
            ps_w.consumed = last_mm
        mm_tok[kc] = last_mm
    def pair_fold(tag, stages, buf, width, deps0):
        last = deps0
        for stg in stages:
            m = 2 ** (stg - 7)
            done = []
            pi = 0
            for B in range(32):
                Pp = B ^ m
                if Pp < B:
                    continue
                pi += 1
                TA = TMPW if pi % 2 == 0 else TMP2W
                kpart = "part" if tag == "u" else "partT"
                da = CO[(tag, stg, "diag")][:, B:B + 1]
                oa = CO[(tag, stg, kpart)][:, B:B + 1]
                db = CO[(tag, stg, "diag")][:, Pp:Pp + 1]
                ob = CO[(tag, stg, kpart)][:, Pp:Pp + 1]
                a = buf[:, B * width:(B + 1) * width]
                b = buf[:, Pp * width:(Pp + 1) * width]
                d0 = list(last)
                x1 = P.op("vector", lambda v, o=TA[:, 0:width], i=b, s=oa:
                          v.tensor_scalar(o, i, s, None, ALU.mult),
                          deps=d0 + co_t[(tag, stg, kpart)] + [g.get("_l2_pf%d" % (pi % 2))],
                          sig="S_vector")
                x2 = P.op("vector", lambda v, o=TA[:, 0:width], i=a, s=da:
                          v.scalar_tensor_tensor(o, i, s, o, ALU.mult, ALU.add),
                          deps=[x1] + co_t[(tag, stg, "diag")], sig="S_vector")
                x3 = P.op("vector", lambda v, o=TA[:, width:2 * width], i=a, s=ob:
                          v.tensor_scalar(o, i, s, None, ALU.mult),
                          deps=d0 + [g.get("_l2_pf%d" % (pi % 2))], sig="S_vector")
                x4 = P.op("vector", lambda v, o=b, i=b, s=db, t2=TA[:, width:2 * width]:
                          v.scalar_tensor_tensor(o, i, s, t2, ALU.mult, ALU.add),
                          deps=[x3, x2], sig="S_vector")
                x5 = P.op("vector", lambda v, o=a, i=TA[:, 0:width]:
                          v.tensor_copy(o, i), deps=[x2, x4], sig="S_vector")
                g["_l2_pf%d" % (pi % 2)] = x5
                done.append(x5)
                done.append(x4)
            last = done
        return last
    ufold_done = pair_fold("u", range(7, 12), W1T, 1024, [last_mm])
    wx_toks = []
    wslab = wstage2[0].bitcast(BF16)
    for kc in range(DC):
        evs = []
        for c in range(32):
            ps_t = psum[0]
            tt = P.op("tensor", lambda t, i=W1T[:, c * 1024 + kc * 128: c * 1024 + (kc + 1) * 128],
                      pf=ps_t.ap.bitcast(BF16)[:, 0:128]: t.transpose(pf, i, g["eyeb_sb"]),
                      deps=ufold_done + [ps_t.consumed] +
                      ([wx_toks[-1]] if wx_toks else []), sig="S_tensor")
            ev = P.op("vector", lambda v, o=wslab[:, c * 128:(c + 1) * 128],
                      pf=ps_t.ap.bitcast(BF16)[:, 0:128]: v.tensor_copy(o, pf),
                      deps=[tt], sig="S_vector")
            ps_t.consumed = ev
            evs.append(ev)
        wx_toks.append(P.dma("sync", wexp2[kc * 128:(kc + 1) * 128, :], wslab[:, 0:4096],
                             deps=evs))
    wx_done = wx_toks[-1]
    wc_l = []
    for fc in range(32):
        wc_l.append(P.dma("gpsimd", W1T[:, fc * 1024:(fc + 1) * 1024],
                          W("L2_ffn_W_con")[fc * 128:(fc + 1) * 128, :],
                          deps=[wx_done]))
    dfold_done = pair_fold("d", range(11, 6, -1), W1T, 1024, wc_l)
    wc_toks = []
    for fc in range(32):
        wc_toks.append(P.dma("sync", wcon2[fc * 128:(fc + 1) * 128, :],
                             W1T[:, fc * 1024:(fc + 1) * 1024],
                             deps=dfold_done + (wc_toks[-1:] if wc_toks else [])))
    wc_done = wc_toks[-1]
    t_be = P.dma("gpsimd", becol, W("L2_ffn_b_exp").rearrange("one (c p) -> p (one c)", p=128),
                 allow_slow_non_contiguous=True)
    for c in range(32):
        ps_b = psum[2]
        cb = P.op("vector", lambda v, o=St[:, 0:1], i=becol[:, c:c + 1]:
                  v.tensor_copy(o, i), deps=[t_be, g.get("_l2_bst")], sig="S_vector")
        tb1 = P.op("tensor", lambda t, l=LT["u"][:, c * 128:(c + 1) * 128],
                   pf=ps_b.ap[:, 0:1]:
                   t.matmul(pf, l, St[:, 0:1], start=True, stop=True),
                   deps=[cb, chain_done[("u", c)], ps_b.consumed], sig="S_tensor")
        tb2 = P.op("vector", lambda v, o=B1[:, c:c + 1], pf=ps_b.ap[:, 0:1]:
                   v.tensor_copy(o, pf), deps=[tb1], sig="S_vector")
        ps_b.consumed = tb2
        g["_l2_bst"] = tb2
    bfold = pair_fold("u", range(7, 12), B1, 1, [g["_l2_bst"]])
    t_ub = P.dma("gpsimd", GB, W("L2_ffn_up_bias").rearrange("one (c p) -> p (one c)", p=128),
                 allow_slow_non_contiguous=True)
    t_gb = P.op("vector", lambda v: v.tensor_tensor(GB, GB, B1, ALU.add),
                deps=bfold + [t_ub], sig="S_vector")
    t_b1 = P.dma("gpsimd", bcol, W("L2_ffn_b_con").rearrange("one (c p) -> p (one c)", p=128),
                 allow_slow_non_contiguous=True)
    t_b2 = P.dma("gpsimd", bcol2, W("L2_ffn_down_bias").rearrange("one (c p) -> p (one c)", p=128),
                 allow_slow_non_contiguous=True)
    t_bc = P.op("vector", lambda v: v.tensor_tensor(bcol, bcol, bcol2, ALU.add),
                deps=[t_b1, t_b2], sig="S_vector")
    GA.ptr = ga_fw
    wes = [W1T[:, kc * 4096:(kc + 1) * 4096] for kc in range(DC)]
    we_t = []
    for kc in range(DC):
        we_t.append(P.dma("gpsimd", wes[kc], wexp2[kc * 128:(kc + 1) * 128, :],
                          deps=[wc_done, wx_toks[kc], t_gb, g.get("_l2_bst")]))
    H = GA.alloc(32 * 512)
    wcs = [GA.alloc(1024), GA.alloc(1024)]
    stgd = GA.alloc(512)
    ydel = g["dram"]("l2f_delta", [DM, T], BF16)
    for tb in range(NT):
        h_toks = {}
        for cp in range(32):
            ps = psum[cp % 4]
            mms = [(wes[kc][:, cp * 128:(cp + 1) * 128],
                    XN[:, kc * T + tb * 512: kc * T + (tb + 1) * 512],
                    None, we_t[kc]) for kc in range(DC)]
            gt = mmgroup(P, ps, mms, deps=list(xdeps))
            ev = P.op("scalar", lambda s, o=H[:, cp * 512:(cp + 1) * 512], pf=ps.ap,
                      b=GB[:, cp:cp + 1]: s.activation(o, pf, AF.Gelu, bias=b),
                      deps=[gt, t_gb, g.get("_l2_h")], sig="S_scalar")
            ps.consumed = ev
            h_toks[cp] = ev
        h2_toks = {}
        for cp in range(32):
            ps = psum[4 + cp % 2]
            gt = P.op("tensor", lambda t, l=LT["d"][:, cp * 128:(cp + 1) * 128],
                      r=H[:, cp * 512:(cp + 1) * 512], pf=ps.ap:
                      t.matmul(pf, l, r, start=True, stop=True),
                      deps=[h_toks[cp], chain_done[("d", cp)], ps.consumed], sig="S_tensor")
            ev = P.op("vector", lambda v, o=H[:, cp * 512:(cp + 1) * 512], pf=ps.ap:
                      v.tensor_copy(o, pf), deps=[gt], sig="S_vector")
            ps.consumed = ev
            h2_toks[cp] = ev
        g["_l2_h"] = h2_toks[31]
        mm_last = {}
        wload = {}
        for kc in range(32):
            wload[kc] = P.dma("gpsimd", wcs[kc % 2], wcon2[kc * 128:(kc + 1) * 128, :],
                              deps=[wc_toks[kc], mm_last.get(kc - 2), t_gb,
                                    g.get("_l2_bst"), g.get("_l2_wc_prev")])
            tk = None
            for oc in range(DC):
                ps = psum[oc]
                tk = P.op("tensor", lambda t, l=wcs[kc % 2][:, oc * 128:(oc + 1) * 128],
                          r=H[:, kc * 512:(kc + 1) * 512], st=(kc == 0), sp=(kc == 31),
                          pf=ps.ap: t.matmul(pf, l, r, start=st, stop=sp),
                          deps=[wload[kc], h2_toks[kc]] +
                          ([ps.consumed] if kc == 0 else []),
                          sig="S_tensor" if oc == DC - 1 else None)
            mm_last[kc] = tk
        g["_l2_wc_prev"] = mm_last[31]
        for oc in range(DC):
            ps = psum[oc]
            ev = P.op("vector", lambda v, o=stgd, pf=ps.ap, b=bcol[:, oc:oc + 1]:
                      v.tensor_scalar(o, pf, b, None, ALU.add),
                      deps=[mm_last[31], t_bc, g.get("_l2_stg")], sig="S_vector")
            dk = P.dma("sync", ydel[oc * 128:(oc + 1) * 128, tb * 512:(tb + 1) * 512],
                       stgd, deps=[ev])
            ps.consumed = dk
            g["_l2_stg"] = dk
    rs_toks = [P.dma("sync", g["XT"][:, c * T:(c + 1) * T], xres[c * 128:(c + 1) * 128, :],
                     deps=[g["_l2_stg"], g["_l2_wc_prev"]]) for c in range(DC)]
    DB = H[:, 0:2 * T]
    db_ld = [None, None]
    for c in range(DC):
        buf = DB[:, (c % 2) * T:(c % 2 + 1) * T]
        lt = P.dma("sync", buf, ydel[c * 128:(c + 1) * 128, :],
                   deps=[g["_l2_stg"], db_ld[c % 2]])
        tk = P.op("vector", lambda v, o=g["XT"][:, c * T:(c + 1) * T], i=buf:
                  v.tensor_tensor(o, o, i, ALU.add), deps=[lt, rs_toks[c]], sig="S_vector")
        db_ld[c % 2] = tk


def emit_L0_attn(P, g):
    W, GA, XN, psum = g["W"], g["GA"], g["XN"], g["psum"]
    H, NST = 8, 64
    xdeps = emit_layernorm(P, g)
    xres = g["dram"]("l0a_xres", [DM, T])
    zdram = g["dram"]("l0a_z", [2048, T], BF16)
    w8d = g["dram"]("l0a_w8", [8 * T])
    u8d = g["dram"]("l0a_u8", [8 * T])
    cumd = g["dram"]("l0a_cum", [8 * T])
    eacd = g["dram"]("l0a_eac", [128])
    sp_toks = [P.dma("sync", xres[c * 128:(c + 1) * 128, :], g["XT"][:, c * T:(c + 1) * T],
                     deps=list(xdeps)) for c in range(DC)]
    XT8 = g["XTspace"][:, 0:4 * T]
    Xtok = g["XTspace"][:, 4 * T:12 * T]
    YT = g["XTspace"][:, 12 * T:16 * T]
    fa = g["FAcol"]
    ga_start = GA.ptr
    wz, wz_t = load_wslabs(P, g, W("L0_attn_Wz"), DM, 2048)
    zstg = GA.alloc(T)
    z_w = []
    zprev = None
    for zc in range(16):
        evs = []
        for tb in range(NT):
            ps = psum[tb % 2]
            mms = [(wz[kc][:, zc * 128:(zc + 1) * 128],
                    XN[:, kc * T + tb * 512: kc * T + (tb + 1) * 512],
                    None, wz_t[kc]) for kc in range(DC)]
            gt = mmgroup(P, ps, mms, deps=list(xdeps))
            ev = P.op("scalar", lambda s, o=zstg[:, tb * 512:(tb + 1) * 512], pf=ps.ap:
                      s.activation(o, pf, AF.Silu), deps=[gt, zprev], sig="S_scalar")
            ps.consumed = ev
            evs.append(ev)
        zprev = P.dma("sync", zdram[zc * 128:(zc + 1) * 128, :], zstg, deps=evs)
        z_w.append(zprev)
    GA.ptr = ga_start
    Bc = GA.alloc(T)
    Cc = GA.alloc(T)
    cv1 = GA.alloc(T, F32)
    cv2 = GA.alloc(T, F32)
    Btok = GA.alloc(TC * 64)
    S = GA.alloc(512, F32)
    SB16 = GA.alloc(512)
    ACB = GA.alloc(128, F32)
    EXPACB = GA.alloc(128, F32)
    DCB = GA.alloc(8, F32)
    WLT = GA.alloc(8, F32)
    DLT = GA.alloc(8, F32)
    GEXP = GA.alloc(128)
    CBMD = GA.alloc(128)
    W2S = GA.alloc(1, F32)
    W8R = GA.alloc(1024, F32)
    U8R = GA.alloc(1024, F32)
    CUMR = GA.alloc(1024, F32)
    UBC = GA.alloc(128)
    CUT = GA.alloc(128)
    GH = GA.alloc(128)
    CBM = GA.alloc(128)
    ONE8 = GA.alloc(128, F32)
    BWB = [GA.alloc(64), GA.alloc(64)]
    zdone = [zprev, evs[-1]]
    t_one8 = P.op("vector", lambda v: v.memset(ONE8[0:8, :], 1.0), deps=zdone, sig="S_vector")
    DT8 = cv1[0:8, :]
    CUM = cv1[64:72, :]
    TMP8 = cv2[0:8, :]
    U8 = cv2[32:40, :]
    W8 = cv2[64:72, :]
    win, win_t = load_wslabs(P, g, W("L0_attn_Win"), DM, 648, deps=zdone)
    def fmproj(dst_ap, col0, ww, psl, deps0=()):
        done = None
        for tb in range(NT):
            ps = psum[psl + (tb % 2)]
            mms = [(win[kc][:, col0:col0 + ww],
                    XN[:, kc * T + tb * 512: kc * T + (tb + 1) * 512],
                    ps.ap[0:ww, 0:512], win_t[kc]) for kc in range(DC)]
            gt = mmgroup(P, ps, mms, deps=list(deps0))
            done = P.op("vector", lambda v, o=dst_ap[0:ww, tb * 512:(tb + 1) * 512],
                        pf=ps.ap[0:ww, 0:512]: v.tensor_copy(o, pf),
                        deps=[gt], sig="S_vector")
            ps.consumed = done
        return done
    b_done = fmproj(Bc, 0, 64, 0)
    c_done = fmproj(Cc, 64, 64, 2)
    dt_done = fmproj(DT8, 128, 8, 0)
    x8_done = None
    for xc in range(4):
        x8_done = fmproj(XT8[:, xc * T:(xc + 1) * T], 136 + xc * 128, 128, 2 * (xc % 2),
                         deps0=sp_toks)
    xt_done = None
    for tc in range(TC):
        ps = psum[4]
        mms = [(XN[:, kc * T + tc * 128: kc * T + (tc + 1) * 128],
                win[kc][:, 136:648], None, win_t[kc]) for kc in range(DC)]
        gt = mmgroup(P, ps, mms, deps=sp_toks)
        xt_done = P.op("vector", lambda v, o=Xtok[:, tc * 512:(tc + 1) * 512], pf=ps.ap:
                       v.tensor_copy(o, pf), deps=[gt], sig="S_vector")
        ps.consumed = xt_done
        ps = psum[5]
        mms = [(XN[:, kc * T + tc * 128: kc * T + (tc + 1) * 128],
                win[kc][:, 0:64], ps.ap[:, 0:64], win_t[kc]) for kc in range(DC)]
        gt = mmgroup(P, ps, mms, deps=[])
        bt_done = P.op("vector", lambda v, o=Btok[:, tc * 64:(tc + 1) * 64],
                       pf=ps.ap[:, 0:64]: v.tensor_copy(o, pf), deps=[gt], sig="S_vector")
        ps.consumed = bt_done
    dtb = fa.alloc(1, F32)
    t_dtb = P.dma("gpsimd", dtb[0:8, :], W("L0_attn_dt_bias").rearrange("one e -> e one"),
                  allow_slow_non_contiguous=True)
    alog = fa.alloc(1, F32)
    t_al = P.dma("gpsimd", alog[0:8, :], W("L0_attn_A_log").rearrange("one e -> e one"),
                 allow_slow_non_contiguous=True)
    nA = fa.alloc(1, F32)
    t_nA1 = P.op("scalar", lambda s: s.activation(nA[0:8, :], alog[0:8, :], AF.Exp),
                 deps=[t_al], sig="S_scalar")
    t_nA = P.op("vector", lambda v: v.tensor_scalar(nA[0:8, :], nA[0:8, :], -1.0, None, ALU.mult),
                deps=[t_nA1], sig="S_vector")
    s1 = P.op("vector", lambda v: v.tensor_scalar(DT8, DT8, dtb[0:8, :], None, ALU.add),
              deps=[dt_done, t_dtb], sig="S_vector")
    s2 = P.op("scalar", lambda s: s.activation(TMP8, DT8, AF.Abs), deps=[s1], sig="S_scalar")
    s3 = P.op("scalar", lambda s: s.activation(TMP8, TMP8, AF.Exp, scale=-1.0),
              deps=[s2], sig="S_scalar")
    s4 = P.op("scalar", lambda s: s.activation(TMP8, TMP8, AF.Ln, bias=g["cc"](1.0)[64:72, :]),
              deps=[s3], sig="S_scalar")
    s5 = P.op("vector", lambda v: v.tensor_scalar(DT8, DT8, 0.0, None, ALU.max),
              deps=[s4], sig="S_vector")
    s6 = P.op("vector", lambda v: v.tensor_tensor(DT8, DT8, TMP8, ALU.add),
              deps=[s5], sig="S_vector")
    s7 = P.op("vector", lambda v: v.tensor_scalar(TMP8, DT8, nA[0:8, :], None, ALU.mult),
              deps=[s6, t_nA], sig="S_vector")
    sc_t = []
    for c in range(TC):
        tk = P.op("vector", lambda v, o=CUM[:, c * 128:(c + 1) * 128],
                  d1=TMP8[:, c * 128:(c + 1) * 128]:
                  v.tensor_tensor_scan(o, ONE8[0:8, :], d1, 0.0, ALU.mult, ALU.add),
                  deps=[s7, t_one8], sig="S_vector")
        sc_t.append(tk)
    s9 = P.op("vector", lambda v: v.tensor_scalar(W8, CUM, -1.0, None, ALU.mult),
              deps=sc_t, sig="S_vector")
    s10 = P.op("scalar", lambda s: s.activation(U8, CUM, AF.Exp), deps=sc_t, sig="S_scalar")
    EACs = fa.alloc(16, F32)
    s11 = P.op("vector", lambda s: s.tensor_copy(EACs[0:8, 0:16], CUM[:, 127:2048:128]),
               deps=sc_t, sig="S_vector")
    d1 = P.dma("sync", w8d.rearrange("(c h s) -> h c s", h=8, c=16), W8, deps=[s9])
    d2 = P.dma("sync", u8d.rearrange("(c h s) -> h c s", h=8, c=16), U8, deps=[s10])
    d2b = P.dma("sync", cumd.rearrange("(c h s) -> h c s", h=8, c=16), CUM, deps=sc_t)
    d3 = P.dma("sync", eacd.rearrange("(h c) -> h c", h=8), EACs[0:8, 0:16], deps=[s11])
    EACR = fa.alloc(128, F32)
    d4 = P.dma("sync", EACR[0:1, 0:128], eacd.rearrange("(one x) -> one x", one=1), deps=[d3])
    pb = psum[4]
    t_e1 = P.op("tensor", lambda t: t.matmul(pb.ap[:, 0:128], g["ones1f_sb"], EACR[0:1, 0:128],
                                             start=True, stop=True),
                deps=[d4, pb.consumed], sig="S_tensor")
    t_ac = P.op("vector", lambda v: v.tensor_copy(ACB, pb.ap[:, 0:128]),
                deps=[t_e1], sig="S_vector")
    t_e2 = P.op("scalar", lambda s: s.activation(EXPACB, ACB, AF.Exp),
                deps=[t_ac], sig="S_scalar")
    pb.consumed = t_e2
    DR = fa.alloc(8, F32)
    d5 = P.dma("sync", DR[0:1, 0:8], W("L0_attn_D_param"), deps=[])
    t_d1 = P.op("tensor", lambda t: t.matmul(pb.ap[:, 0:8], g["ones1f_sb"], DR[0:1, 0:8],
                                             start=True, stop=True),
                deps=[d5, pb.consumed], sig="S_tensor")
    t_d2 = P.op("vector", lambda v: v.tensor_copy(DCB, pb.ap[:, 0:8]),
                deps=[t_d1], sig="S_vector")
    pb.consumed = t_d2
    t_s0 = P.op("vector", lambda v: v.memset(S, 0.0), sig="S_vector")
    wu_prev = None
    su_last = t_s0
    sb_tok = None
    for c in range(TC):
        csl = slice(c * 128, (c + 1) * 128)
        r1 = P.dma("sync", W8R[0:1, 0:1024],
                   w8d[c * 1024:(c + 1) * 1024].rearrange("(one x) -> one x", one=1),
                   deps=[d1, wu_prev])
        r2 = P.dma("sync", U8R[0:1, 0:1024],
                   u8d[c * 1024:(c + 1) * 1024].rearrange("(one x) -> one x", one=1),
                   deps=[d2, wu_prev])
        r3 = P.dma("sync", CUMR[0:1, 0:1024],
                   cumd[c * 1024:(c + 1) * 1024].rearrange("(one x) -> one x", one=1),
                   deps=[d2b, wu_prev])
        g["_l0_cumr"] = r3
        g["cumr_slice"] = lambda cc, hh: CUMR[0:1, hh * 128:(hh + 1) * 128]
        ps5 = psum[5]
        tw = P.op("tensor", lambda t, i=CUM[:, csl], pf=ps5.ap[:, 0:8]:
                  t.transpose(pf, i, g["eye_sb"][64:72, 64:72]),
                  deps=[sc_t[c], ps5.consumed], sig="S_tensor")
        twe = P.op("vector", lambda v, pf=ps5.ap[:, 0:8]: v.tensor_copy(WLT, pf),
                   deps=[tw, g.get("_l0_wlt")], sig="S_vector")
        ps5.consumed = twe
        td = P.op("tensor", lambda t, i=DT8[:, csl], pf=ps5.ap[:, 8:16]:
                  t.transpose(pf, i, g["eye_sb"][0:8, 0:8]),
                  deps=[s6, twe], sig="S_tensor")
        tde = P.op("vector", lambda v, pf=ps5.ap[:, 8:16]: v.tensor_copy(DLT, pf),
                   deps=[td, g.get("_l0_dlt")], sig="S_vector")
        ps5.consumed = tde
        sb_tok = P.op("vector", lambda v: v.tensor_copy(SB16[0:64, :], S[0:64, :]),
                      deps=[su_last, g.get("_l0_sb")], sig="S_vector")
        ps_cb = psum[6]
        tcb = P.op("tensor", lambda t, l=Bc[0:64, csl], r=Cc[0:64, csl], pf=ps_cb.ap[:, 0:128]:
                   t.matmul(pf, l, r, start=True, stop=True),
                   deps=[b_done, c_done, ps_cb.consumed], sig="S_tensor")
        tcbm = P.op("vector", lambda v, pf=ps_cb.ap[:, 0:128]:
                    v.tensor_tensor(CBM, pf, g["triub_sb"], ALU.mult),
                    deps=[tcb, g.get("_l0_cbm")], sig="S_vector")
        ps_cb.consumed = tcbm
        for h in range(H):
            ps_g = psum[7]
            tg0 = P.op("tensor", lambda t, l=W8R[0:1, h * 128:(h + 1) * 128],
                       pf=ps_g.ap[:, 0:128]:
                       t.matmul(pf, l, g["ones1f_sb"], start=True, stop=False),
                       deps=[r1, ps_g.consumed], sig=None)
            tg = P.op("tensor", lambda t, r=g["cumr_slice"](c, h),
                      pf=ps_g.ap[:, 0:128]:
                      t.matmul(pf, g["ones1f_sb"], r, start=False, stop=True),
                      deps=[g["_l0_cumr"]], sig="S_tensor")
            tmsk = P.op("vector", lambda v, pf=ps_g.ap[:, 0:128]:
                        v.tensor_tensor(pf, pf, g["maskfl_sb"], ALU.add),
                        deps=[tg], sig="S_vector")
            tex = P.op("scalar", lambda s, pf=ps_g.ap[:, 0:128]:
                       s.activation(GEXP, pf, AF.Exp),
                       deps=[tmsk, g.get("_l0_gh")], sig="S_scalar")
            ps_g.consumed = tex
            tcbd = P.op("vector", lambda v, dcol=DLT[:, h:h + 1]:
                        v.tensor_scalar(CBMD, CBM, dcol, None, ALU.mult),
                        deps=[tcbm, tde, g.get("_l0_cbmd")], sig="S_vector")
            tgh = P.op("vector", lambda v:
                       v.tensor_tensor(GH, CBMD, GEXP, ALU.mult),
                       deps=[tex, tcbd, g.get("_l0_gh")], sig="S_vector")
            ps_u = psum[4]
            tu = P.op("tensor", lambda t, r=U8R[0:1, h * 128:(h + 1) * 128],
                      pf=ps_u.ap[0:64, 0:128]:
                      t.matmul(pf, g["ones1f_sb"][0:1, 0:64], r, start=True, stop=True),
                      deps=[r2, ps_u.consumed], sig="S_tensor")
            tue = P.op("vector", lambda v, pf=ps_u.ap[0:64, 0:128]:
                       v.tensor_copy(UBC[0:64, :], pf), deps=[tu, g.get("_l0_ubc")],
                       sig="S_vector")
            ps_u.consumed = tue
            tcu = P.op("vector", lambda v, i=Cc[0:64, csl]:
                       v.tensor_tensor(CUT[0:64, :], i, UBC[0:64, :], ALU.mult),
                       deps=[tue, g.get("_l0_cut")], sig="S_vector")
            ps_y = psum[0 + h % 2]
            m1 = (Xtok[:, c * 512 + h * 64: c * 512 + (h + 1) * 64], GH, ps_y.ap[0:64, 0:128], tgh)
            m2 = (SB16[0:64, h * 64:(h + 1) * 64], CUT[0:64, :], ps_y.ap[0:64, 0:128], tcu)
            gy = mmgroup(P, ps_y, [m1, m2], deps=[xt_done, sb_tok])
            yv = YT[:, (h // 2) * T + c * 128:(h // 2) * T + (c + 1) * 128][(h % 2) * 64:(h % 2) * 64 + 64, :]
            xv = XT8[:, (h // 2) * T + c * 128:(h // 2) * T + (c + 1) * 128][(h % 2) * 64:(h % 2) * 64 + 64, :]
            ty = P.op("vector", lambda v, o=yv, i=xv, dc=DCB[(h % 2) * 64:(h % 2) * 64 + 64, h:h + 1],
                      pf=ps_y.ap[0:64, 0:128]:
                      v.scalar_tensor_tensor(o, i, dc, pf, ALU.mult, ALU.add),
                      deps=[gy, t_d2, x8_done], sig="S_vector")
            ps_y.consumed = ty
            g["_yh%d" % h] = ty
            if "_l0_w2c%d" % h not in g:
                g["_l0_w2c%d" % h] = fa.alloc(1, F32)
            w2c = g["_l0_w2c%d" % h]
            tw2a = P.op("vector", lambda v, o=W2S, a=ACB[:, h * 16 + c: h * 16 + c + 1],
                        b=WLT[:, h:h + 1]:
                        v.tensor_tensor(o, a, b, ALU.subtract),
                        deps=[twe, t_ac, g.get("_l0_w2s")], sig="S_vector")
            tw2b = P.op("scalar", lambda s: s.activation(W2S, W2S, AF.Exp),
                        deps=[tw2a], sig="S_scalar")
            tw2 = P.op("vector", lambda v, o=w2c, b=DLT[:, h:h + 1]:
                       v.tensor_tensor(o, W2S, b, ALU.mult),
                       deps=[tw2b, tde, g.get("_l0_bw%d" % h)], sig="S_vector")
            g["_l0_w2s"] = tw2
            BW = BWB[h % 2]
            tbw = P.op("vector", lambda v, o=BW[:, 0:64], i=Btok[:, c * 64:(c + 1) * 64], w=w2c:
                       v.tensor_scalar(o, i, w, None, ALU.mult),
                       deps=[tw2, g.get("_l0_bwmm%d" % (h % 2))], sig="S_vector")
            g["_l0_bw%d" % h] = tbw
            ps_su = psum[2 + h % 2]
            tsu = P.op("tensor", lambda t, l=BW[:, 0:64],
                       r=Xtok[:, c * 512 + h * 64: c * 512 + (h + 1) * 64],
                       pf=ps_su.ap[0:64, 0:64]:
                       t.matmul(pf, l, r, start=True, stop=True),
                       deps=[tbw, ps_su.consumed], sig="S_tensor")
            g["_l0_bwmm%d" % (h % 2)] = tsu
            tss = P.op("vector", lambda v, o=S[0:64, h * 64:(h + 1) * 64],
                       e=EXPACB[0:64, h * 16 + c: h * 16 + c + 1],
                       pf=ps_su.ap[0:64, 0:64]:
                       v.scalar_tensor_tensor(o, o, e, pf, ALU.mult, ALU.add),
                       deps=[tsu, sb_tok], sig="S_vector")
            ps_su.consumed = tss
            su_last = tss
            g["_l0_gh"] = gy
            g["_l0_cbmd"] = gy
            g["_l0_ubc"] = tue
            g["_l0_cut"] = tcu
        g["_l0_wlt"] = twe
        g["_l0_dlt"] = tde
        g["_l0_cbm"] = tcbm
        g["_l0_sb"] = sb_tok
        wu_prev = tsu
    ydel = g["dram"]("l0a_delta", [DM, T], BF16)
    ydeps = [g["_yh%d" % h] for h in range(H)] + [su_last]
    GA.ptr = ga_start
    wzg, wzg_t = load_wslabs(P, g, W("L0_attn_Wzg"), 2048, 512, deps=ydeps)
    wout, wout_t = load_wslabs(P, g, W("L0_attn_Wout"), 512, DM, deps=ydeps)
    zsl = [GA.alloc(512), GA.alloc(512)]
    stg2 = GA.alloc(512)
    yg_toks = []
    prev_tb_mm = None
    for tb in range(NT):
        zload = [None] * 16
        mm_t = {}
        for zc in range(16):
            zload[zc] = P.dma("gpsimd", zsl[zc % 2],
                              zdram[zc * 128:(zc + 1) * 128, tb * 512:(tb + 1) * 512],
                              deps=[z_w[zc], mm_t.get(zc - 2), prev_tb_mm]
                              + (ydeps if zc < 2 else []))
            tk = None
            for oc in range(4):
                ps = psum[oc]
                tk = P.op("tensor", lambda t, l=wzg[zc][:, oc * 128:(oc + 1) * 128],
                          r=zsl[zc % 2], st=(zc == 0), sp=(zc == 15), pf=ps.ap:
                          t.matmul(pf, l, r, start=st, stop=sp),
                          deps=[zload[zc], wzg_t[zc]] + ([ps.consumed] if zc == 0 else []),
                          sig="S_tensor" if oc == 3 else None)
            mm_t[zc] = tk
        prev_tb_mm = mm_t[15]
        for oc in range(4):
            ps = psum[oc]
            yg = P.op("vector", lambda v, o=YT[:, oc * T + tb * 512: oc * T + (tb + 1) * 512],
                      pf=ps.ap: v.tensor_tensor(o, o, pf, ALU.mult),
                      deps=[mm_t[15]] + ydeps, sig="S_vector")
            ps.consumed = yg
            yg_toks.append(yg)
    for oc in range(DC):
        for tb in range(NT):
            ps = psum[4 + ((oc * NT + tb) % 2)]
            mms = [(wout[kc][:, oc * 128:(oc + 1) * 128],
                    YT[:, kc * T + tb * 512: kc * T + (tb + 1) * 512],
                    None, wout_t[kc]) for kc in range(4)]
            gt = mmgroup(P, ps, mms, deps=[yg_toks[-1], yg_toks[0]])
            ev = P.op("vector", lambda v, o=stg2, pf=ps.ap:
                      v.tensor_copy(o, pf), deps=[gt, g.get("_l0_stg")], sig="S_vector")
            dk = P.dma("sync", ydel[oc * 128:(oc + 1) * 128, tb * 512:(tb + 1) * 512],
                       stg2, deps=[ev])
            ps.consumed = dk
            g["_l0_stg"] = dk
    rs_toks = [P.dma("sync", g["XT"][:, cc * T:(cc + 1) * T], xres[cc * 128:(cc + 1) * 128, :],
                     deps=[g["_l0_stg"]]) for cc in range(DC)]
    GA.ptr = ga_start
    DB = GA.alloc(2 * T)
    db_ld = [None, None]
    for cc in range(DC):
        buf = DB[:, (cc % 2) * T:(cc % 2 + 1) * T]
        lt = P.dma("sync", buf, ydel[cc * 128:(cc + 1) * 128, :],
                   deps=[g["_l0_stg"], db_ld[cc % 2], yg_toks[-1]])
        tk = P.op("vector", lambda v, o=g["XT"][:, cc * T:(cc + 1) * T], i=buf:
                  v.tensor_tensor(o, o, i, ALU.add), deps=[lt, rs_toks[cc]], sig="S_vector")
        db_ld[cc % 2] = tk


def emit_L1_attn(P, g):
    W, GA, XN, psum = g["W"], g["GA"], g["XN"], g["psum"]
    H = 16
    xdeps = emit_layernorm(P, g)
    xres = g["dram"]("l1a_xres", [DM, T])
    ydel = g["dram"]("l1a_delta", [DM, T], BF16)
    sp_toks = [P.dma("sync", xres[c * 128:(c + 1) * 128, :], g["XT"][:, c * T:(c + 1) * T],
                     deps=list(xdeps)) for c in range(DC)]
    QT = g["XTspace"][:, 0:DC * T]
    KT = g["XTspace"][:, DC * T:2 * DC * T]
    VTOK = GA.alloc(TC * 1024)
    onesr = GA.alloc(512)
    t_ones = P.op("vector", lambda v: v.memset(onesr[0:1, :], 1.0), sig="S_vector")
    brow = GA.alloc(3072)
    t_br = P.dma("gpsimd", brow[0:1, :], W("L1_attn_b_attn"))
    ga0 = GA.ptr

    def proj_fm(col0, dst, deps0):
        slabs, st = load_wslabs(P, g, W("L1_attn_W_attn")[:, col0:col0 + 1024],
                                DM, 1024, deps=deps0)
        done = None
        for oc in range(DC):
            for tb in range(NT):
                ps = psum[(oc * NT + tb) % 2]
                mms = [(slabs[kc][:, oc * 128:(oc + 1) * 128],
                        XN[:, kc * T + tb * 512: kc * T + (tb + 1) * 512],
                        None, st[kc]) for kc in range(DC)]
                mms.append((brow[0:1, col0 + oc * 128: col0 + (oc + 1) * 128],
                            onesr[0:1, 0:512], None, t_br))
                gt = mmgroup(P, ps, mms, deps=list(xdeps) + [t_ones] +
                             (sp_toks if col0 == 0 else []))
                done = P.op("vector", lambda v, o=dst[:, oc * T + tb * 512: oc * T + (tb + 1) * 512],
                            pf=ps.ap: v.tensor_copy(o, pf), deps=[gt], sig="S_vector")
                ps.consumed = done
        return done

    q_done = proj_fm(0, QT, [])
    GA.ptr = ga0
    k_done = proj_fm(1024, KT, [q_done])
    GA.ptr = ga0
    slabs, st = load_wslabs(P, g, W("L1_attn_W_attn")[:, 2048:3072], DM, 1024,
                            deps=[k_done])
    v_done = None
    for tc in range(TC):
        for fb in range(2):
            ps = psum[2 + fb]
            mms = [(XN[:, kc * T + tc * 128: kc * T + (tc + 1) * 128],
                    slabs[kc][:, fb * 512:(fb + 1) * 512], None, st[kc])
                   for kc in range(DC)]
            mms.append((onesr[0:1, 0:128],
                        brow[0:1, 2048 + fb * 512: 2048 + (fb + 1) * 512], None, t_br))
            gt = mmgroup(P, ps, mms, deps=list(xdeps) + [t_ones])
            v_done = P.op("vector", lambda v, o=VTOK[:, tc * 1024 + fb * 512: tc * 1024 + (fb + 1) * 512],
                          pf=ps.ap: v.tensor_copy(o, pf), deps=[gt], sig="S_vector")
            ps.consumed = v_done
    GA.ptr = ga0
    KSC = GA.alloc(8, F32)
    KSCB = GA.alloc(8)
    ks_t = []
    for c in range(DC):
        tk = P.op("vector", lambda v, o=KSC[:, c:c + 1], i=KT[:, c * T:(c + 1) * T]:
                  v.tensor_reduce(o, i, AX, ALU.add), deps=[k_done], sig="S_vector")
        ks_t.append(tk)
    ksb = P.op("vector", lambda v: v.tensor_copy(KSCB, KSC), deps=ks_t, sig="S_vector")
    sgr = g["FAcol"].alloc(1, F32)
    t_sg = P.dma("sync", sgr[0:1, 0:1], W("L1_attn_score_gain"))
    gcol, g_t = bcast_cols(P, g, sgr[0:1, 0:1], 1, [t_sg])
    SC2 = [GA.alloc(T), GA.alloc(T)]
    _EB = GA.alloc(T)
    EB2 = [_EB, _EB]
    ATB2 = [GA.alloc(T), GA.alloc(T)]
    YH2 = [GA.alloc(DM), GA.alloc(DM)]
    stg = GA.alloc(512)
    fa = g["FAcol"]
    mcol2 = [fa.alloc(1, F32), fa.alloc(1, F32)]
    madc2 = [fa.alloc(1, F32), fa.alloc(1, F32)]
    ccol2 = [fa.alloc(1, F32), fa.alloc(1, F32)]
    rrc2 = [fa.alloc(1, F32), fa.alloc(1, F32)]
    wp, wp_t = load_wslabs(P, g, W("L1_attn_W_proj"), DM, DM, deps=[v_done])
    bpcol = fa.alloc(DC, F32)
    t_bp = P.dma("gpsimd", bpcol, W("L1_attn_b_proj").rearrange("one (oc p) -> p (one oc)", p=128),
                 allow_slow_non_contiguous=True)
    iters = [(tci, h) for tci in range(TC) for h in range(H)]
    state = {}

    def front(k):
        tci, h = iters[k]
        pp = k % 2
        SC = SC2[pp]
        mcol = mcol2[pp]
        qsl = QT[:, (h // 2) * T + tci * 128: (h // 2) * T + (tci + 1) * 128][(h % 2) * 64:(h % 2) * 64 + 64, :]
        bank_toks = []
        for bb in range(4):
            ps = psum[bb]
            ksl = KT[:, (h // 2) * T + bb * 512: (h // 2) * T + (bb + 1) * 512][(h % 2) * 64:(h % 2) * 64 + 64, :]
            gt = mmgroup(P, ps, [(qsl, ksl, None, None)], deps=[q_done, k_done])
            bank_toks.append((ps, gt))
        tci_, h_ = iters[k]
        s_end_f = (tci_ + 1) * 128
        nb_cf = (s_end_f + 511) // 512
        racc = fa_racc(g, fa, pp)
        ps5 = psum[7]
        tm = P.op("tensor", lambda t, l=qsl,
                  r=KSCB[(h % 2) * 64:(h % 2) * 64 + 64, (h // 2):(h // 2) + 1],
                  pf=ps5.ap[:, pp:pp + 1]: t.matmul(pf, l, r, start=True, stop=True),
                  deps=[ksb, ps5.consumed], sig="S_tensor")
        tmc = P.op("vector", lambda v, o=mcol, pf=ps5.ap[:, pp:pp + 1]:
                   v.tensor_scalar(o, pf, -1.0 / T, None, ALU.mult),
                   deps=[tm, g.get("_l1_cols%d" % pp)], sig="S_vector")
        ps5.consumed = tmc
        d_toks = []
        eb_guard = g.get("_l1_eb")
        for bb, (ps, gt) in enumerate(bank_toks):
            ta = P.op("scalar", lambda s, o=EB2[0][:, bb * 512:(bb + 1) * 512],
                      pf=ps.ap, m=mcol, a=racc[:, bb:bb + 1]:
                      s.activation(o, pf, AF.Abs, bias=m, accum_out=a),
                      deps=[gt, tmc, eb_guard, g.get("_l1_ra%d" % pp)], sig="S_scalar")
            if bb < nb_cf:
                tk = P.op("scalar", lambda s, o=SC[:, bb * 512:(bb + 1) * 512],
                          pf=ps.ap, m=mcol:
                          s.activation(o, pf, AF.Relu, bias=m),
                          deps=[ta, g.get("_l1_sc%d" % pp)], sig="S_scalar")
            else:
                tk = ta
            ps.consumed = tk
            d_toks.append(tk)
        state[k] = d_toks

    def middle(k):
        tci, h = iters[k]
        pp = k % 2
        s_end = (tci + 1) * 128
        SC, EB = SC2[pp], EB2[pp]
        madc, ccol, rrc = madc2[pp], ccol2[pp], rrc2[pp]
        racc = fa_racc(g, fa, pp)
        d_toks = state.pop(k)
        tmad = P.op("vector", lambda v, o=madc, i=racc[:, 0:4]:
                    v.tensor_reduce(o, i, AX, ALU.add), deps=d_toks, sig="S_vector")
        t1 = P.op("vector", lambda v, o=madc: v.tensor_scalar(o, o, 1.0 / T, 1e-6,
                                                              ALU.mult, ALU.add),
                  deps=[tmad], sig="S_vector")
        g["_l1_ra%d" % pp] = tmad
        t2 = P.op("vector", lambda v, o=ccol, i=madc: v.reciprocal(o, i),
                  deps=[t1], sig="S_vector")
        t3 = P.op("vector", lambda v, o=ccol: v.tensor_tensor(o, o, gcol, ALU.mult),
                  deps=[t2, g_t], sig="S_vector")
        nb_c = (s_end + 511) // 512
        u_toks = []
        for bb in range(nb_c):
            w = min(512, s_end - bb * 512)
            tk = P.op("vector", lambda v, o=SC[:, bb * 512: bb * 512 + w], c2=ccol:
                      v.tensor_scalar(o, o, c2, None, ALU.mult),
                      deps=[t3], sig="S_vector")
            u_toks.append(tk)
        tmk = P.op("vector", lambda v, o=SC[:, (tci * 128):(tci * 128) + 128]:
                   v.tensor_tensor(o, o, g["trilb_sb"], ALU.mult),
                   deps=u_toks, sig="S_vector")
        r_toks = []
        for bb in range(nb_c):
            w = min(512, s_end - bb * 512)
            te = P.op("vector", lambda v, o=EB[:, bb * 512: bb * 512 + w],
                      i=SC[:, bb * 512: bb * 512 + w]:
                      v.tensor_tensor(o, i, i, ALU.mult),
                      deps=[tmk, g.get("_l1_eb")], sig="S_vector")
            tr = P.op("vector", lambda v, o=SC[:, bb * 512: bb * 512 + w],
                      i=EB[:, bb * 512: bb * 512 + w], a=racc[:, 4 + bb:5 + bb]:
                      v.scalar_tensor_tensor(o, i, 1.0, i, ALU.mult, ALU.mult,
                                             accum_out=a),
                      deps=[te], sig="S_vector")
            r_toks.append(tr)
        tz = P.op("vector", lambda v, o=rrc, i=racc[:, 4:4 + nb_c]:
                  v.tensor_reduce(o, i, AX, ALU.add), deps=r_toks, sig="S_vector")
        t5 = P.op("vector", lambda v, o=rrc:
                  v.tensor_scalar(o, o, 1.0, 1e-6, ALU.mult, ALU.add),
                  deps=[tz], sig="S_vector")
        t5 = P.op("vector", lambda v, o=rrc, i=rrc: v.reciprocal(o, i),
                  deps=[t5], sig="S_vector")
        tn = P.op("vector", lambda v, o=SC[:, 0:s_end], r2=rrc:
                  v.tensor_scalar(o, o, r2, None, ALU.mult), deps=[t5], sig="S_vector")
        g["_l1_eb"] = r_toks[-1]
        g["_l1_cols%d" % pp] = t5
        state[("n", k)] = tn

    def back(k):
        tci, h = iters[k]
        pp = k % 2
        s_end = (tci + 1) * 128
        nsc = tci + 1
        yh_p = tci % 2
        YH = YH2[yh_p]
        SC, ATB = SC2[pp], ATB2[pp]
        tn = state.pop(("n", k))
        ps_t = psum[5]
        ps_y = psum[4]
        ev = None
        for scb in range(0, nsc, 4):
            nn = min(4, nsc - scb)
            tts = []
            for j in range(nn):
                tt = P.op("tensor", lambda t, i=SC[:, (scb + j) * 128:(scb + j + 1) * 128],
                          pf=ps_t.ap.bitcast(BF16)[:, pp * 512 + j * 128: pp * 512 + (j + 1) * 128]:
                          t.transpose(pf, i, g["eyeb_sb"]),
                          deps=[tn, ps_t.consumed if j == 0 else None], sig="S_tensor")
                tts.append(tt)
            ev = P.op("vector", lambda v, o=ATB[:, scb * 128: scb * 128 + nn * 128],
                      pf=ps_t.ap.bitcast(BF16)[:, pp * 512: pp * 512 + nn * 128]:
                      v.tensor_copy(o, pf),
                      deps=[tts[-1]], sig="S_vector")
            ps_t.consumed = ev
        g["_l1_sc%d" % pp] = ev
        mms = [(VTOK[:, sc * 1024 + h * 64: sc * 1024 + (h + 1) * 64],
                ATB[:, sc * 128:(sc + 1) * 128],
                ps_y.ap[0:64, 128 + pp * 128: 256 + pp * 128], ev) for sc in range(nsc)]
        gt = mmgroup(P, ps_y, mms, deps=[ev, v_done])
        yev = P.op("vector", lambda v, o=YH[(h % 2) * 64:(h % 2) * 64 + 64,
                                            (h // 2) * 128:(h // 2) * 128 + 128],
                   pf=ps_y.ap[0:64, 128 + pp * 128: 256 + pp * 128]:
                   v.tensor_copy(o, pf),
                   deps=[gt, g.get("_l1_yh%d" % yh_p)], sig="S_vector")
        ps_y.consumed = yev
        g["_yh%d" % h] = yev
        if h == H - 1:
            ydeps = [g["_yh%d" % hh] for hh in range(H)]
            for oc in range(DC):
                ps = psum[6]
                mms2 = [(wp[kc][:, oc * 128:(oc + 1) * 128],
                         YH[:, kc * 128:(kc + 1) * 128], None, wp_t[kc]) for kc in range(DC)]
                gt2 = mmgroup(P, ps, mms2, deps=ydeps, sub=ps.ap[:, 0:128])
                ev2 = P.op("vector", lambda v, o=stg[:, 0:128], pf=ps.ap[:, 0:128],
                           b=bpcol[:, oc:oc + 1]:
                           v.tensor_scalar(o, pf, b, None, ALU.add),
                           deps=[gt2, t_bp, g.get("_l1_stg")], sig="S_vector")
                dk = P.dma("sync", ydel[oc * 128:(oc + 1) * 128, tci * 128:(tci + 1) * 128],
                           stg[:, 0:128], deps=[ev2])
                ps.consumed = dk
                g["_l1_stg"] = dk
                g["_l1_yh%d" % yh_p] = gt2

    N = len(iters)
    for k in range(N + 1):
        if k < N:
            front(k)
        if k >= 1:
            back(k - 1)
        if k < N:
            middle(k)
    rs_toks = [P.dma("sync", g["XT"][:, c * T:(c + 1) * T], xres[c * 128:(c + 1) * 128, :],
                     deps=[g["_l1_stg"], q_done, k_done]) for c in range(DC)]
    DB = GA.alloc(2 * T)
    db_ld = [None, None]
    for c in range(DC):
        buf = DB[:, (c % 2) * T:(c % 2 + 1) * T]
        lt = P.dma("sync", buf, ydel[c * 128:(c + 1) * 128, :],
                   deps=[g["_l1_stg"], db_ld[c % 2]])
        tk = P.op("vector", lambda v, o=g["XT"][:, c * T:(c + 1) * T], i=buf:
                  v.tensor_tensor(o, o, i, ALU.add), deps=[lt, rs_toks[c]], sig="S_vector")
        db_ld[c % 2] = tk


def fa_racc(g, fa, pp):
    if "_l1_racc" not in g:
        g["_l1_racc"] = fa.alloc(16, F32)
    return g["_l1_racc"][:, pp * 8:(pp + 1) * 8]


def emit_rope(P, g, buf, nblk, tpos0, ready):
    psum = g["psum"]
    SQ = [g["SQ0"], g["SQ1"]]
    ncol = nblk // 4 * 512
    t0 = tpos0 or 0
    last = None
    for ch in range(4):
        for tb in range(ncol // 512):
            col = ch * ncol + tb * 512
            tcol = t0 + tb * 512
            ps = psum[2]
            t1 = P.op("tensor", lambda t, r=buf[:, col:col + 512], pf=ps.ap:
                      t.matmul(pf, g["roperot_sb"], r, start=True, stop=True),
                      deps=[ps.consumed, ready], sig="S_tensor")
            t2 = P.op("vector", lambda v, o=SQ[0], i=buf[:, col:col + 512],
                      cs=g["cos4_sb"][:, tcol:tcol + 512]:
                      v.tensor_tensor(o, i, cs, ALU.mult),
                      deps=[ready, last, g["_cs_tok"], g["_cs_tok2"]], sig="S_vector")
            t3 = P.op("vector", lambda v, o=SQ[1], pf=ps.ap,
                      sn=g["sin4_sb"][:, tcol:tcol + 512]:
                      v.tensor_tensor(o, pf, sn, ALU.mult), deps=[t1, last], sig="S_vector")
            last = P.op("vector", lambda v, o=buf[:, col:col + 512], a=SQ[0], b=SQ[1]:
                        v.tensor_tensor(o, a, b, ALU.add), deps=[t2, t3], sig="S_vector")
            ps.consumed = last
    return last


def emit_L3_ffn(P, g):
    emit_moe(P, g, "L3_ffn", 4096, router_L3, add_xn=False, TBW=256)


def emit_L0_ffn(P, g):
    emit_moe(P, g, "L0_ffn", 2048, router_L0, add_xn=True, TBW=256)


LAYER_EMITTERS = {}


def build_program(layers, weight_names):
    nc = bass.Bass("TRN2", target_bir_lowering=False)
    es = ExitStack()
    g = {}
    xin = nc.dram_tensor("x_in", [DM, T], F32, kind="ExternalInput").ap()
    xout = nc.dram_tensor("x_out", [DM, T], F32, kind="ExternalOutput").ap()
    wdram = {}

    def W(name):
        if name not in wdram:
            shape = list(WEIGHT_SHAPES[name])
            if len(shape) == 0:
                shape = [1]
            if len(shape) == 1:
                shape = [1] + shape
            wdram[name] = nc.dram_tensor(name, shape, F32, kind="ExternalInput").ap()
        return wdram[name]
    g["W"] = W
    scratch = {}

    def dram(name, shape, dtype=F32):
        if name not in scratch:
            scratch[name] = nc.dram_tensor("scr_" + name, list(shape), dtype,
                                           kind="Internal").ap()
        return scratch[name]
    g["dram"] = dram
    g["ones_sb"] = es.enter_context(nc.sbuf_tensor("ones_sb", [128, 128], BF16))[:]
    g["eye_sb"] = es.enter_context(nc.sbuf_tensor("eye_sb", [128, 128], F32))[:]
    g["eyeb_sb"] = es.enter_context(nc.sbuf_tensor("eyeb_sb", [128, 128], BF16))[:]
    g["ones1f_sb"] = es.enter_context(nc.sbuf_tensor("ones1f_sb", [1, 128], F32))[:]
    cones = nc.dram_tensor("c_ones", [128, 128], BF16, kind="ExternalInput").ap()
    ceye = nc.dram_tensor("c_eye", [128, 128], F32, kind="ExternalInput").ap()
    ceyeb = nc.dram_tensor("c_eyeb", [128, 128], BF16, kind="ExternalInput").ap()
    ccols = nc.dram_tensor("c_cols", [128, 16], F32, kind="ExternalInput").ap()
    chm = nc.dram_tensor("c_hmask", [16, 1024], BF16, kind="ExternalInput").ap()
    g["hmask_sb"] = es.enter_context(nc.sbuf_tensor("hmask_sb", [16, 1024], BF16))[:]
    cmf = nc.dram_tensor("c_maskf", [128, 128], F32, kind="ExternalInput").ap()
    g["maskf_sb"] = es.enter_context(nc.sbuf_tensor("maskf_sb", [128, 128], F32))[:]
    g["c_cos4"] = nc.dram_tensor("c_cos4", [128, T], BF16, kind="ExternalInput").ap()
    g["c_sin4"] = nc.dram_tensor("c_sin4", [128, T], BF16, kind="ExternalInput").ap()
    crot = nc.dram_tensor("c_roperot", [128, 128], BF16, kind="ExternalInput").ap()
    g["roperot_sb"] = es.enter_context(nc.sbuf_tensor("roperot_sb", [128, 128], BF16))[:]
    ctri = nc.dram_tensor("c_trilb", [128, 128], BF16, kind="ExternalInput").ap()
    g["trilb_sb"] = es.enter_context(nc.sbuf_tensor("trilb_sb", [128, 128], BF16))[:]
    ctri2 = nc.dram_tensor("c_triub", [128, 128], BF16, kind="ExternalInput").ap()
    g["triub_sb"] = es.enter_context(nc.sbuf_tensor("triub_sb", [128, 128], BF16))[:]
    cmfl = nc.dram_tensor("c_maskfl", [128, 128], F32, kind="ExternalInput").ap()
    g["maskfl_sb"] = es.enter_context(nc.sbuf_tensor("maskfl_sb", [128, 128], F32))[:]
    cpm = nc.dram_tensor("c_pmask", [128, 7 * 128], BF16, kind="ExternalInput").ap()
    g["pmask_sb"] = es.enter_context(nc.sbuf_tensor("pmask_sb", [128, 7 * 128], BF16))[:]
    g["cc_sb"] = es.enter_context(nc.sbuf_tensor("cc_sb", [128, 16], F32))[:]
    g["cc_vals"] = []

    def cc(val):
        val = float(val)
        if val not in g["cc_vals"]:
            assert len(g["cc_vals"]) < 16
            g["cc_vals"].append(val)
        i = g["cc_vals"].index(val)
        return g["cc_sb"][:, i:i + 1]
    g["cc"] = cc
    GAN = 98176 + 4096
    GA = Arena(es.enter_context(nc.sbuf_tensor("GA", [128, GAN], BF16))[:], GAN)
    g["GA"] = g["BA"] = g["FA"] = GA
    g["XT"] = GA.alloc(DC * T, F32)
    g["XTspace"] = GA.ap[:, 0:2 * DC * T]
    g["XN"] = GA.alloc(DC * T)
    g["XNspace"] = GA.ap[:, 2 * DC * T: 3 * DC * T]

    for nm in ("M", "MSQ", "VT", "SD", "R", "MR"):
        if nm == "MSQ":
            g["scr2k"] = GA.ap[:, GA.ptr:GA.ptr + 2048]
        g[nm] = GA.alloc(512, F32)
    g["SQ0"] = GA.alloc(512)
    g["SQ1"] = GA.alloc(512)
    g["colmax"] = GA.alloc(8, F32)
    g["cmax1"] = GA.alloc(1, F32)
    _r2 = GA.alloc(8, F32)
    g["row2"], g["row2b"] = _r2[0:1, 0:4], _r2[0:1, 4:8]
    GA.set_mark()
    g["FAcol"] = Arena(es.enter_context(nc.sbuf_tensor("FAcol", [128, 512], BF16))[:], 512)
    g["psum"] = [Psum(es.enter_context(nc.psum_tensor(f"P{i}", [128, 512], F32))[:])
                 for i in range(8)]
    g["nc"] = nc

    P = Prog()
    g["P"] = P
    ld = []
    for c in range(DC):
        ld.append(P.dma("sync", g["XT"][:, c * T:(c + 1) * T], xin[c * 128:(c + 1) * 128, :]))
    ld.append(P.dma("sync", g["ones_sb"], cones))
    ld.append(P.dma("sync", g["eye_sb"], ceye))
    ld.append(P.dma("sync", g["eyeb_sb"], ceyeb))
    ld.append(P.dma("sync", g["cc_sb"], ccols))
    ld.append(P.dma("sync", g["hmask_sb"], chm))
    ld.append(P.dma("sync", g["maskf_sb"], cmf))
    ld.append(P.dma("sync", g["roperot_sb"], crot))
    ld.append(P.dma("sync", g["trilb_sb"], ctri))
    ld.append(P.dma("sync", g["triub_sb"], ctri2))
    ld.append(P.dma("sync", g["maskfl_sb"], cmfl))
    ld.append(P.dma("sync", g["pmask_sb"], cpm))
    g["ones1_tok"] = P.op("vector", lambda v: v.memset(g["ones1f_sb"], 1.0), sig="S_vector")
    for tok in ld + [g["ones1_tok"]]:
        for e in ENGS:
            P.op(e, None, deps=[tok])

    for i, name in enumerate(layers):
        LAYER_EMITTERS[name](P, g)
        P.barrier()
        g["GA"].reset()
        g["FAcol"].reset()

    sts = []
    for c in range(DC):
        sts.append(P.dma("sync", xout[c * 128:(c + 1) * 128, :], g["XT"][:, c * T:(c + 1) * T]))
    for tok in sts:
        P.op("sync", None, deps=[tok])

    sems = {}
    with ExitStack() as es2:
        for s in sorted(P.sem_names):
            sems[s] = es2.enter_context(nc.semaphore(s))
        block = es2.enter_context(nc.Block())
        P.replay(block, sems)
    es.close()
    return nc, sorted(wdram.keys()), g


WEIGHT_SHAPES = {
    "L0_attn_Wz": (1024, 2048), "L0_attn_Win": (1024, 648),
    "L0_attn_Wzg": (2048, 512), "L0_attn_Wout": (512, 1024),
    "L0_attn_dt_bias": (8,), "L0_attn_A_log": (8,), "L0_attn_D_param": (8,),
    "L0_ffn_rw1": (1024, 1024), "L0_ffn_rb1": (1024,), "L0_ffn_rw2": (1024, 8),
    "L0_ffn_rb2": (8,), "L0_ffn_ew1": (8, 1024, 2048), "L0_ffn_eb1": (8, 2048),
    "L0_ffn_ew2": (8, 2048, 1024), "L0_ffn_eb2": (8, 1024),
    "L1_attn_W_attn": (1024, 3072), "L1_attn_b_attn": (3072,),
    "L1_attn_W_proj": (1024, 1024), "L1_attn_b_proj": (1024,),
    "L1_attn_score_gain": (),
    "L1_ffn_kernel": (1024, 1024), "L1_ffn_bias": (1024,),
    "L1_ffn_a": (1024, 16), "L1_ffn_b": (16, 1024),
    "L2_attn_Wq": (1024, 1024), "L2_attn_Wk": (1024, 1024),
    "L2_attn_Wv": (1024, 1024), "L2_attn_Wo": (1024, 1024),
    "L2_ffn_W_exp": (1024, 4096), "L2_ffn_b_exp": (4096,),
    "L2_ffn_up_weights": (12, 2048, 2, 2), "L2_ffn_up_bias": (4096,),
    "L2_ffn_down_weights": (12, 2048, 2, 2), "L2_ffn_W_con": (4096, 1024),
    "L2_ffn_b_con": (1024,), "L2_ffn_down_bias": (1024,),
    "L3_attn_W_kv": (1024, 256), "L3_attn_g_kv": (256,),
    "L3_attn_W_kdec": (256, 1024), "L3_attn_W_vdec": (256, 1024),
    "L3_attn_W_q": (1024, 512), "L3_attn_W_qdec": (512, 1024),
    "L3_attn_W_krope": (1024, 512), "L3_attn_W_qrope": (512, 512),
    "L3_attn_W_o": (1024, 1024),
    "L3_ffn_gate_w": (1024, 8), "L3_ffn_gate_b": (8,),
    "L3_ffn_W1": (8, 1024, 4096), "L3_ffn_b1": (8, 4096),
    "L3_ffn_W2": (8, 4096, 1024), "L3_ffn_b2": (8, 1024),
}

ALL_LAYERS = ["L0_attn", "L0_ffn", "L1_attn", "L1_ffn",
              "L2_attn", "L2_ffn", "L3_attn", "L3_ffn"]

LAYER_EMITTERS["L1_ffn"] = emit_L1_ffn
LAYER_EMITTERS["L3_ffn"] = emit_L3_ffn
LAYER_EMITTERS["L0_ffn"] = emit_L0_ffn
LAYER_EMITTERS["L2_attn"] = emit_L2_attn
LAYER_EMITTERS["L3_attn"] = emit_L3_attn
LAYER_EMITTERS["L1_attn"] = emit_L1_attn
LAYER_EMITTERS["L0_attn"] = emit_L0_attn
LAYER_EMITTERS["L2_ffn"] = emit_L2_ffn


def _pmasks():
    pm = np.zeros((128, 7 * 128), dtype=ml_dtypes.bfloat16)
    for st in range(7):
        for p in range(128):
            pm[p, st * 128 + (p ^ (1 << st))] = 1
    return pm


def _consts_np():
    hm = np.zeros((16, 1024), dtype=ml_dtypes.bfloat16)
    for c in range(8):
        for p in range(128):
            hm[2 * c + p // 64, c * 128 + p] = 1
    mf = np.triu(np.full((128, 128), -1e9, dtype=np.float32), 1)
    ROPE = 32
    inv_freq = 1.0 / 10000.0 ** (np.arange(0, ROPE, 2, dtype=np.float32) / ROPE)
    tt = np.arange(T, dtype=np.float32)
    emb = np.concatenate([np.outer(tt, inv_freq)] * 2, -1)
    cos1, sin1 = np.cos(emb).T, np.sin(emb).T
    cos4 = np.tile(cos1, (4, 1)).astype(ml_dtypes.bfloat16)
    sin4 = np.tile(sin1, (4, 1)).astype(ml_dtypes.bfloat16)
    rot = np.zeros((128, 128), dtype=np.float32)
    for hh in range(4):
        b = hh * 32
        for i in range(16):
            rot[b + i + 16, b + i] = -1.0
            rot[b + i, b + i + 16] = 1.0
    return {
        "c_ones": np.ones((128, 128), dtype=ml_dtypes.bfloat16),
        "c_eye": np.eye(128, dtype=np.float32),
        "c_eyeb": np.eye(128, dtype=ml_dtypes.bfloat16),
        "c_hmask": hm,
        "c_maskf": mf,
        "c_cos4": cos4,
        "c_sin4": sin4,
        "c_roperot": rot.astype(ml_dtypes.bfloat16),
        "c_trilb": np.tril(np.ones((128, 128))).astype(ml_dtypes.bfloat16),
        "c_triub": np.triu(np.ones((128, 128))).astype(ml_dtypes.bfloat16),
        "c_maskfl": np.tril(np.full((128, 128), -1e9, dtype=np.float32), -1),
        "c_pmask": _pmasks(),
    }


def run_layers(layers, x, inputs):
    nc, wnames, g = build_program(layers, None)
    consts = _consts_np()
    cvals = np.zeros((128, 16), dtype=np.float32)
    for i, v in enumerate(g["cc_vals"]):
        cvals[:, i] = v
    consts["c_cols"] = cvals
    in_maps = []
    for ci in range(NCORES):
        m = {"x_in": np.ascontiguousarray(x[ci].T)}
        for wn in wnames:
            a = np.asarray(inputs[wn], dtype=np.float32)
            if a.ndim == 0:
                a = a.reshape(1, 1)
            elif a.ndim == 1:
                a = a.reshape(1, -1)
            m[wn] = a
        m.update(consts)
        in_maps.append(m)
    res = run_bass_kernel_spmd(nc, in_maps, core_ids=list(range(NCORES)))
    out = np.stack([np.ascontiguousarray(r["x_out"].T) for r in res.results])
    return out


def kernel(**inputs):
    x = np.asarray(inputs["L0_attn_x"], dtype=np.float32)
    return run_layers(ALL_LAYERS, x, inputs)


# version 09
# speedup vs baseline: 1.1019x; 1.0134x over previous
import numpy as np
import ml_dtypes
from collections import defaultdict
from contextlib import ExitStack

import concourse.bass as bass
import concourse.mybir as mybir
from concourse.bass_utils import run_bass_kernel_spmd

F32 = mybir.dt.float32
BF16 = mybir.dt.bfloat16
AF = mybir.ActivationFunctionType
ALU = mybir.AluOpType
AX = mybir.AxisListType.X

T = 2048
NT = T // 512
TC = T // 128
DM = 1024
DC = DM // 128
NCORES = 8
MAGIC = float(np.float32(1.5 * 2 ** 23))

ENGS = ("tensor", "vector", "scalar", "gpsimd", "sync")


class Prog:

    NSLOT = 20

    def __init__(self):
        self.items = {e: [] for e in ENGS}
        self.counts = defaultdict(int)
        self.last_wait = {e: defaultdict(int) for e in ENGS}
        self.sem_names = set()
        self.dma_idx = defaultdict(int)

    def op(self, eng, fn, deps=(), sig=None, n=1):
        waits = []
        for d in deps:
            if d is None:
                continue
            s, v = d
            if self.last_wait[eng][s] >= v:
                continue
            self.last_wait[eng][s] = v
            self.sem_names.add(s)
            waits.append((s, v))
        incs = []
        tok = None
        if sig is not None:
            self.sem_names.add(sig)
            self.counts[sig] += n
            incs.append((sig, n))
            tok = (sig, self.counts[sig])
        self.items[eng].append((fn, waits, incs))
        return tok

    def dma(self, eng, out, in_, deps=(), **kw):
        i = self.dma_idx[eng]
        self.dma_idx[eng] += 1
        sem = f"D_{eng}{i % self.NSLOT}"
        prev = 16 * (i // self.NSLOT)
        deps = list(deps)
        if prev > 0:
            deps.append((sem, prev))
        return self.op(eng, lambda e: e.dma_start(out=out, in_=in_, **kw),
                       deps=deps, sig=sem, n=16)

    def barrier(self):
        toks = []
        for e in ENGS:
            if e in ("vector", "scalar", "gpsimd"):
                toks.append(self.op(e, lambda eng: eng.drain(), sig="B_" + e))
            else:
                toks.append(self.op(e, lambda eng: eng.nop(), sig="B_" + e))
        for e in ENGS:
            self.op(e, None, deps=[t for t in toks if t[0] != "B_" + e])

    def replay(self, block, sems):
        def runner(name):
            def run(engine):
                for fn, waits, incs in self.items[name]:
                    for s, v in waits:
                        engine.wait_ge(sems[s], v)
                    if fn is None:
                        continue
                    inst = fn(engine)
                    for s, nn in incs:
                        inst.then_inc(sems[s], nn)
            return run
        block.tensor(runner("tensor"))
        block.vector(runner("vector"))
        block.scalar(runner("scalar"))
        block.gpsimd(runner("gpsimd"))
        block.sync(runner("sync"))


class Psum:
    def __init__(self, ap):
        self.ap = ap
        self.consumed = None
        self.written = None


class Arena:

    def __init__(self, ap, n):
        self.ap, self.n, self.ptr = ap, n, 0
        self.hi = 0
        self.mark = 0

    def alloc(self, elems, dtype=BF16, align=2):
        nb = elems if dtype == BF16 else 2 * elems
        self.ptr = (self.ptr + align - 1) // align * align
        assert self.ptr + nb <= self.n, f"arena overflow {self.ptr}+{nb}>{self.n}"
        v = self.ap[:, self.ptr:self.ptr + nb]
        if dtype != BF16:
            v = v.bitcast(dtype)
        self.ptr += nb
        self.hi = max(self.hi, self.ptr)
        return v

    def set_mark(self):
        self.mark = self.ptr

    def reset(self):
        self.ptr = self.mark


def mmgroup(P, ps, mms, deps=(), sub=None):
    tok = None
    nmm = len(mms)
    for i, (l, r, o, d) in enumerate(mms):
        first, last = (i == 0), (i == nmm - 1)
        o = (ps.ap if sub is None else sub) if o is None else o
        dd = list(deps) + [d] + ([ps.consumed] if first else [])
        tok = P.op("tensor",
                   lambda t, oo=o, ll=l, rr=r, st=first, sp=last:
                   t.matmul(oo, ll, rr, start=st, stop=sp),
                   deps=dd, sig="S_tensor" if last else None)
    ps.written = tok
    return tok


def emit_layernorm(P, g, xdep=()):
    XT, XN = g["XT"], g["XN"]
    ones = g["ones_sb"]
    SQ = [g["SQ0"], g["SQ1"]]
    ps_s, ps_q = g["psum"][6], g["psum"][7]
    M, MSQ, VT, SD, R, MR = g["M"], g["MSQ"], g["VT"], g["SD"], g["R"], g["MR"]
    stats_prev = None
    done = []
    for tb in range(NT):
        cp_toks = []
        sq_mm = [None, None]
        t1 = t2 = None
        for c in range(DC):
            xt_c = XT[:, c * T + tb * 512: c * T + (tb + 1) * 512]
            xn_c = XN[:, c * T + tb * 512: c * T + (tb + 1) * 512]
            tcp = P.op("vector", lambda v, o=xn_c, i=xt_c: v.tensor_copy(o, i),
                       deps=list(xdep), sig="S_vector")
            cp_toks.append(tcp)
            sq = SQ[c % 2]
            tsq = P.op("scalar", lambda s, o=sq, i=xn_c: s.activation(o, i, AF.Square),
                       deps=[tcp, sq_mm[c % 2]], sig="S_scalar")
            t1 = P.op("tensor", lambda t, i=xn_c, st=(c == 0), sp=(c == DC - 1):
                      t.matmul(ps_s.ap, ones, i, start=st, stop=sp),
                      deps=[tcp] + ([ps_s.consumed, stats_prev] if c == 0 else []),
                      sig="S_tensor" if c == DC - 1 else None)
            t2 = P.op("tensor", lambda t, i=sq, st=(c == 0), sp=(c == DC - 1):
                      t.matmul(ps_q.ap, ones, i, start=st, stop=sp),
                      deps=[tsq] + ([ps_q.consumed] if c == 0 else []),
                      sig="S_tensor")
            sq_mm[c % 2] = t2
        tM = P.op("vector", lambda v: v.tensor_scalar(M, ps_s.ap, 1.0 / DM, None, ALU.mult),
                  deps=[t1, stats_prev], sig="S_vector")
        ps_s.consumed = tM
        tM2 = P.op("scalar", lambda s: s.activation(MSQ, M, AF.Square),
                   deps=[tM], sig="S_scalar")
        tV = P.op("vector", lambda v: v.scalar_tensor_tensor(
            VT, ps_q.ap, 1.0 / DM, MSQ, ALU.mult, ALU.subtract),
            deps=[t2, tM2], sig="S_vector")
        ps_q.consumed = tV
        tSD = P.op("scalar", lambda s: s.activation(SD, VT, AF.Sqrt, bias=g["cc"](1e-5)),
                   deps=[tV], sig="S_scalar")
        tR = P.op("vector", lambda v: v.reciprocal(R, SD), deps=[tSD], sig="S_vector")
        tMR = P.op("vector", lambda v: v.tensor_tensor(MR, M, R, ALU.mult),
                   deps=[tR], sig="S_vector")
        tk = None
        for c in range(DC):
            xn_c = XN[:, c * T + tb * 512: c * T + (tb + 1) * 512]
            eng = "vector" if c % 2 == 0 else "gpsimd"
            tn1 = P.op(eng, lambda v, o=xn_c: v.tensor_tensor(o, o, R, ALU.mult),
                       deps=[tR, t1, t2, cp_toks[c]], sig="S_" + eng)
            tk = P.op(eng, lambda v, o=xn_c: v.tensor_tensor(o, o, MR, ALU.subtract),
                      deps=[tMR, tn1], sig="S_" + eng)
            if tb == NT - 1 and c >= DC - 2:
                done.append(tk)
        stats_prev = tk
    return done


def load_wslabs(P, g, dram_ap, in_dim, out_dim, deps=()):
    KC = (in_dim + 127) // 128
    views, toks = [], []
    for kc in range(KC):
        kw = min(128, in_dim - kc * 128)
        v = g["BA"].alloc(out_dim)
        tk = P.dma("gpsimd", v[0:kw, :], dram_ap[kc * 128: kc * 128 + kw, :], deps=deps)
        views.append(v)
        toks.append(tk)
    return views, toks


def residual_evac(P, g, oc, t0, n, ps, deps, bias_col=None):
    xt = g["XT"][:, oc * T + t0: oc * T + t0 + n]
    if bias_col is not None:
        tk = P.op("vector", lambda v: v.scalar_tensor_tensor(
            xt, ps.ap[:, 0:n], bias_col, xt, ALU.add, ALU.add),
            deps=deps, sig="S_vector")
    else:
        tk = P.op("vector", lambda v: v.tensor_tensor(xt, xt, ps.ap[:, 0:n], ALU.add),
                  deps=deps, sig="S_vector")
    ps.consumed = tk
    return tk


def bcast_cols(P, g, row_ap, n, deps):
    pb = g["psum"][4]
    t1 = P.op("tensor", lambda t: t.matmul(pb.ap[:, 0:n], g["ones1f_sb"], row_ap,
                                           start=True, stop=True),
              deps=list(deps) + [pb.consumed], sig="S_tensor")
    out = g["FA"].alloc(n, F32)
    t2 = P.op("vector", lambda v: v.tensor_copy(out, pb.ap[:, 0:n]),
              deps=[t1], sig="S_vector")
    pb.consumed = t2
    return out, t2


def emit_L1_ffn(P, g):
    W = g["W"]
    xdeps = emit_layernorm(P, g)
    BA, FA = g["BA"], g["FA"]
    XN = g["XN"]
    psum = g["psum"]

    kslab = [FA.alloc(1024, F32) for _ in range(DC)]
    ktok = [P.dma("sync", kslab[kc], W("L1_ffn_kernel")[kc * 128:(kc + 1) * 128, :])
            for kc in range(DC)]
    CM = g["colmax"]
    cm_toks = []
    for kc in range(DC):
        tk = P.op("vector", lambda v, o=CM[:, kc:kc + 1], i=kslab[kc]:
                  v.tensor_reduce(o, i, AX, ALU.max, apply_absolute_value=True),
                  deps=[ktok[kc]], sig="S_vector")
        cm_toks.append(tk)
    t1 = P.op("vector", lambda v: v.tensor_reduce(g["cmax1"], CM, AX, ALU.max),
              deps=cm_toks, sig="S_vector")
    pt = psum[5]
    t2 = P.op("tensor", lambda t: t.transpose(pt.ap[0:1, 0:128], g["cmax1"], g["eye_sb"]),
              deps=[t1, pt.consumed], sig="S_tensor")
    t3 = P.op("vector", lambda v: v.tensor_reduce(
        g["row2"][0:1, 0:1], pt.ap[0:1, 0:128], AX, ALU.max), deps=[t2], sig="S_vector")
    pt.consumed = t3
    t4 = P.op("vector", lambda v: v.reciprocal(g["row2"][0:1, 1:2], g["row2"][0:1, 0:1]),
              deps=[t3], sig="S_vector")
    t5a = P.op("vector", lambda v: v.tensor_scalar(
        g["row2b"][0:1, 0:1], g["row2"][0:1, 1:2], 7.0, None, ALU.mult),
        deps=[t4], sig="S_vector")
    t5b = P.op("vector", lambda v: v.tensor_scalar(
        g["row2b"][0:1, 1:2], g["row2"][0:1, 0:1], 1.0 / 7.0, None, ALU.mult),
        deps=[t3], sig="S_vector")
    qs, tbc = bcast_cols(P, g, g["row2b"][0:1, 0:2], 2, [t5a, t5b])
    ic, sc = qs[:, 0:1], qs[:, 1:2]
    wq_views, wq_toks = [], []
    for kc in range(DC):
        wv = BA.alloc(1024)
        a = kslab[kc]
        tq1 = P.op("vector", lambda v, a=a: v.tensor_scalar(a, a, ic, MAGIC, ALU.mult, ALU.add),
                   deps=[tbc, cm_toks[kc]], sig="S_vector")
        tq2 = P.op("vector", lambda v, a=a: v.tensor_scalar(a, a, MAGIC, -8.0, ALU.subtract, ALU.max),
                   deps=[tq1], sig="S_vector")
        tq3 = P.op("vector", lambda v, a=a, o=wv: v.tensor_scalar(o, a, 7.0, sc, ALU.min, ALU.mult),
                   deps=[tq2], sig="S_vector")
        wq_views.append(wv)
        wq_toks.append(tq3)

    aslab, atoks = load_wslabs(P, g, W("L1_ffn_a"), DM, 16)
    bslab = BA.alloc(1024)
    btok = P.dma("gpsimd", bslab[0:16, :], W("L1_ffn_b"))
    bias_sb = g["FAcol"].alloc(8, F32)
    btok2 = P.dma("gpsimd", bias_sb,
                  W("L1_ffn_bias").rearrange("one (oc p) -> p (one oc)", p=128),
                  allow_slow_non_contiguous=True)

    XA = BA.alloc(T)
    ps_xa = psum[3]
    xa_toks = []
    for tb in range(NT):
        mms = [(aslab[kc][:, 0:16],
                XN[:, kc * T + tb * 512: kc * T + (tb + 1) * 512],
                ps_xa.ap[0:16, 0:512], atoks[kc]) for kc in range(DC)]
        gt = mmgroup(P, ps_xa, mms, deps=xdeps)
        ev = P.op("vector", lambda v, o=XA[0:16, tb * 512:(tb + 1) * 512]:
                  v.tensor_copy(o, ps_xa.ap[0:16, 0:512]), deps=[gt], sig="S_vector")
        ps_xa.consumed = ev
        xa_toks.append(ev)

    for oc in range(DC):
        for tb in range(NT):
            ps = psum[(oc * NT + tb) % 3]
            mms = [(wq_views[kc][:, oc * 128:(oc + 1) * 128],
                    XN[:, kc * T + tb * 512: kc * T + (tb + 1) * 512],
                    None, wq_toks[kc]) for kc in range(DC)]
            mms.append((bslab[0:16, oc * 128:(oc + 1) * 128],
                        XA[0:16, tb * 512:(tb + 1) * 512], None, btok))
            gt = mmgroup(P, ps, mms, deps=list(xdeps) + [xa_toks[tb], btok2])
            residual_evac(P, g, oc, tb * 512, 512, ps, [gt],
                          bias_col=bias_sb[:, oc:oc + 1])


def gating_topk2(P, g, LG8, lg_tok, GW, GWT, E=8):
    fa = g["FAcol"]
    scr = fa.alloc(E, F32)
    scr2 = fa.alloc(E, F32)
    E8 = fa.alloc(E, F32)
    SEL = fa.alloc(E, F32)
    m1 = fa.alloc(1, F32)
    nm1 = fa.alloc(1, F32)
    m2 = fa.alloc(1, F32)
    zz = fa.alloc(1, F32)
    rz = fa.alloc(1, F32)
    ps_g = g["psum"][5]
    gw_toks = []
    gwt_toks = []
    for tci in range(TC):
        tt = P.op("tensor", lambda t, i=LG8[0:8, tci * 128:(tci + 1) * 128]:
                  t.transpose(ps_g.ap[0:128, 0:8], i, g["eye_sb"][0:8, 0:8]),
                  deps=[lg_tok, ps_g.consumed], sig="S_tensor")
        t0 = P.op("vector", lambda v: v.tensor_copy(scr, ps_g.ap[0:128, 0:8]),
                  deps=[tt], sig="S_vector")
        ps_g.consumed = t0
        tm1 = P.op("vector", lambda v: v.tensor_reduce(m1, scr, AX, ALU.max),
                   deps=[t0], sig="S_vector")
        tn = P.op("vector", lambda v: v.tensor_scalar(nm1, m1, -1.0, None, ALU.mult),
                  deps=[tm1], sig="S_vector")
        te = P.op("scalar", lambda s: s.activation(E8, scr, AF.Exp, bias=nm1),
                  deps=[t0, tn], sig="S_scalar")
        teq = P.op("vector", lambda v: v.tensor_scalar(SEL, scr, m1, None, ALU.is_ge),
                   deps=[tm1], sig="S_vector")
        tl2 = P.op("vector", lambda v: v.scalar_tensor_tensor(
            scr2, SEL, g["cc"](-1e30), scr, ALU.mult, ALU.add),
            deps=[teq], sig="S_vector")
        tm2 = P.op("vector", lambda v: v.tensor_reduce(m2, scr2, AX, ALU.max),
                   deps=[tl2], sig="S_vector")
        tsel = P.op("vector", lambda v: v.tensor_scalar(SEL, scr, m2, None, ALU.is_ge),
                    deps=[tm2], sig="S_vector")
        tes = P.op("vector", lambda v: v.tensor_tensor(E8, E8, SEL, ALU.mult),
                   deps=[te, tsel], sig="S_vector")
        tz = P.op("vector", lambda v: v.tensor_reduce(zz, E8, AX, ALU.add),
                  deps=[tes], sig="S_vector")
        trz = P.op("vector", lambda v: v.reciprocal(rz, zz), deps=[tz], sig="S_vector")
        gwc = GW[:, tci * E:(tci + 1) * E]
        tw = P.op("vector", lambda v, o=gwc: v.tensor_scalar(o, E8, rz, None, ALU.mult),
                  deps=[trz], sig="S_vector")
        gw_toks.append(tw)
        tt2 = P.op("tensor", lambda t, i=gwc: t.transpose(
            ps_g.ap[0:8, 0:128], i, g["eye_sb"]),
            deps=[tw, ps_g.consumed], sig="S_tensor")
        tc2 = P.op("vector", lambda v, o=GWT[0:8, tci * 128:(tci + 1) * 128]:
                   v.tensor_copy(o, ps_g.ap[0:8, 0:128]), deps=[tt2], sig="S_vector")
        ps_g.consumed = tc2
        gwt_toks.append(tc2)
    return gw_toks, gwt_toks


def emit_moe(P, g, pref, HID, router_fn, add_xn, TBW):
    E = 8
    HC = HID // 128
    W, GA, XN, psum = g["W"], g["GA"], g["XN"], g["psum"]
    xdeps = emit_layernorm(P, g)
    xres = g["dram"](pref + "_xres", [DM, T])
    acc = g["dram"](pref + "_acc", [T, DM])
    sp_toks = [P.dma("sync", xres[c * 128:(c + 1) * 128, :], g["XT"][:, c * T:(c + 1) * T],
                     deps=list(xdeps)) for c in range(DC)]
    GW = GA.alloc(TC * E, F32)
    GWT = GA.alloc(T)
    ga_save = GA.ptr
    LG8, lg_tok = router_fn(P, g, xdeps)
    gw_toks, gwt_toks = gating_topk2(P, g, LG8, lg_tok, GW, GWT)
    GA.ptr = ga_save
    gate_done = [gw_toks[-1], gwt_toks[-1]]
    b2slab = GA.alloc(DM)
    b2tok = P.dma("gpsimd", b2slab[0:8, :],
                  W(pref + ("_b2" if pref == "L3_ffn" else "_eb2")),
                  deps=gate_done)
    b1col = g["FAcol"].alloc(HC, F32)
    b1name = pref + ("_b1" if pref == "L3_ffn" else "_eb1")
    w1name = pref + ("_W1" if pref == "L3_ffn" else "_ew1")
    w2name = pref + ("_W2" if pref == "L3_ffn" else "_ew2")
    ga_w2 = GA.ptr
    W2space = GA.alloc(HC * 1024)
    h_tb = GA.alloc(HC * TBW)
    stage = [g["M"], g["VT"]]

    NTB = T // TBW
    ntc2 = TBW // 128
    prev_acc = [None] * (2 * TC)
    prev_h_last = None
    prev_out_last = None
    w1_views = [g["XTspace"][:, kc * HID:(kc + 1) * HID] for kc in range(DC)]
    for e in range(E):
        w1_toks = []
        for kc in range(DC):
            tk = P.dma("gpsimd", w1_views[kc],
                       W(w1name)[e:e + 1, kc * 128:(kc + 1) * 128, :],
                       deps=(sp_toks + list(xdeps) if e == 0 else [prev_h_last]))
            w1_toks.append(tk)
        w2_toks = []
        for hc in range(HC):
            tk = P.dma("gpsimd", W2space[:, hc * 1024:(hc + 1) * 1024],
                       W(w2name)[e:e + 1, hc * 128:(hc + 1) * 128, :],
                       deps=(gate_done if e == 0 else [prev_out_last]))
            w2_toks.append(tk)
        tb1 = P.dma("gpsimd", b1col,
                    W(b1name)[e:e + 1, :].rearrange("one (hc p) -> p (one hc)", p=128),
                    deps=[prev_h_last] if e > 0 else [],
                    allow_slow_non_contiguous=True)
        h_toks = {}
        out_last = None
        for tw in range(NTB):
            for hc in range(HC):
                ps_h = psum[hc % 2]
                mms = [(w1_views[kc][:, hc * 128:(hc + 1) * 128],
                        XN[:, kc * T + tw * TBW: kc * T + (tw + 1) * TBW],
                        None, w1_toks[kc]) for kc in range(DC)]
                gt = mmgroup(P, ps_h, mms, deps=list(xdeps), sub=ps_h.ap[:, 0:TBW])
                hv = h_tb[:, hc * TBW:(hc + 1) * TBW]
                ev = P.op("scalar", lambda s, o=hv, i=ps_h.ap[:, 0:TBW], b=b1col[:, hc:hc + 1]:
                          s.activation(o, i, AF.Silu, bias=b), deps=[gt, tb1, out_last],
                          sig="S_scalar")
                ps_h.consumed = ev
                h_toks[hc] = ev
                prev_h_last = gt
            for tc2 in range(ntc2):
                tci = tw * ntc2 + tc2
                for half in range(2):
                    ps_o = psum[2 + half]
                    mms = [(h_tb[:, hc * TBW + tc2 * 128: hc * TBW + (tc2 + 1) * 128],
                            W2space[:, hc * 1024 + half * 512: hc * 1024 + (half + 1) * 512],
                            None, w2_toks[hc]) for hc in range(HC)]
                    gt = mmgroup(P, ps_o, mms,
                                 deps=[h_toks[hc] for hc in range(HC)][-1:] + [h_toks[0]])
                    out_last = gt
                    wcol = GW[:, tci * E + e: tci * E + e + 1]
                    ev = P.op("vector", lambda v, o=stage[half], i=ps_o.ap, w=wcol:
                              v.tensor_scalar(o, i, w, None, ALU.mult),
                              deps=[gt, gw_toks[tci],
                                    prev_acc[2 * tci + half]], sig="S_vector")
                    dk = P.dma("gpsimd",
                               acc[tci * 128:(tci + 1) * 128, half * 512:(half + 1) * 512],
                               stage[half],
                               deps=[ev] + ([prev_acc[2 * tci + half]] if prev_acc[2 * tci + half] else []),
                               **({} if e == 0 else {"accum_op": ALU.add}))
                    prev_acc[2 * tci + half] = dk
                    ps_o.consumed = dk
        prev_out_last = out_last
    rs_toks = [P.dma("sync", g["XT"][:, c * T:(c + 1) * T], xres[c * 128:(c + 1) * 128, :],
                     deps=[prev_h_last]) for c in range(DC)]
    GA.ptr = ga_w2
    ACCS = [GA.alloc(DM, F32), GA.alloc(DM, F32)]
    acc_ld = [None, None]
    for tci in range(TC):
        buf = ACCS[tci % 2]
        lt = P.dma("sync", buf, acc[tci * 128:(tci + 1) * 128, :],
                   deps=[prev_acc[2 * tci], prev_acc[2 * tci + 1], acc_ld[tci % 2],
                         prev_out_last])
        for fc in range(DC):
            ps_f = psum[4 + (fc % 2)]
            t1 = P.op("tensor", lambda t, i=buf[:, fc * 128:(fc + 1) * 128],
                      pf=ps_f.ap[:, 0:128]:
                      t.transpose(pf, i, g["eye_sb"]),
                      deps=[lt, ps_f.consumed], sig="S_tensor")
            t2 = P.op("tensor", lambda t, l=b2slab[0:8, fc * 128:(fc + 1) * 128],
                      r=GWT[0:8, tci * 128:(tci + 1) * 128], pf=ps_f.ap[:, 0:128]:
                      t.matmul(pf, l, r, start=False, stop=True,
                               skip_group_check=True),
                      deps=[t1, b2tok, gwt_toks[tci]], sig="S_tensor")
            xt = g["XT"][:, fc * T + tci * 128: fc * T + (tci + 1) * 128]
            t3 = P.op("vector", lambda v, o=xt, pf=ps_f.ap[:, 0:128]:
                      v.tensor_tensor(o, o, pf, ALU.add),
                      deps=[t2, rs_toks[fc]], sig="S_vector")
            ps_f.consumed = t3
            if add_xn:
                xnv = XN[:, fc * T + tci * 128: fc * T + (tci + 1) * 128]
                t3 = P.op("vector", lambda v, o=xt, i=xnv: v.tensor_tensor(o, o, i, ALU.add),
                          deps=[t3], sig="S_vector")
            acc_ld[tci % 2] = t3 if fc == DC - 1 else acc_ld[tci % 2]


def router_L3(P, g, xdeps):
    W, GA, XN, psum = g["W"], g["GA"], g["XN"], g["psum"]
    slabs, toks = load_wslabs(P, g, W("L3_ffn_gate_w"), DM, 8)
    gbcol = g["FAcol"].alloc(1, F32)
    gbt = P.dma("gpsimd", gbcol[0:8, :],
                W("L3_ffn_gate_b").rearrange("one e -> e one"),
                allow_slow_non_contiguous=True)
    LG8 = GA.alloc(T, F32)
    ps = psum[4]
    tk = None
    for tb in range(NT):
        mms = [(slabs[kc], XN[:, kc * T + tb * 512: kc * T + (tb + 1) * 512],
                ps.ap[0:8, 0:512], toks[kc]) for kc in range(DC)]
        gt = mmgroup(P, ps, mms, deps=list(xdeps))
        tk = P.op("vector", lambda v, o=LG8[0:8, tb * 512:(tb + 1) * 512]:
                  v.tensor_scalar(o, ps.ap[0:8, 0:512], gbcol[0:8, :], None, ALU.add),
                  deps=[gt, gbt], sig="S_vector")
        ps.consumed = tk
    return LG8, tk


def router_L0(P, g, xdeps):
    W, GA, XN, psum = g["W"], g["GA"], g["XN"], g["psum"]
    r1slabs, r1toks = load_wslabs(P, g, W("L0_ffn_rw1"), DM, DM)
    rb1col = g["FAcol"].alloc(DC, F32)
    t_rb1 = P.dma("gpsimd", rb1col,
                  W("L0_ffn_rb1").rearrange("one (oc p) -> p (one oc)", p=128),
                  allow_slow_non_contiguous=True)
    R1 = GA.alloc(DC * T)
    r1_done = []
    for oc in range(DC):
        for tb in range(NT):
            ps = psum[2 + (tb % 2)]
            mms = [(r1slabs[kc][:, oc * 128:(oc + 1) * 128],
                    XN[:, kc * T + tb * 512: kc * T + (tb + 1) * 512],
                    None, r1toks[kc]) for kc in range(DC)]
            gt = mmgroup(P, ps, mms, deps=list(xdeps))
            ev = P.op("scalar", lambda s, o=R1[:, oc * T + tb * 512: oc * T + (tb + 1) * 512],
                      i=ps.ap, b=rb1col[:, oc:oc + 1]:
                      s.activation(o, i, AF.Silu, bias=b), deps=[gt, t_rb1], sig="S_scalar")
            ps.consumed = ev
            r1_done.append(ev)
    r2slabs, r2toks = load_wslabs(P, g, W("L0_ffn_rw2"), DM, 8)
    rb2col = g["FAcol"].alloc(1, F32)
    t_rb2 = P.dma("gpsimd", rb2col[0:8, :],
                  W("L0_ffn_rb2").rearrange("one e -> e one"),
                  allow_slow_non_contiguous=True)
    LG8 = GA.alloc(T, F32)
    ps = psum[4]
    tk = None
    for tb in range(NT):
        mms = [(r2slabs[kc], R1[:, kc * T + tb * 512: kc * T + (tb + 1) * 512],
                ps.ap[0:8, 0:512], r2toks[kc]) for kc in range(DC)]
        gt = mmgroup(P, ps, mms, deps=r1_done[-1:] + [r1_done[tb]])
        tk = P.op("vector", lambda v, o=LG8[0:8, tb * 512:(tb + 1) * 512]:
                  v.tensor_scalar(o, ps.ap[0:8, 0:512], rb2col[0:8, :], None, ALU.add),
                  deps=[gt, t_rb2], sig="S_vector")
        ps.consumed = tk
    return LG8, tk


def transpose_weight(P, g, dram_ap, n_rows, n_cols, deps=()):
    GA, psum = g["GA"], g["psum"]
    RC, CC = n_rows // 128, n_cols // 128
    out = [GA.alloc(n_rows) for _ in range(CC)]
    stg = [GA.alloc(128, F32), GA.alloc(128, F32)]
    stok = [None, None]
    ps = psum[5]
    k = 0
    for dc in range(CC):
        for rc in range(RC):
            sb = stg[k % 2]
            lt = P.dma("sync", sb, dram_ap[rc * 128:(rc + 1) * 128, dc * 128:(dc + 1) * 128],
                       deps=[stok[k % 2]] + list(deps))
            off = (rc % 4) * 128
            tt = P.op("tensor", lambda t, i=sb, pf=ps.ap[:, off:off + 128]:
                      t.transpose(pf, i, g["eye_sb"]),
                      deps=[lt, ps.consumed if rc % 4 == 0 else None], sig="S_tensor")
            stok[k % 2] = tt
            k += 1
            if rc % 4 == 3 or rc == RC - 1:
                lo = (rc // 4) * 4 * 128
                w = (rc % 4 + 1) * 128
                ev = P.op("vector", lambda v, o=out[dc][:, lo:lo + w],
                          pf=ps.ap[:, 0:w]: v.tensor_copy(o, pf),
                          deps=[tt], sig="S_vector")
                ps.consumed = ev
    return out, ps.consumed


def elu1_evac(P, g, dst, ps, n, deps, scr):
    prev = g.get("_elu_prev")
    t1 = P.op("vector", lambda v, o=scr[0][:, 0:n], i=ps.ap[:, 0:n]:
              v.tensor_scalar(o, i, 0.0, None, ALU.min),
              deps=list(deps) + [prev], sig="S_vector")
    t2 = P.op("scalar", lambda s, o=scr[0][:, 0:n]: s.activation(o, o, AF.Exp),
              deps=[t1, prev], sig="S_scalar")
    t3 = P.op("vector", lambda v, o=scr[1][:, 0:n], i=ps.ap[:, 0:n]:
              v.tensor_scalar(o, i, 0.0, None, ALU.max), deps=deps, sig="S_vector")
    t4 = P.op("vector", lambda v, o=dst, a=scr[0][:, 0:n], b=scr[1][:, 0:n]:
              v.tensor_tensor(o, a, b, ALU.add), deps=[t2, t3], sig="S_vector")
    ps.consumed = t4
    g["_elu_prev"] = t4
    return t4


def emit_L2_attn(P, g):
    W, GA, XN, psum = g["W"], g["GA"], g["XN"], g["psum"]
    H, HD = 16, 64
    xdeps = emit_layernorm(P, g)
    xres = g["dram"]("l2a_xres", [DM, T])
    sp_toks = [P.dma("sync", xres[c * 128:(c + 1) * 128, :], g["XT"][:, c * T:(c + 1) * T],
                     deps=list(xdeps)) for c in range(DC)]
    KTOK = g["XTspace"][:, 0:TC * 1024]
    VTOK = g["XTspace"][:, TC * 1024:2 * TC * 1024]
    scr = [g["SQ0"], g["SQ1"]]
    OUTC = GA.alloc(DC * T)
    ga_post_outc = GA.ptr
    QF = GA.alloc(DC * T)
    ga_save = GA.ptr

    def proj_tokmajor(wname, dst, elu, first_deps):
        WT, wt_done = transpose_weight(P, g, W(wname), DM, DM, deps=first_deps)
        toks = []
        for tc in range(TC):
            for fb in range(2):
                ps = psum[fb]
                mms = [(XN[:, kc * T + tc * 128: kc * T + (tc + 1) * 128],
                        WT[kc][:, fb * 512:(fb + 1) * 512], None, wt_done)
                       for kc in range(DC)]
                gt = mmgroup(P, ps, mms, deps=list(xdeps) + (sp_toks if wname == "L2_attn_Wk" else []))
                dv = dst[:, tc * 1024 + fb * 512: tc * 1024 + (fb + 1) * 512]
                if elu:
                    tk = elu1_evac(P, g, dv, ps, 512, [gt], scr)
                else:
                    tk = P.op("vector", lambda v, o=dv, i=ps.ap: v.tensor_copy(o, i),
                              deps=[gt], sig="S_vector")
                    ps.consumed = tk
                toks.append(tk)
        return toks

    k_toks = proj_tokmajor("L2_attn_Wk", KTOK, True, [])
    GA.ptr = ga_save
    v_toks = proj_tokmajor("L2_attn_Wv", VTOK, False, [k_toks[-1]])
    GA.ptr = ga_save
    WTq, wtq_done = transpose_weight(P, g, W("L2_attn_Wq"), DM, DM, deps=[v_toks[-1]])
    q_toks = []
    for oc in range(DC):
        for tb in range(NT):
            ps = psum[2 + ((oc * NT + tb) % 2)]
            mms = [(WTq[kc][:, oc * 128:(oc + 1) * 128],
                    XN[:, kc * T + tb * 512: kc * T + (tb + 1) * 512],
                    None, wtq_done) for kc in range(DC)]
            gt = mmgroup(P, ps, mms, deps=list(xdeps))
            dv = QF[:, oc * T + tb * 512: oc * T + (tb + 1) * 512]
            q_toks.append(elu1_evac(P, g, dv, ps, 512, [gt], scr))
    GA.ptr = ga_save

    KVBD = GA.alloc(DC * 128)
    kdep = [k_toks[-1], v_toks[-1]]
    kv_toks = []
    zt = P.op("vector", lambda v: v.memset(KVBD, 0.0), deps=[], sig="S_vector")
    for c in range(DC):
        toks2 = []
        banks = [psum[c % 2], psum[2 + c % 2]]
        for par in range(2):
            h = 2 * c + par
            ps = banks[par]
            mms = [(KTOK[:, tc * 1024 + c * 128: tc * 1024 + (c + 1) * 128],
                    VTOK[:, tc * 1024 + h * 64: tc * 1024 + (h + 1) * 64],
                    ps.ap[:, 0:64], None) for tc in range(TC)]
            toks2.append(mmgroup(P, ps, mms, deps=kdep))
        c1 = P.op("vector", lambda v, o=KVBD[0:64, c * 128: c * 128 + 64],
                  pf=banks[0].ap[0:64, 0:64]: v.tensor_copy(o, pf),
                  deps=[toks2[0], zt], sig="S_vector")
        c2 = P.op("vector", lambda v, o=KVBD[64:128, c * 128 + 64: c * 128 + 128],
                  pf=banks[1].ap[64:128, 0:64]: v.tensor_copy(o, pf),
                  deps=[toks2[1]], sig="S_vector")
        banks[0].consumed = c1
        banks[1].consumed = c2
        kv_toks.append(c2)

    KS = GA.alloc(1024, F32)
    ks_toks = []
    for fb in range(2):
        ps = psum[2 + fb]
        mms = [(g["ones_sb"], KTOK[:, tc * 1024 + fb * 512: tc * 1024 + (fb + 1) * 512],
                None, None) for tc in range(TC)]
        gt = mmgroup(P, ps, mms, deps=kdep)
        ev = P.op("vector", lambda v, o=KS[:, fb * 512:(fb + 1) * 512], i=ps.ap:
                  v.tensor_copy(o, i), deps=[gt], sig="S_vector")
        ps.consumed = ev
        ks_toks.append(ev)
    DKL = GA.alloc(16 * DC)
    tz = P.op("vector", lambda v: v.memset(DKL, 0.0), deps=[], sig="S_vector")
    dkl_toks = []
    ps5 = psum[5]
    for c in range(DC):
        tt = P.op("tensor", lambda t, i=KS[:, c * 128:(c + 1) * 128],
                  pf=ps5.ap[:, 0:128]: t.transpose(pf, i, g["eye_sb"]),
                  deps=[ks_toks[-1], ps5.consumed], sig="S_tensor")
        h0, h1 = 2 * c, 2 * c + 1
        c1 = P.op("vector", lambda v, o=DKL[0:64, c * 16 + h0: c * 16 + h0 + 1],
                  pf=ps5.ap[0:64, 0:1]: v.tensor_copy(o, pf), deps=[tt, tz], sig="S_vector")
        c2 = P.op("vector", lambda v, o=DKL[64:128, c * 16 + h1: c * 16 + h1 + 1],
                  pf=ps5.ap[64:128, 0:1]: v.tensor_copy(o, pf), deps=[tt], sig="S_vector")
        ps5.consumed = c2
        dkl_toks.append(c2)
    DRCPB = GA.alloc(T)
    den_toks = []
    for tb in range(NT):
        ps = psum[4]
        mms = [(DKL[:, c * 16:(c + 1) * 16],
                QF[:, c * T + tb * 512: c * T + (tb + 1) * 512],
                ps.ap[0:16, 0:512], dkl_toks[c]) for c in range(DC)]
        gt = mmgroup(P, ps, mms, deps=[q_toks[-1]])
        t1 = P.op("vector", lambda v, pf=ps.ap[0:16, 0:512]:
                  v.tensor_scalar(pf, pf, 1e-6, None, ALU.add), deps=[gt], sig="S_vector")
        t2 = P.op("vector", lambda v, o=g["M"][0:16, 0:512], pf=ps.ap[0:16, 0:512]:
                  v.reciprocal(o, pf), deps=[t1], sig="S_vector")
        t3 = P.op("vector", lambda v, o=DRCPB[0:16, tb * 512:(tb + 1) * 512],
                  i=g["M"][0:16, 0:512]: v.tensor_copy(o, i), deps=[t2], sig="S_vector")
        ps.consumed = t3
        den_toks.append(t3)
    HM = g["hmask_sb"]
    RB = GA.alloc(512, F32)
    num_toks = []
    prev_ev = None
    for c in range(DC):
        for tb in range(NT):
            ps_rb = psum[4]
            tb1 = P.op("tensor", lambda t, l=HM[0:16, c * 128:(c + 1) * 128],
                       r=DRCPB[0:16, tb * 512:(tb + 1) * 512], pf=ps_rb.ap:
                       t.matmul(pf, l, r, start=True, stop=True),
                       deps=[den_toks[tb], ps_rb.consumed], sig="S_tensor")
            tb2 = P.op("vector", lambda v, o=RB, pf=ps_rb.ap: v.tensor_copy(o, pf),
                       deps=[tb1, prev_ev], sig="S_vector")
            ps_rb.consumed = tb2
            ps_n = psum[(c * NT + tb) % 2]
            t2 = P.op("tensor", lambda t, ll=KVBD[:, c * 128:(c + 1) * 128],
                      rr=QF[:, c * T + tb * 512: c * T + (tb + 1) * 512],
                      pf=ps_n.ap: t.matmul(pf, ll, rr, start=True, stop=True),
                      deps=[kv_toks[c], q_toks[-1], ps_n.consumed], sig="S_tensor")
            ev = P.op("vector", lambda v, o=OUTC[:, c * T + tb * 512: c * T + (tb + 1) * 512],
                      a=RB, pf=ps_n.ap: v.tensor_tensor(o, pf, a, ALU.mult),
                      deps=[t2, tb2], sig="S_vector")
            ps_n.consumed = ev
            prev_ev = ev
            num_toks.append(ev)
    rs_toks = [P.dma("sync", g["XT"][:, c * T:(c + 1) * T], xres[c * 128:(c + 1) * 128, :],
                     deps=[num_toks[-1]]) for c in range(DC)]
    GA.ptr = ga_post_outc
    WoT, wo_done = transpose_weight(P, g, W("L2_attn_Wo"), DM, DM,
                                    deps=[num_toks[-1]])
    for oc in range(DC):
        for tb in range(NT):
            ps = psum[2 + ((oc * NT + tb) % 2)]
            mms = [(WoT[kc][:, oc * 128:(oc + 1) * 128],
                    OUTC[:, kc * T + tb * 512: kc * T + (tb + 1) * 512],
                    None, wo_done) for kc in range(DC)]
            gt = mmgroup(P, ps, mms, deps=[num_toks[-1]])
            residual_evac(P, g, oc, tb * 512, 512, ps, [gt, rs_toks[oc]])


def emit_L3_attn(P, g):
    W, GA, XN, psum = g["W"], g["GA"], g["XN"], g["psum"]
    H = 16
    xdeps = emit_layernorm(P, g)
    xres = g["dram"]("l3a_xres", [DM, T])
    ydel = g["dram"]("l3a_delta", [DM, T], BF16)
    sp_toks = [P.dma("sync", xres[c * 128:(c + 1) * 128, :], g["XT"][:, c * T:(c + 1) * T],
                     deps=list(xdeps)) for c in range(DC)]
    KCOM = g["XTspace"]
    VTOK = g["XNspace"]
    kz = P.op("vector", lambda v: v.memset(KCOM, 0.0), deps=sp_toks + list(xdeps),
              sig="S_vector")
    HT = T // 2
    g["cos4_sb"] = GA.alloc(T)
    g["sin4_sb"] = GA.alloc(T)
    g["_cs_tok"] = P.dma("sync", g["cos4_sb"], g["c_cos4"])
    g["_cs_tok2"] = P.dma("sync", g["sin4_sb"], g["c_sin4"])
    QCOM = GA.alloc(H * HT)
    qz = P.op("gpsimd", lambda v: v.memset(QCOM, 0.0), deps=[], sig="S_gpsimd")
    ga0 = GA.ptr
    QC = GA.alloc(4 * T)
    ga_qc = GA.ptr
    KVN = GA.alloc(2 * T)
    ga1 = GA.ptr
    wkv, wkv_t = load_wslabs(P, g, W("L3_attn_W_kv"), DM, 256)
    gcol = g["FAcol"].alloc(2, F32)
    gct = P.dma("gpsimd", gcol, W("L3_attn_g_kv").rearrange("one (c p) -> p (one c)", p=128),
                allow_slow_non_contiguous=True)
    M, VT, SD, R = g["M"], g["VT"], g["SD"], g["R"]
    SQ = [g["SQ0"], g["SQ1"]]
    for tb in range(NT):
        sq_mm = None
        raw_toks = []
        for oc in range(2):
            ps = psum[oc]
            kvslot = KVN[:, oc * T + tb * 512: oc * T + (tb + 1) * 512]
            mms = [(wkv[kc][:, oc * 128:(oc + 1) * 128],
                    XN[:, kc * T + tb * 512: kc * T + (tb + 1) * 512],
                    None, wkv_t[kc]) for kc in range(DC)]
            gt = mmgroup(P, ps, mms, deps=list(xdeps))
            t1 = P.op("vector", lambda v, o=SQ[oc], pf=ps.ap: v.tensor_copy(o, pf),
                      deps=[gt, g.get("_kvn_prev")], sig="S_vector")
            ps.consumed = t1
            raw_toks.append(t1)
            t2 = P.op("scalar", lambda s, o=kvslot, i=SQ[oc]: s.activation(o, i, AF.Square),
                      deps=[t1], sig="S_scalar")
            sq_mm = P.op("tensor", lambda t, i=kvslot, st=(oc == 0), sp=(oc == 1):
                         t.matmul(psum[2].ap, g["ones_sb"], i, start=st, stop=sp),
                         deps=[t2] + ([psum[2].consumed] if oc == 0 else []),
                         sig="S_tensor")
        t4 = P.op("vector", lambda v: v.tensor_scalar(VT, psum[2].ap, 1.0 / 256, None, ALU.mult),
                  deps=[sq_mm], sig="S_vector")
        psum[2].consumed = t4
        t5 = P.op("scalar", lambda s: s.activation(SD, VT, AF.Sqrt, bias=g["cc"](1e-6)),
                  deps=[t4], sig="S_scalar")
        t6 = P.op("vector", lambda v: v.reciprocal(R, SD), deps=[t5], sig="S_vector")
        for oc in range(2):
            kvslot = KVN[:, oc * T + tb * 512: oc * T + (tb + 1) * 512]
            tk = P.op("vector", lambda v, o=kvslot, i=SQ[oc], gc=gcol[:, oc:oc + 1]:
                      v.scalar_tensor_tensor(o, i, gc, R, ALU.mult, ALU.mult),
                      deps=[t6, gct, sq_mm], sig="S_vector")
            g["_kvn_prev"] = tk
    kvn_done = g["_kvn_prev"]
    GA.ptr = ga1
    wq, wq_t = load_wslabs(P, g, W("L3_attn_W_q"), DM, 512, deps=[kvn_done])
    qc_done = None
    for oc in range(4):
        for tb in range(NT):
            ps = psum[3 + (oc % 2)]
            mms = [(wq[kc][:, oc * 128:(oc + 1) * 128],
                    XN[:, kc * T + tb * 512: kc * T + (tb + 1) * 512],
                    None, wq_t[kc]) for kc in range(DC)]
            gt = mmgroup(P, ps, mms, deps=list(xdeps))
            qc_done = P.op("vector", lambda v, o=QC[:, oc * T + tb * 512: oc * T + (tb + 1) * 512],
                           pf=ps.ap: v.tensor_copy(o, pf), deps=[gt], sig="S_vector")
            ps.consumed = qc_done
    GA.ptr = ga1
    KR = GA.alloc(4 * T)
    wkr, wkr_t = load_wslabs(P, g, W("L3_attn_W_krope"), DM, 512, deps=[qc_done])
    kr_done = None
    for oc in range(4):
        for tb in range(NT):
            ps = psum[3 + (oc % 2)]
            mms = [(wkr[kc][:, oc * 128:(oc + 1) * 128],
                    XN[:, kc * T + tb * 512: kc * T + (tb + 1) * 512],
                    None, wkr_t[kc]) for kc in range(DC)]
            gt = mmgroup(P, ps, mms, deps=list(xdeps))
            kr_done = P.op("vector", lambda v, o=KR[:, oc * T + tb * 512: oc * T + (tb + 1) * 512],
                           pf=ps.ap: v.tensor_copy(o, pf), deps=[gt], sig="S_vector")
            ps.consumed = kr_done
    kr_rope = emit_rope(P, g, KR, 4 * NT, None, kr_done)
    sc_done = []
    for h in range(H):
        src = KR[:, (h // 4) * T: (h // 4 + 1) * T][(h % 4) * 32:(h % 4) * 32 + 32, :]
        dst = KCOM[:, h * T:(h + 1) * T][64:96, :]
        sc_done.append(P.op("vector", lambda v, o=dst, i=src: v.tensor_copy(o, i),
                            deps=[kr_rope, kz], sig="S_vector"))
    GA.ptr = ga1
    wkd, wkd_t = load_wslabs(P, g, W("L3_attn_W_kdec"), 256, 1024,
                             deps=[sc_done[-1]])
    for oc in range(DC):
        for tb in range(NT):
            ps = psum[3 + (oc % 2)]
            mms = [(wkd[kc][:, oc * 128:(oc + 1) * 128],
                    KVN[:, kc * T + tb * 512: kc * T + (tb + 1) * 512],
                    None, wkd_t[kc]) for kc in range(2)]
            gt = mmgroup(P, ps, mms, deps=[kvn_done])
            for par in range(2):
                h = 2 * oc + par
                dst = KCOM[:, h * T:(h + 1) * T][0:64, tb * 512:(tb + 1) * 512]
                tk = P.op("vector", lambda v, o=dst, pf=ps.ap[par * 64:(par + 1) * 64, :]:
                          v.tensor_copy(o, pf), deps=[gt, kz], sig="S_vector")
            ps.consumed = tk
            sc_done.append(tk)
    wvd, wvd_t = load_wslabs(P, g, W("L3_attn_W_vdec"), 256, 1024,
                             deps=[sc_done[-1]])
    v_done = []
    for tc in range(TC):
        for fb in range(2):
            ps = psum[3 + fb]
            mms = [(KVN[:, kc * T + tc * 128: kc * T + (tc + 1) * 128],
                    wvd[kc][:, fb * 512:(fb + 1) * 512], None, wvd_t[kc])
                   for kc in range(2)]
            gt = mmgroup(P, ps, mms, deps=[kvn_done])
            tk = P.op("vector", lambda v, o=VTOK[:, tc * 1024 + fb * 512: tc * 1024 + (fb + 1) * 512],
                      pf=ps.ap: v.tensor_copy(o, pf), deps=[gt], sig="S_vector")
            ps.consumed = tk
            v_done.append(tk)
    GA.ptr = ga_qc
    QSCALE = 1.0 / np.sqrt(96.0)
    stg = GA.alloc(512)
    EXPB = GA.alloc(T)
    ATB = GA.alloc(T)
    YH = GA.alloc(DM)
    cols = GA.alloc(16, F32)
    g["l3racc"] = g["FAcol"].alloc(8, F32)
    ga2 = GA.ptr
    for half in range(2):
        hsl = slice(half * HT, (half + 1) * HT)
        GA.ptr = ga2
        wo_prev = [g.get("_wo_last"), v_done[-1], sc_done[-1]]
        wqd, wqd_t = load_wslabs(P, g, W("L3_attn_W_qdec"), 512, 1024, deps=wo_prev)
        wqr, wqr_t = load_wslabs(P, g, W("L3_attn_W_qrope"), 512, 512, deps=wo_prev)
        QR = GA.alloc(2 * T)
        qcd = []
        for oc in range(DC):
            for tb in range(2):
                ps = psum[3 + (tb % 2)]
                mms = [(wqd[kc][:, oc * 128:(oc + 1) * 128],
                        QC[:, kc * T + half * HT + tb * 512: kc * T + half * HT + (tb + 1) * 512],
                        None, wqd_t[kc]) for kc in range(4)]
                gt = mmgroup(P, ps, mms, deps=[qc_done])
                for par in range(2):
                    h = 2 * oc + par
                    dst = QCOM[:, h * HT:(h + 1) * HT][0:64, tb * 512:(tb + 1) * 512]
                    tk = P.op("vector", lambda v, o=dst, pf=ps.ap[par * 64:(par + 1) * 64, :]:
                              v.tensor_scalar(o, pf, QSCALE, None, ALU.mult),
                              deps=[gt, qz], sig="S_vector")
                ps.consumed = tk
                qcd.append(tk)
        qr_d = None
        for oc in range(4):
            for tb in range(2):
                ps = psum[3 + (tb % 2)]
                mms = [(wqr[kc][:, oc * 128:(oc + 1) * 128],
                        QC[:, kc * T + half * HT + tb * 512: kc * T + half * HT + (tb + 1) * 512],
                        None, wqr_t[kc]) for kc in range(4)]
                gt = mmgroup(P, ps, mms, deps=[qc_done])
                qr_d = P.op("vector", lambda v, o=QR[:, oc * HT + tb * 512: oc * HT + (tb + 1) * 512],
                            pf=ps.ap: v.tensor_copy(o, pf), deps=[gt], sig="S_vector")
                ps.consumed = qr_d
        qr_rope = emit_rope(P, g, QR, 4 * 2, half * HT, qr_d)
        qs_done = []
        for h in range(H):
            src = QR[:, (h // 4) * HT: (h // 4 + 1) * HT][(h % 4) * 32:(h % 4) * 32 + 32, :]
            dst = QCOM[:, h * HT:(h + 1) * HT][64:96, :]
            qs_done.append(P.op("vector", lambda v, o=dst, i=src:
                                v.tensor_scalar(o, i, QSCALE, None, ALU.mult),
                                deps=[qr_rope, qz], sig="S_vector"))
        qready = [qcd[-1], qs_done[-1]]
        GA.ptr = ga2
        wo, wo_t = load_wslabs(P, g, W("L3_attn_W_o"), DM, DM, deps=qready)
        EXPB2 = [EXPB, g["scr2k"]]
        itersA = [(tcl, h) for tcl in range(TC // 2) for h in range(H)]
        stA = {}

        def frontA(k):
            tcl, h = itersA[k]
            pp = k % 2
            tci = half * (TC // 2) + tcl
            s_end = (tci + 1) * 128
            NB = (s_end + 511) // 512
            bank_toks = []
            for bb in range(NB):
                w = min(512, s_end - bb * 512)
                ps = psum[bb % 4]
                mms = [(QCOM[:, h * HT + tcl * 128: h * HT + (tcl + 1) * 128],
                        KCOM[:, h * T + bb * 512: h * T + bb * 512 + w],
                        ps.ap[:, 0:w], None)]
                gt = mmgroup(P, ps, mms, deps=qready + [sc_done[-1], kz])
                if bb == NB - 1:
                    off = tci * 128 - bb * 512
                    gt = P.op("vector", lambda v, pf=ps.ap[:, off:off + 128]:
                              v.tensor_tensor(pf, pf, g["maskf_sb"], ALU.add),
                              deps=[gt], sig="S_vector")
                bank_toks.append((ps, gt, w))
            mx = []
            for bb, (ps, gt, w) in enumerate(bank_toks):
                tk = P.op("vector", lambda v, o=cols[:, pp * 8 + bb:pp * 8 + bb + 1],
                          pf=ps.ap[:, 0:w]:
                          v.tensor_reduce(o, pf, AX, ALU.max), deps=[gt], sig="S_vector")
                mx.append(tk)
            tm = P.op("vector", lambda v, o=cols[:, pp * 8 + 4:pp * 8 + 5],
                      i=cols[:, pp * 8:pp * 8 + NB]:
                      v.tensor_reduce(o, i, AX, ALU.max), deps=mx, sig="S_vector")
            tnm = P.op("vector", lambda v, o=cols[:, pp * 8 + 5:pp * 8 + 6],
                       i=cols[:, pp * 8 + 4:pp * 8 + 5]:
                       v.tensor_scalar(o, i, -1.0, None, ALU.mult), deps=[tm], sig="S_vector")
            stA[k] = (bank_toks, tnm)

        def middleA(k):
            tcl, h = itersA[k]
            pp = k % 2
            tci = half * (TC // 2) + tcl
            s_end = (tci + 1) * 128
            NB = (s_end + 511) // 512
            EXPB_ = EXPB2[pp]
            bank_toks, tnm = stA.pop(k)
            rs = []
            for bb, (ps, gt, w) in enumerate(bank_toks):
                tk = P.op("scalar", lambda s, o=EXPB_[:, bb * 512: bb * 512 + w],
                          pf=ps.ap[:, 0:w], b=cols[:, pp * 8 + 5:pp * 8 + 6],
                          a=g["l3racc"][:, pp * 4 + bb:pp * 4 + bb + 1]:
                          s.activation(o, pf, AF.Exp, bias=b, accum_out=a),
                          deps=[tnm, gt, g.get("_l3_eb%d" % pp)], sig="S_scalar")
                ps.consumed = tk
                rs.append(tk)
            tz2 = P.op("vector", lambda v, o=cols[:, pp * 8 + 6:pp * 8 + 7],
                       i=g["l3racc"][:, pp * 4:pp * 4 + NB]:
                       v.tensor_reduce(o, i, AX, ALU.add), deps=rs, sig="S_vector")
            trr = P.op("vector", lambda v, o=cols[:, pp * 8 + 7:pp * 8 + 8],
                       i=cols[:, pp * 8 + 6:pp * 8 + 7]:
                       v.reciprocal(o, i), deps=[tz2], sig="S_vector")
            tn = P.op("vector", lambda v, o=EXPB_[:, 0:s_end],
                      rr=cols[:, pp * 8 + 7:pp * 8 + 8]:
                      v.tensor_scalar(o, o, rr, None, ALU.mult), deps=[trr] + rs,
                      sig="S_vector")
            stA[("n", k)] = tn

        def backA(k):
            tcl, h = itersA[k]
            pp = k % 2
            tci = half * (TC // 2) + tcl
            s_end = (tci + 1) * 128
            EXPB_ = EXPB2[pp]
            tn = stA.pop(("n", k))
            ps_t = psum[4]
            ps_y = psum[5]
            nsc = (s_end + 127) // 128
            ev = None
            for scb in range(0, nsc, 4):
                nn = min(4, nsc - scb)
                tts = []
                for j in range(nn):
                    tt = P.op("tensor", lambda t, i=EXPB_[:, (scb + j) * 128:(scb + j + 1) * 128],
                              pf=ps_t.ap.bitcast(BF16)[:, j * 128:(j + 1) * 128]:
                              t.transpose(pf, i, g["eyeb_sb"]),
                              deps=[tn, ps_t.consumed if j == 0 else None], sig="S_tensor")
                    tts.append(tt)
                ev = P.op("scalar", lambda se, o=ATB[:, scb * 128: scb * 128 + nn * 128],
                          pf=ps_t.ap.bitcast(BF16)[:, 0:nn * 128]:
                          se.activation(o, pf, AF.Copy),
                          deps=[tts[-1]], sig="S_scalar")
                ps_t.consumed = ev
            g["_l3_eb%d" % pp] = ev
            mms = [(VTOK[:, sc * 1024 + h * 64: sc * 1024 + (h + 1) * 64],
                    ATB[:, sc * 128:(sc + 1) * 128],
                    ps_y.ap[0:64, 0:128], ev) for sc in range(nsc)]
            gt = mmgroup(P, ps_y, mms, deps=[ev, v_done[-1]])
            yev = P.op("vector", lambda v, o=YH[(h % 2) * 64:(h % 2) * 64 + 64,
                                                (h // 2) * 128:(h // 2) * 128 + 128],
                       pf=ps_y.ap[0:64, 0:128]: v.tensor_copy(o, pf),
                       deps=[gt, g.get("_l3_yh")], sig="S_vector")
            ps_y.consumed = yev
            g["_yh%d" % h] = yev
            if h == H - 1:
                wo_emit(tci, YH)

        def wo_emit(tci, YH):
            ydeps = [g["_yh%d" % hh] for hh in range(H)]
            for oc in range(DC):
                ps = psum[6]
                mms2 = [(wo[kc][:, oc * 128:(oc + 1) * 128],
                         YH[:, kc * 128:(kc + 1) * 128], None, wo_t[kc]) for kc in range(DC)]
                gt2 = mmgroup(P, ps, mms2, deps=ydeps, sub=ps.ap[:, 0:128])
                ev2 = P.op("vector", lambda v, o=stg[:, 0:128], pf=ps.ap[:, 0:128]:
                           v.tensor_copy(o, pf), deps=[gt2, g.get("_stg_guard")],
                           sig="S_vector")
                dk = P.dma("sync", ydel[oc * 128:(oc + 1) * 128, tci * 128:(tci + 1) * 128],
                           stg[:, 0:128], deps=[ev2])
                ps.consumed = dk
                g["_stg_guard"] = dk
                g["_wo_last"] = gt2
                g["_l3_yh"] = gt2

        NA = len(itersA)
        for k in range(NA + 1):
            if k < NA:
                frontA(k)
            if k >= 1:
                backA(k - 1)
            if k < NA:
                middleA(k)
    last_h = [g["_yh%d" % h] for h in range(H)]
    rs_toks = [P.dma("sync", g["XT"][:, c * T:(c + 1) * T], xres[c * 128:(c + 1) * 128, :],
                     deps=last_h + [g["_stg_guard"]]) for c in range(DC)]
    GA.ptr = ga2
    DB = GA.alloc(2 * T)
    db_ld = [None, None]
    for c in range(DC):
        buf = DB[:, (c % 2) * T:(c % 2 + 1) * T]
        lt = P.dma("sync", buf, ydel[c * 128:(c + 1) * 128, :],
                   deps=[g["_stg_guard"], db_ld[c % 2]])
        tk = P.op("vector", lambda v, o=g["XT"][:, c * T:(c + 1) * T], i=buf:
                  v.tensor_tensor(o, o, i, ALU.add), deps=[lt, rs_toks[c]], sig="S_vector")
        db_ld[c % 2] = tk


def _bfly_coeff(P, g, wap, stage, kind, dst, deps=()):
    s = 2 ** stage
    toks = []
    pitch = dst.tensor.shape[1]
    doff = dst.offset
    if stage <= 6:
        q_cnt = 128 // (2 * s)
        base = stage * 8192
        if kind == "diag":
            bdim, boff = [3, 2], 0
        elif kind == "part":
            bdim, boff = [1, 2], 1
        else:
            bdim, boff = [-1, 2], 2
        dims = ([[4 * s, q_cnt]] if q_cnt > 1 else []) + [bdim] + \
            ([[4, s]] if s > 1 else []) + [[256, 32]]
        odims = ([[pitch * 2 * s, q_cnt]] if q_cnt > 1 else []) + \
            [[pitch * s, 2]] + ([[pitch, s]] if s > 1 else []) + [[1, 32]]
        _peel_dma(P, toks, wap.tensor, base + boff, dims,
                  dst.tensor, doff, odims, deps)
    else:
        sb = stage - 7
        lo_cnt = 2 ** sb
        hi_cnt = 32 // (2 * lo_cnt)
        base = stage * 8192
        boff, bstep = {"diag": (0, 3), "part": (1, 1), "partT": (2, -1)}[kind]
        dims = ([[s * 4, hi_cnt]] if hi_cnt > 1 else []) + [[bstep, 2]] + \
            ([[512, lo_cnt]] if lo_cnt > 1 else [])
        odims = ([[2 * lo_cnt, hi_cnt]] if hi_cnt > 1 else []) + [[lo_cnt, 2]] + \
            ([[1, lo_cnt]] if lo_cnt > 1 else [])
        _peel_dma(P, toks, wap.tensor, base + boff, dims,
                  dst.tensor, doff, odims, deps, lead=([4, 128], [pitch, 128]))
    return toks


def _peel_dma(P, toks, it, ioff, idims, ot, ooff, odims, deps, lead=None):
    maxd = 1 if lead else 2
    if len(idims) <= maxd:
        ii = ([list(lead[0])] if lead else []) + [list(d) for d in idims]
        oo = ([list(lead[1])] if lead else []) + [list(d) for d in odims]
        eng = ("gpsimd", "sync", "scalar")[len(toks) % 3]
        toks.append(P.dma(eng, bass.AP(ot, ooff, oo), bass.AP(it, ioff, ii),
                          deps=deps, allow_slow_non_contiguous=True))
        return
    (istep, cnt), (ostep, cnt2) = idims[0], odims[0]
    assert cnt == cnt2
    for k in range(cnt):
        _peel_dma(P, toks, it, ioff + k * istep, idims[1:],
                  ot, ooff + k * ostep, odims[1:], deps, lead)


def emit_L2_ffn(P, g):
    W, GA, XN, psum = g["W"], g["GA"], g["XN"], g["psum"]
    xdeps = emit_layernorm(P, g)
    xres = g["dram"]("l2f_xres", [DM, T])
    wexp2 = g["dram"]("l2f_wexp2", [DM, 4096], BF16)
    wcon2 = g["dram"]("l2f_wcon2", [4096, DM], BF16)
    sp_toks = [P.dma("sync", xres[c * 128:(c + 1) * 128, :], g["XT"][:, c * T:(c + 1) * T],
                     deps=list(xdeps)) for c in range(DC)]
    upw, dww = W("L2_ffn_up_weights"), W("L2_ffn_down_weights")
    CO = {}
    co_t = {}
    for wap, tag in ((upw, "u"), (dww, "d")):
        for st in range(12):
            if st < 6:
                kinds = ("diag", "part")
            elif st == 6:
                kinds = ("diag", "part", "partT")
            elif tag == "d":
                kinds = ("diag", "partT")
            else:
                kinds = ("diag", "part")
            for kind in kinds:
                buf = GA.alloc(32, F32)
                co_t[(tag, st, kind)] = _bfly_coeff(P, g, wap, st, kind, buf)
                CO[(tag, st, kind)] = buf
    LT = {"u": GA.alloc(32 * 128), "d": GA.alloc(32 * 128)}
    becol = GA.alloc(32, F32)
    B1 = GA.alloc(32, F32)
    GB = GA.alloc(32, F32)
    bcol = GA.alloc(8, F32)
    bcol2 = GA.alloc(8, F32)
    ga_fw = GA.ptr
    St2 = [GA.alloc(128), GA.alloc(128)]
    Rt_ps2 = [psum[5], psum[6]]
    chain_done = {}
    prev_ev = None
    ccnt = 0
    for tag in ("u", "d"):
        for c in range(32):
            St = St2[ccnt % 2]
            Rt_ps = Rt_ps2[ccnt % 2]
            ccnt += 1
            rt = LT[tag][:, c * 128:(c + 1) * 128]
            t1 = P.op("scalar", lambda se, o=rt, a=CO[(tag, 6, "diag")][:, c:c + 1]:
                      se.activation(o, g["eyeb_sb"], AF.Copy, scale=a),
                      deps=co_t[(tag, 6, "diag")] + [prev_ev], sig="S_scalar")
            t2 = P.op("vector", lambda v, o=rt, pm=g["pmask_sb"][:, 6 * 128:7 * 128],
                      a=CO[(tag, 6, "partT")][:, c:c + 1]:
                      v.scalar_tensor_tensor(o, pm, a, o, ALU.mult, ALU.add),
                      deps=[t1] + co_t[(tag, 6, "partT")], sig="S_vector")
            prev = t2
            for k in range(5, -1, -1):
                t3 = P.op("scalar", lambda se, a=CO[(tag, k, "diag")][:, c:c + 1]:
                          se.activation(St, g["eyeb_sb"], AF.Copy, scale=a),
                          deps=co_t[(tag, k, "diag")] + [g.get("_l2_st%d" % (ccnt % 2))], sig="S_scalar")
                t4 = P.op("vector", lambda v, pm=g["pmask_sb"][:, k * 128:(k + 1) * 128],
                          a=CO[(tag, k, "part")][:, c:c + 1]:
                          v.scalar_tensor_tensor(St, pm, a, St, ALU.mult, ALU.add),
                          deps=[t3] + co_t[(tag, k, "part")], sig="S_vector")
                t5 = P.op("tensor", lambda t, r=rt, pf=Rt_ps.ap[:, 0:128]:
                          t.matmul(pf, St, r, start=True, stop=True),
                          deps=[t4, prev, Rt_ps.consumed], sig="S_tensor")
                g["_l2_st%d" % (ccnt % 2)] = t5
                prev = P.op("vector", lambda v, o=rt, pf=Rt_ps.ap[:, 0:128]:
                            v.tensor_copy(o, pf), deps=[t5], sig="S_vector")
                Rt_ps.consumed = prev
            chain_done[(tag, c)] = prev
            prev_ev = prev
    W1T = g["XTspace"]
    wstage2 = [GA.alloc(4096, F32), GA.alloc(4096, F32)]
    TMPW = GA.alloc(2048)
    TMP2W = GA.alloc(2048)
    tile2 = [GA.alloc(128), GA.alloc(128)]
    last_mm = None
    mm_tok = {}
    st_tok = [None, None]
    for kc in range(DC):
        wstage = wstage2[kc % 2]
        st_tok[kc % 2] = P.dma("sync", wstage, W("L2_ffn_W_exp")[kc * 128:(kc + 1) * 128, :],
                               deps=sp_toks + [st_tok[kc % 2], mm_tok.get(kc - 2)])
        for c in range(32):
            cp = c % 2
            tile = tile2[cp]
            ps_t = psum[0 + cp]
            tt = P.op("tensor", lambda t, i=wstage[:, c * 128:(c + 1) * 128],
                      pf=ps_t.ap[:, 0:128]: t.transpose(pf, i, g["eye_sb"]),
                      deps=[st_tok[kc % 2], ps_t.consumed], sig="S_tensor")
            te = P.op("vector", lambda v, o=tile, pf=ps_t.ap[:, 0:128]: v.tensor_copy(o, pf),
                      deps=[tt, g.get("_l2_tile%d" % cp)], sig="S_vector")
            ps_t.consumed = te
            ps_w = psum[2 + cp]
            tm = P.op("tensor", lambda t, l=LT["u"][:, c * 128:(c + 1) * 128], r=tile,
                      pf=ps_w.ap[:, 0:128]:
                      t.matmul(pf, l, r, start=True, stop=True),
                      deps=[te, chain_done[("u", c)], ps_w.consumed], sig="S_tensor")
            g["_l2_tile%d" % cp] = tm
            last_mm = P.op("vector", lambda v, o=W1T[:, c * 1024 + kc * 128: c * 1024 + (kc + 1) * 128],
                           pf=ps_w.ap[:, 0:128]: v.tensor_copy(o, pf),
                           deps=[tm], sig="S_vector")
            ps_w.consumed = last_mm
        mm_tok[kc] = last_mm
    def pair_fold(tag, stages, buf, width, deps0):
        last = deps0
        for stg in stages:
            m = 2 ** (stg - 7)
            done = []
            pi = 0
            for B in range(32):
                Pp = B ^ m
                if Pp < B:
                    continue
                pi += 1
                TA = TMPW if pi % 2 == 0 else TMP2W
                kpart = "part" if tag == "u" else "partT"
                da = CO[(tag, stg, "diag")][:, B:B + 1]
                oa = CO[(tag, stg, kpart)][:, B:B + 1]
                db = CO[(tag, stg, "diag")][:, Pp:Pp + 1]
                ob = CO[(tag, stg, kpart)][:, Pp:Pp + 1]
                a = buf[:, B * width:(B + 1) * width]
                b = buf[:, Pp * width:(Pp + 1) * width]
                d0 = list(last)
                x1 = P.op("scalar", lambda se, o=TA[:, 0:width], i=b, s=oa:
                          se.activation(o, i, AF.Copy, scale=s),
                          deps=d0 + co_t[(tag, stg, kpart)] + [g.get("_l2_pf%d" % (pi % 2))],
                          sig="S_scalar")
                x2 = P.op("vector", lambda v, o=TA[:, 0:width], i=a, s=da:
                          v.scalar_tensor_tensor(o, i, s, o, ALU.mult, ALU.add),
                          deps=[x1] + co_t[(tag, stg, "diag")], sig="S_vector")
                x3 = P.op("scalar", lambda se, o=TA[:, width:2 * width], i=a, s=ob:
                          se.activation(o, i, AF.Copy, scale=s),
                          deps=d0 + [g.get("_l2_pf%d" % (pi % 2)), x1], sig="S_scalar")
                x4 = P.op("vector", lambda v, o=b, i=b, s=db, t2=TA[:, width:2 * width]:
                          v.scalar_tensor_tensor(o, i, s, t2, ALU.mult, ALU.add),
                          deps=[x3, x2], sig="S_vector")
                x5 = P.op("vector", lambda v, o=a, i=TA[:, 0:width]:
                          v.tensor_copy(o, i), deps=[x2, x4], sig="S_vector")
                g["_l2_pf%d" % (pi % 2)] = x5
                done.append(x5)
                done.append(x4)
            last = done
        return last
    ufold_done = pair_fold("u", range(7, 12), W1T, 1024, [last_mm])
    wx_toks = []
    wslab = wstage2[0].bitcast(BF16)
    for kc in range(DC):
        evs = []
        for c in range(32):
            ps_t = psum[0]
            tt = P.op("tensor", lambda t, i=W1T[:, c * 1024 + kc * 128: c * 1024 + (kc + 1) * 128],
                      pf=ps_t.ap.bitcast(BF16)[:, 0:128]: t.transpose(pf, i, g["eyeb_sb"]),
                      deps=ufold_done + [ps_t.consumed] +
                      ([wx_toks[-1]] if wx_toks else []), sig="S_tensor")
            ev = P.op("vector", lambda v, o=wslab[:, c * 128:(c + 1) * 128],
                      pf=ps_t.ap.bitcast(BF16)[:, 0:128]: v.tensor_copy(o, pf),
                      deps=[tt], sig="S_vector")
            ps_t.consumed = ev
            evs.append(ev)
        wx_toks.append(P.dma("sync", wexp2[kc * 128:(kc + 1) * 128, :], wslab[:, 0:4096],
                             deps=evs))
    wx_done = wx_toks[-1]
    wc_l = []
    for fc in range(32):
        wc_l.append(P.dma("gpsimd", W1T[:, fc * 1024:(fc + 1) * 1024],
                          W("L2_ffn_W_con")[fc * 128:(fc + 1) * 128, :],
                          deps=[wx_done]))
    dfold_done = pair_fold("d", range(11, 6, -1), W1T, 1024, wc_l)
    wc_toks = []
    for fc in range(32):
        wc_toks.append(P.dma("sync", wcon2[fc * 128:(fc + 1) * 128, :],
                             W1T[:, fc * 1024:(fc + 1) * 1024],
                             deps=dfold_done + (wc_toks[-1:] if wc_toks else [])))
    wc_done = wc_toks[-1]
    t_be = P.dma("gpsimd", becol, W("L2_ffn_b_exp").rearrange("one (c p) -> p (one c)", p=128),
                 allow_slow_non_contiguous=True)
    for c in range(32):
        ps_b = psum[2]
        cb = P.op("vector", lambda v, o=St[:, 0:1], i=becol[:, c:c + 1]:
                  v.tensor_copy(o, i), deps=[t_be, g.get("_l2_bst")], sig="S_vector")
        tb1 = P.op("tensor", lambda t, l=LT["u"][:, c * 128:(c + 1) * 128],
                   pf=ps_b.ap[:, 0:1]:
                   t.matmul(pf, l, St[:, 0:1], start=True, stop=True),
                   deps=[cb, chain_done[("u", c)], ps_b.consumed], sig="S_tensor")
        tb2 = P.op("vector", lambda v, o=B1[:, c:c + 1], pf=ps_b.ap[:, 0:1]:
                   v.tensor_copy(o, pf), deps=[tb1], sig="S_vector")
        ps_b.consumed = tb2
        g["_l2_bst"] = tb2
    bfold = pair_fold("u", range(7, 12), B1, 1, [g["_l2_bst"]])
    t_ub = P.dma("gpsimd", GB, W("L2_ffn_up_bias").rearrange("one (c p) -> p (one c)", p=128),
                 allow_slow_non_contiguous=True)
    t_gb = P.op("vector", lambda v: v.tensor_tensor(GB, GB, B1, ALU.add),
                deps=bfold + [t_ub], sig="S_vector")
    t_b1 = P.dma("gpsimd", bcol, W("L2_ffn_b_con").rearrange("one (c p) -> p (one c)", p=128),
                 allow_slow_non_contiguous=True)
    t_b2 = P.dma("gpsimd", bcol2, W("L2_ffn_down_bias").rearrange("one (c p) -> p (one c)", p=128),
                 allow_slow_non_contiguous=True)
    t_bc = P.op("vector", lambda v: v.tensor_tensor(bcol, bcol, bcol2, ALU.add),
                deps=[t_b1, t_b2], sig="S_vector")
    GA.ptr = ga_fw
    wes = [W1T[:, kc * 4096:(kc + 1) * 4096] for kc in range(DC)]
    we_t = []
    for kc in range(DC):
        we_t.append(P.dma("gpsimd", wes[kc], wexp2[kc * 128:(kc + 1) * 128, :],
                          deps=[wc_done, wx_toks[kc], t_gb, g.get("_l2_bst")]))
    H = GA.alloc(32 * 512)
    wcs = [GA.alloc(1024), GA.alloc(1024)]
    stgd = GA.alloc(512)
    ydel = g["dram"]("l2f_delta", [DM, T], BF16)
    for tb in range(NT):
        h_toks = {}
        for cp in range(32):
            ps = psum[cp % 4]
            mms = [(wes[kc][:, cp * 128:(cp + 1) * 128],
                    XN[:, kc * T + tb * 512: kc * T + (tb + 1) * 512],
                    None, we_t[kc]) for kc in range(DC)]
            gt = mmgroup(P, ps, mms, deps=list(xdeps))
            ev = P.op("scalar", lambda s, o=H[:, cp * 512:(cp + 1) * 512], pf=ps.ap,
                      b=GB[:, cp:cp + 1]: s.activation(o, pf, AF.Gelu, bias=b),
                      deps=[gt, t_gb, g.get("_l2_h")], sig="S_scalar")
            ps.consumed = ev
            h_toks[cp] = ev
        h2_toks = {}
        for cp in range(32):
            ps = psum[4 + cp % 2]
            gt = P.op("tensor", lambda t, l=LT["d"][:, cp * 128:(cp + 1) * 128],
                      r=H[:, cp * 512:(cp + 1) * 512], pf=ps.ap:
                      t.matmul(pf, l, r, start=True, stop=True),
                      deps=[h_toks[cp], chain_done[("d", cp)], ps.consumed], sig="S_tensor")
            ev = P.op("vector", lambda v, o=H[:, cp * 512:(cp + 1) * 512], pf=ps.ap:
                      v.tensor_copy(o, pf), deps=[gt], sig="S_vector")
            ps.consumed = ev
            h2_toks[cp] = ev
        g["_l2_h"] = h2_toks[31]
        mm_last = {}
        wload = {}
        for kc in range(32):
            wload[kc] = P.dma("gpsimd", wcs[kc % 2], wcon2[kc * 128:(kc + 1) * 128, :],
                              deps=[wc_toks[kc], mm_last.get(kc - 2), t_gb,
                                    g.get("_l2_bst"), g.get("_l2_wc_prev")])
            tk = None
            for oc in range(DC):
                ps = psum[oc]
                tk = P.op("tensor", lambda t, l=wcs[kc % 2][:, oc * 128:(oc + 1) * 128],
                          r=H[:, kc * 512:(kc + 1) * 512], st=(kc == 0), sp=(kc == 31),
                          pf=ps.ap: t.matmul(pf, l, r, start=st, stop=sp),
                          deps=[wload[kc], h2_toks[kc]] +
                          ([ps.consumed] if kc == 0 else []),
                          sig="S_tensor" if oc == DC - 1 else None)
            mm_last[kc] = tk
        g["_l2_wc_prev"] = mm_last[31]
        for oc in range(DC):
            ps = psum[oc]
            ev = P.op("vector", lambda v, o=stgd, pf=ps.ap, b=bcol[:, oc:oc + 1]:
                      v.tensor_scalar(o, pf, b, None, ALU.add),
                      deps=[mm_last[31], t_bc, g.get("_l2_stg")], sig="S_vector")
            dk = P.dma("sync", ydel[oc * 128:(oc + 1) * 128, tb * 512:(tb + 1) * 512],
                       stgd, deps=[ev])
            ps.consumed = dk
            g["_l2_stg"] = dk
    rs_toks = [P.dma("sync", g["XT"][:, c * T:(c + 1) * T], xres[c * 128:(c + 1) * 128, :],
                     deps=[g["_l2_stg"], g["_l2_wc_prev"]]) for c in range(DC)]
    DB = H[:, 0:2 * T]
    db_ld = [None, None]
    for c in range(DC):
        buf = DB[:, (c % 2) * T:(c % 2 + 1) * T]
        lt = P.dma("sync", buf, ydel[c * 128:(c + 1) * 128, :],
                   deps=[g["_l2_stg"], db_ld[c % 2]])
        tk = P.op("vector", lambda v, o=g["XT"][:, c * T:(c + 1) * T], i=buf:
                  v.tensor_tensor(o, o, i, ALU.add), deps=[lt, rs_toks[c]], sig="S_vector")
        db_ld[c % 2] = tk


def emit_L0_attn(P, g):
    W, GA, XN, psum = g["W"], g["GA"], g["XN"], g["psum"]
    H, NST = 8, 64
    xdeps = emit_layernorm(P, g)
    xres = g["dram"]("l0a_xres", [DM, T])
    zdram = g["dram"]("l0a_z", [2048, T], BF16)
    w8d = g["dram"]("l0a_w8", [8 * T])
    u8d = g["dram"]("l0a_u8", [8 * T])
    cumd = g["dram"]("l0a_cum", [8 * T])
    eacd = g["dram"]("l0a_eac", [128])
    sp_toks = [P.dma("sync", xres[c * 128:(c + 1) * 128, :], g["XT"][:, c * T:(c + 1) * T],
                     deps=list(xdeps)) for c in range(DC)]
    XT8 = g["XTspace"][:, 0:4 * T]
    Xtok = g["XTspace"][:, 4 * T:12 * T]
    YT = g["XTspace"][:, 12 * T:16 * T]
    fa = g["FAcol"]
    ga_start = GA.ptr
    wz, wz_t = load_wslabs(P, g, W("L0_attn_Wz"), DM, 2048)
    zstg = GA.alloc(T)
    z_w = []
    zprev = None
    for zc in range(16):
        evs = []
        for tb in range(NT):
            ps = psum[tb % 2]
            mms = [(wz[kc][:, zc * 128:(zc + 1) * 128],
                    XN[:, kc * T + tb * 512: kc * T + (tb + 1) * 512],
                    None, wz_t[kc]) for kc in range(DC)]
            gt = mmgroup(P, ps, mms, deps=list(xdeps))
            ev = P.op("scalar", lambda s, o=zstg[:, tb * 512:(tb + 1) * 512], pf=ps.ap:
                      s.activation(o, pf, AF.Silu), deps=[gt, zprev], sig="S_scalar")
            ps.consumed = ev
            evs.append(ev)
        zprev = P.dma("sync", zdram[zc * 128:(zc + 1) * 128, :], zstg, deps=evs)
        z_w.append(zprev)
    GA.ptr = ga_start
    Bc = GA.alloc(T)
    Cc = GA.alloc(T)
    cv1 = GA.alloc(T, F32)
    cv2 = GA.alloc(T, F32)
    Btok = GA.alloc(TC * 64)
    S = GA.alloc(512, F32)
    SB16 = GA.alloc(512)
    ACB = GA.alloc(128, F32)
    EXPACB = GA.alloc(128, F32)
    DCB = GA.alloc(8, F32)
    WLT = GA.alloc(8, F32)
    DLT = GA.alloc(8, F32)
    GEXP = GA.alloc(128)
    CBMD = GA.alloc(128)
    W2S = GA.alloc(1, F32)
    W8R = GA.alloc(1024, F32)
    U8R = GA.alloc(1024, F32)
    CUMR = GA.alloc(1024, F32)
    UBC = GA.alloc(128)
    CUT = GA.alloc(128)
    GH = GA.alloc(128)
    CBM = GA.alloc(128)
    ONE8 = GA.alloc(128, F32)
    BWB = [GA.alloc(64), GA.alloc(64)]
    zdone = [zprev, evs[-1]]
    t_one8 = P.op("vector", lambda v: v.memset(ONE8[0:8, :], 1.0), deps=zdone, sig="S_vector")
    DT8 = cv1[0:8, :]
    CUM = cv1[64:72, :]
    TMP8 = cv2[0:8, :]
    U8 = cv2[32:40, :]
    W8 = cv2[64:72, :]
    win, win_t = load_wslabs(P, g, W("L0_attn_Win"), DM, 648, deps=zdone)
    def fmproj(dst_ap, col0, ww, psl, deps0=()):
        done = None
        for tb in range(NT):
            ps = psum[psl + (tb % 2)]
            mms = [(win[kc][:, col0:col0 + ww],
                    XN[:, kc * T + tb * 512: kc * T + (tb + 1) * 512],
                    ps.ap[0:ww, 0:512], win_t[kc]) for kc in range(DC)]
            gt = mmgroup(P, ps, mms, deps=list(deps0))
            done = P.op("vector", lambda v, o=dst_ap[0:ww, tb * 512:(tb + 1) * 512],
                        pf=ps.ap[0:ww, 0:512]: v.tensor_copy(o, pf),
                        deps=[gt], sig="S_vector")
            ps.consumed = done
        return done
    b_done = fmproj(Bc, 0, 64, 0)
    c_done = fmproj(Cc, 64, 64, 2)
    dt_done = fmproj(DT8, 128, 8, 0)
    x8_done = None
    for xc in range(4):
        x8_done = fmproj(XT8[:, xc * T:(xc + 1) * T], 136 + xc * 128, 128, 2 * (xc % 2),
                         deps0=sp_toks)
    xt_done = None
    for tc in range(TC):
        ps = psum[4]
        mms = [(XN[:, kc * T + tc * 128: kc * T + (tc + 1) * 128],
                win[kc][:, 136:648], None, win_t[kc]) for kc in range(DC)]
        gt = mmgroup(P, ps, mms, deps=sp_toks)
        xt_done = P.op("vector", lambda v, o=Xtok[:, tc * 512:(tc + 1) * 512], pf=ps.ap:
                       v.tensor_copy(o, pf), deps=[gt], sig="S_vector")
        ps.consumed = xt_done
        ps = psum[5]
        mms = [(XN[:, kc * T + tc * 128: kc * T + (tc + 1) * 128],
                win[kc][:, 0:64], ps.ap[:, 0:64], win_t[kc]) for kc in range(DC)]
        gt = mmgroup(P, ps, mms, deps=[])
        bt_done = P.op("vector", lambda v, o=Btok[:, tc * 64:(tc + 1) * 64],
                       pf=ps.ap[:, 0:64]: v.tensor_copy(o, pf), deps=[gt], sig="S_vector")
        ps.consumed = bt_done
    dtb = fa.alloc(1, F32)
    t_dtb = P.dma("gpsimd", dtb[0:8, :], W("L0_attn_dt_bias").rearrange("one e -> e one"),
                  allow_slow_non_contiguous=True)
    alog = fa.alloc(1, F32)
    t_al = P.dma("gpsimd", alog[0:8, :], W("L0_attn_A_log").rearrange("one e -> e one"),
                 allow_slow_non_contiguous=True)
    nA = fa.alloc(1, F32)
    t_nA1 = P.op("scalar", lambda s: s.activation(nA[0:8, :], alog[0:8, :], AF.Exp),
                 deps=[t_al], sig="S_scalar")
    t_nA = P.op("vector", lambda v: v.tensor_scalar(nA[0:8, :], nA[0:8, :], -1.0, None, ALU.mult),
                deps=[t_nA1], sig="S_vector")
    s1 = P.op("vector", lambda v: v.tensor_scalar(DT8, DT8, dtb[0:8, :], None, ALU.add),
              deps=[dt_done, t_dtb], sig="S_vector")
    s2 = P.op("scalar", lambda s: s.activation(TMP8, DT8, AF.Abs), deps=[s1], sig="S_scalar")
    s3 = P.op("scalar", lambda s: s.activation(TMP8, TMP8, AF.Exp, scale=-1.0),
              deps=[s2], sig="S_scalar")
    s4 = P.op("scalar", lambda s: s.activation(TMP8, TMP8, AF.Ln, bias=g["cc"](1.0)[64:72, :]),
              deps=[s3], sig="S_scalar")
    s5 = P.op("vector", lambda v: v.tensor_scalar(DT8, DT8, 0.0, None, ALU.max),
              deps=[s4], sig="S_vector")
    s6 = P.op("vector", lambda v: v.tensor_tensor(DT8, DT8, TMP8, ALU.add),
              deps=[s5], sig="S_vector")
    s7 = P.op("vector", lambda v: v.tensor_scalar(TMP8, DT8, nA[0:8, :], None, ALU.mult),
              deps=[s6, t_nA], sig="S_vector")
    sc_t = []
    for c in range(TC):
        tk = P.op("vector", lambda v, o=CUM[:, c * 128:(c + 1) * 128],
                  d1=TMP8[:, c * 128:(c + 1) * 128]:
                  v.tensor_tensor_scan(o, ONE8[0:8, :], d1, 0.0, ALU.mult, ALU.add),
                  deps=[s7, t_one8], sig="S_vector")
        sc_t.append(tk)
    s9 = P.op("vector", lambda v: v.tensor_scalar(W8, CUM, -1.0, None, ALU.mult),
              deps=sc_t, sig="S_vector")
    s10 = P.op("scalar", lambda s: s.activation(U8, CUM, AF.Exp), deps=sc_t, sig="S_scalar")
    EACs = fa.alloc(16, F32)
    s11 = P.op("vector", lambda s: s.tensor_copy(EACs[0:8, 0:16], CUM[:, 127:2048:128]),
               deps=sc_t, sig="S_vector")
    d1 = P.dma("sync", w8d.rearrange("(c h s) -> h c s", h=8, c=16), W8, deps=[s9])
    d2 = P.dma("sync", u8d.rearrange("(c h s) -> h c s", h=8, c=16), U8, deps=[s10])
    d2b = P.dma("sync", cumd.rearrange("(c h s) -> h c s", h=8, c=16), CUM, deps=sc_t)
    d3 = P.dma("sync", eacd.rearrange("(h c) -> h c", h=8), EACs[0:8, 0:16], deps=[s11])
    EACR = fa.alloc(128, F32)
    d4 = P.dma("sync", EACR[0:1, 0:128], eacd.rearrange("(one x) -> one x", one=1), deps=[d3])
    pb = psum[4]
    t_e1 = P.op("tensor", lambda t: t.matmul(pb.ap[:, 0:128], g["ones1f_sb"], EACR[0:1, 0:128],
                                             start=True, stop=True),
                deps=[d4, pb.consumed], sig="S_tensor")
    t_ac = P.op("vector", lambda v: v.tensor_copy(ACB, pb.ap[:, 0:128]),
                deps=[t_e1], sig="S_vector")
    t_e2 = P.op("scalar", lambda s: s.activation(EXPACB, ACB, AF.Exp),
                deps=[t_ac], sig="S_scalar")
    pb.consumed = t_e2
    DR = fa.alloc(8, F32)
    d5 = P.dma("sync", DR[0:1, 0:8], W("L0_attn_D_param"), deps=[])
    t_d1 = P.op("tensor", lambda t: t.matmul(pb.ap[:, 0:8], g["ones1f_sb"], DR[0:1, 0:8],
                                             start=True, stop=True),
                deps=[d5, pb.consumed], sig="S_tensor")
    t_d2 = P.op("vector", lambda v: v.tensor_copy(DCB, pb.ap[:, 0:8]),
                deps=[t_d1], sig="S_vector")
    pb.consumed = t_d2
    t_s0 = P.op("vector", lambda v: v.memset(S, 0.0), sig="S_vector")
    wu_prev = None
    su_last = t_s0
    sb_tok = None
    for c in range(TC):
        csl = slice(c * 128, (c + 1) * 128)
        r1 = P.dma("sync", W8R[0:1, 0:1024],
                   w8d[c * 1024:(c + 1) * 1024].rearrange("(one x) -> one x", one=1),
                   deps=[d1, wu_prev])
        r2 = P.dma("sync", U8R[0:1, 0:1024],
                   u8d[c * 1024:(c + 1) * 1024].rearrange("(one x) -> one x", one=1),
                   deps=[d2, wu_prev])
        r3 = P.dma("sync", CUMR[0:1, 0:1024],
                   cumd[c * 1024:(c + 1) * 1024].rearrange("(one x) -> one x", one=1),
                   deps=[d2b, wu_prev])
        g["_l0_cumr"] = r3
        g["cumr_slice"] = lambda cc, hh: CUMR[0:1, hh * 128:(hh + 1) * 128]
        ps5 = psum[5]
        tw = P.op("tensor", lambda t, i=CUM[:, csl], pf=ps5.ap[:, 0:8]:
                  t.transpose(pf, i, g["eye_sb"][64:72, 64:72]),
                  deps=[sc_t[c], ps5.consumed], sig="S_tensor")
        twe = P.op("vector", lambda v, pf=ps5.ap[:, 0:8]: v.tensor_copy(WLT, pf),
                   deps=[tw, g.get("_l0_wlt")], sig="S_vector")
        ps5.consumed = twe
        td = P.op("tensor", lambda t, i=DT8[:, csl], pf=ps5.ap[:, 8:16]:
                  t.transpose(pf, i, g["eye_sb"][0:8, 0:8]),
                  deps=[s6, twe], sig="S_tensor")
        tde = P.op("vector", lambda v, pf=ps5.ap[:, 8:16]: v.tensor_copy(DLT, pf),
                   deps=[td, g.get("_l0_dlt")], sig="S_vector")
        ps5.consumed = tde
        sb_tok = P.op("vector", lambda v: v.tensor_copy(SB16[0:64, :], S[0:64, :]),
                      deps=[su_last, g.get("_l0_sb")], sig="S_vector")
        ps_cb = psum[6]
        tcb = P.op("tensor", lambda t, l=Bc[0:64, csl], r=Cc[0:64, csl], pf=ps_cb.ap[:, 0:128]:
                   t.matmul(pf, l, r, start=True, stop=True),
                   deps=[b_done, c_done, ps_cb.consumed], sig="S_tensor")
        tcbm = P.op("vector", lambda v, pf=ps_cb.ap[:, 0:128]:
                    v.tensor_tensor(CBM, pf, g["triub_sb"], ALU.mult),
                    deps=[tcb, g.get("_l0_cbm")], sig="S_vector")
        ps_cb.consumed = tcbm
        for h in range(H):
            ps_g = psum[7]
            tg0 = P.op("tensor", lambda t, l=W8R[0:1, h * 128:(h + 1) * 128],
                       pf=ps_g.ap[:, 0:128]:
                       t.matmul(pf, l, g["ones1f_sb"], start=True, stop=False),
                       deps=[r1, ps_g.consumed], sig=None)
            tg = P.op("tensor", lambda t, r=g["cumr_slice"](c, h),
                      pf=ps_g.ap[:, 0:128]:
                      t.matmul(pf, g["ones1f_sb"], r, start=False, stop=True),
                      deps=[g["_l0_cumr"]], sig="S_tensor")
            tmsk = P.op("vector", lambda v, pf=ps_g.ap[:, 0:128]:
                        v.tensor_tensor(pf, pf, g["maskfl_sb"], ALU.add),
                        deps=[tg], sig="S_vector")
            tex = P.op("scalar", lambda s, pf=ps_g.ap[:, 0:128]:
                       s.activation(GEXP, pf, AF.Exp),
                       deps=[tmsk, g.get("_l0_gh")], sig="S_scalar")
            ps_g.consumed = tex
            tcbd = P.op("vector", lambda v, dcol=DLT[:, h:h + 1]:
                        v.tensor_scalar(CBMD, CBM, dcol, None, ALU.mult),
                        deps=[tcbm, tde, g.get("_l0_cbmd")], sig="S_vector")
            tgh = P.op("vector", lambda v:
                       v.tensor_tensor(GH, CBMD, GEXP, ALU.mult),
                       deps=[tex, tcbd, g.get("_l0_gh")], sig="S_vector")
            ps_u = psum[4]
            tu = P.op("tensor", lambda t, r=U8R[0:1, h * 128:(h + 1) * 128],
                      pf=ps_u.ap[0:64, 0:128]:
                      t.matmul(pf, g["ones1f_sb"][0:1, 0:64], r, start=True, stop=True),
                      deps=[r2, ps_u.consumed], sig="S_tensor")
            tue = P.op("vector", lambda v, pf=ps_u.ap[0:64, 0:128]:
                       v.tensor_copy(UBC[0:64, :], pf), deps=[tu, g.get("_l0_ubc")],
                       sig="S_vector")
            ps_u.consumed = tue
            tcu = P.op("vector", lambda v, i=Cc[0:64, csl]:
                       v.tensor_tensor(CUT[0:64, :], i, UBC[0:64, :], ALU.mult),
                       deps=[tue, g.get("_l0_cut")], sig="S_vector")
            ps_y = psum[0 + h % 2]
            m1 = (Xtok[:, c * 512 + h * 64: c * 512 + (h + 1) * 64], GH, ps_y.ap[0:64, 0:128], tgh)
            m2 = (SB16[0:64, h * 64:(h + 1) * 64], CUT[0:64, :], ps_y.ap[0:64, 0:128], tcu)
            gy = mmgroup(P, ps_y, [m1, m2], deps=[xt_done, sb_tok])
            yv = YT[:, (h // 2) * T + c * 128:(h // 2) * T + (c + 1) * 128][(h % 2) * 64:(h % 2) * 64 + 64, :]
            xv = XT8[:, (h // 2) * T + c * 128:(h // 2) * T + (c + 1) * 128][(h % 2) * 64:(h % 2) * 64 + 64, :]
            ty = P.op("vector", lambda v, o=yv, i=xv, dc=DCB[(h % 2) * 64:(h % 2) * 64 + 64, h:h + 1],
                      pf=ps_y.ap[0:64, 0:128]:
                      v.scalar_tensor_tensor(o, i, dc, pf, ALU.mult, ALU.add),
                      deps=[gy, t_d2, x8_done], sig="S_vector")
            ps_y.consumed = ty
            g["_yh%d" % h] = ty
            if "_l0_w2c%d" % h not in g:
                g["_l0_w2c%d" % h] = fa.alloc(1, F32)
            w2c = g["_l0_w2c%d" % h]
            tw2a = P.op("vector", lambda v, o=W2S, a=ACB[:, h * 16 + c: h * 16 + c + 1],
                        b=WLT[:, h:h + 1]:
                        v.tensor_tensor(o, a, b, ALU.subtract),
                        deps=[twe, t_ac, g.get("_l0_w2s")], sig="S_vector")
            tw2b = P.op("scalar", lambda s: s.activation(W2S, W2S, AF.Exp),
                        deps=[tw2a], sig="S_scalar")
            tw2 = P.op("vector", lambda v, o=w2c, b=DLT[:, h:h + 1]:
                       v.tensor_tensor(o, W2S, b, ALU.mult),
                       deps=[tw2b, tde, g.get("_l0_bw%d" % h)], sig="S_vector")
            g["_l0_w2s"] = tw2
            BW = BWB[h % 2]
            tbw = P.op("vector", lambda v, o=BW[:, 0:64], i=Btok[:, c * 64:(c + 1) * 64], w=w2c:
                       v.tensor_scalar(o, i, w, None, ALU.mult),
                       deps=[tw2, g.get("_l0_bwmm%d" % (h % 2))], sig="S_vector")
            g["_l0_bw%d" % h] = tbw
            ps_su = psum[2 + h % 2]
            tsu = P.op("tensor", lambda t, l=BW[:, 0:64],
                       r=Xtok[:, c * 512 + h * 64: c * 512 + (h + 1) * 64],
                       pf=ps_su.ap[0:64, 0:64]:
                       t.matmul(pf, l, r, start=True, stop=True),
                       deps=[tbw, ps_su.consumed], sig="S_tensor")
            g["_l0_bwmm%d" % (h % 2)] = tsu
            tss = P.op("vector", lambda v, o=S[0:64, h * 64:(h + 1) * 64],
                       e=EXPACB[0:64, h * 16 + c: h * 16 + c + 1],
                       pf=ps_su.ap[0:64, 0:64]:
                       v.scalar_tensor_tensor(o, o, e, pf, ALU.mult, ALU.add),
                       deps=[tsu, sb_tok], sig="S_vector")
            ps_su.consumed = tss
            su_last = tss
            g["_l0_gh"] = gy
            g["_l0_cbmd"] = gy
            g["_l0_ubc"] = tue
            g["_l0_cut"] = tcu
        g["_l0_wlt"] = twe
        g["_l0_dlt"] = tde
        g["_l0_cbm"] = tcbm
        g["_l0_sb"] = sb_tok
        wu_prev = tsu
    ydel = g["dram"]("l0a_delta", [DM, T], BF16)
    ydeps = [g["_yh%d" % h] for h in range(H)] + [su_last]
    GA.ptr = ga_start
    wzg, wzg_t = load_wslabs(P, g, W("L0_attn_Wzg"), 2048, 512, deps=ydeps)
    wout, wout_t = load_wslabs(P, g, W("L0_attn_Wout"), 512, DM, deps=ydeps)
    zsl = [GA.alloc(512), GA.alloc(512)]
    stg2 = GA.alloc(512)
    yg_toks = []
    prev_tb_mm = None
    for tb in range(NT):
        zload = [None] * 16
        mm_t = {}
        for zc in range(16):
            zload[zc] = P.dma("gpsimd", zsl[zc % 2],
                              zdram[zc * 128:(zc + 1) * 128, tb * 512:(tb + 1) * 512],
                              deps=[z_w[zc], mm_t.get(zc - 2), prev_tb_mm]
                              + (ydeps if zc < 2 else []))
            tk = None
            for oc in range(4):
                ps = psum[oc]
                tk = P.op("tensor", lambda t, l=wzg[zc][:, oc * 128:(oc + 1) * 128],
                          r=zsl[zc % 2], st=(zc == 0), sp=(zc == 15), pf=ps.ap:
                          t.matmul(pf, l, r, start=st, stop=sp),
                          deps=[zload[zc], wzg_t[zc]] + ([ps.consumed] if zc == 0 else []),
                          sig="S_tensor" if oc == 3 else None)
            mm_t[zc] = tk
        prev_tb_mm = mm_t[15]
        for oc in range(4):
            ps = psum[oc]
            yg = P.op("vector", lambda v, o=YT[:, oc * T + tb * 512: oc * T + (tb + 1) * 512],
                      pf=ps.ap: v.tensor_tensor(o, o, pf, ALU.mult),
                      deps=[mm_t[15]] + ydeps, sig="S_vector")
            ps.consumed = yg
            yg_toks.append(yg)
    for oc in range(DC):
        for tb in range(NT):
            ps = psum[4 + ((oc * NT + tb) % 2)]
            mms = [(wout[kc][:, oc * 128:(oc + 1) * 128],
                    YT[:, kc * T + tb * 512: kc * T + (tb + 1) * 512],
                    None, wout_t[kc]) for kc in range(4)]
            gt = mmgroup(P, ps, mms, deps=[yg_toks[-1], yg_toks[0]])
            ev = P.op("vector", lambda v, o=stg2, pf=ps.ap:
                      v.tensor_copy(o, pf), deps=[gt, g.get("_l0_stg")], sig="S_vector")
            dk = P.dma("sync", ydel[oc * 128:(oc + 1) * 128, tb * 512:(tb + 1) * 512],
                       stg2, deps=[ev])
            ps.consumed = dk
            g["_l0_stg"] = dk
    rs_toks = [P.dma("sync", g["XT"][:, cc * T:(cc + 1) * T], xres[cc * 128:(cc + 1) * 128, :],
                     deps=[g["_l0_stg"]]) for cc in range(DC)]
    GA.ptr = ga_start
    DB = GA.alloc(2 * T)
    db_ld = [None, None]
    for cc in range(DC):
        buf = DB[:, (cc % 2) * T:(cc % 2 + 1) * T]
        lt = P.dma("sync", buf, ydel[cc * 128:(cc + 1) * 128, :],
                   deps=[g["_l0_stg"], db_ld[cc % 2], yg_toks[-1]])
        tk = P.op("vector", lambda v, o=g["XT"][:, cc * T:(cc + 1) * T], i=buf:
                  v.tensor_tensor(o, o, i, ALU.add), deps=[lt, rs_toks[cc]], sig="S_vector")
        db_ld[cc % 2] = tk


def emit_L1_attn(P, g):
    W, GA, XN, psum = g["W"], g["GA"], g["XN"], g["psum"]
    H = 16
    xdeps = emit_layernorm(P, g)
    xres = g["dram"]("l1a_xres", [DM, T])
    ydel = g["dram"]("l1a_delta", [DM, T], BF16)
    sp_toks = [P.dma("sync", xres[c * 128:(c + 1) * 128, :], g["XT"][:, c * T:(c + 1) * T],
                     deps=list(xdeps)) for c in range(DC)]
    QT = g["XTspace"][:, 0:DC * T]
    KT = g["XTspace"][:, DC * T:2 * DC * T]
    VTOK = GA.alloc(TC * 1024)
    onesr = GA.alloc(512)
    t_ones = P.op("vector", lambda v: v.memset(onesr[0:1, :], 1.0), sig="S_vector")
    brow = GA.alloc(3072)
    t_br = P.dma("gpsimd", brow[0:1, :], W("L1_attn_b_attn"))
    ga0 = GA.ptr

    def proj_fm(col0, dst, deps0):
        slabs, st = load_wslabs(P, g, W("L1_attn_W_attn")[:, col0:col0 + 1024],
                                DM, 1024, deps=deps0)
        done = None
        for oc in range(DC):
            for tb in range(NT):
                ps = psum[(oc * NT + tb) % 2]
                mms = [(slabs[kc][:, oc * 128:(oc + 1) * 128],
                        XN[:, kc * T + tb * 512: kc * T + (tb + 1) * 512],
                        None, st[kc]) for kc in range(DC)]
                mms.append((brow[0:1, col0 + oc * 128: col0 + (oc + 1) * 128],
                            onesr[0:1, 0:512], None, t_br))
                gt = mmgroup(P, ps, mms, deps=list(xdeps) + [t_ones] +
                             (sp_toks if col0 == 0 else []))
                done = P.op("vector", lambda v, o=dst[:, oc * T + tb * 512: oc * T + (tb + 1) * 512],
                            pf=ps.ap: v.tensor_copy(o, pf), deps=[gt], sig="S_vector")
                ps.consumed = done
        return done

    q_done = proj_fm(0, QT, [])
    GA.ptr = ga0
    k_done = proj_fm(1024, KT, [q_done])
    GA.ptr = ga0
    slabs, st = load_wslabs(P, g, W("L1_attn_W_attn")[:, 2048:3072], DM, 1024,
                            deps=[k_done])
    v_done = None
    for tc in range(TC):
        for fb in range(2):
            ps = psum[2 + fb]
            mms = [(XN[:, kc * T + tc * 128: kc * T + (tc + 1) * 128],
                    slabs[kc][:, fb * 512:(fb + 1) * 512], None, st[kc])
                   for kc in range(DC)]
            mms.append((onesr[0:1, 0:128],
                        brow[0:1, 2048 + fb * 512: 2048 + (fb + 1) * 512], None, t_br))
            gt = mmgroup(P, ps, mms, deps=list(xdeps) + [t_ones])
            v_done = P.op("vector", lambda v, o=VTOK[:, tc * 1024 + fb * 512: tc * 1024 + (fb + 1) * 512],
                          pf=ps.ap: v.tensor_copy(o, pf), deps=[gt], sig="S_vector")
            ps.consumed = v_done
    GA.ptr = ga0
    KSC = GA.alloc(8, F32)
    KSCB = GA.alloc(8)
    ks_t = []
    for c in range(DC):
        tk = P.op("vector", lambda v, o=KSC[:, c:c + 1], i=KT[:, c * T:(c + 1) * T]:
                  v.tensor_reduce(o, i, AX, ALU.add), deps=[k_done], sig="S_vector")
        ks_t.append(tk)
    ksb = P.op("vector", lambda v: v.tensor_copy(KSCB, KSC), deps=ks_t, sig="S_vector")
    sgr = g["FAcol"].alloc(1, F32)
    t_sg = P.dma("sync", sgr[0:1, 0:1], W("L1_attn_score_gain"))
    gcol, g_t = bcast_cols(P, g, sgr[0:1, 0:1], 1, [t_sg])
    SC2 = [GA.alloc(T), GA.alloc(T)]
    _EB = GA.alloc(T)
    EB2 = [_EB, _EB]
    ATB2 = [GA.alloc(T), GA.alloc(T)]
    YH2 = [GA.alloc(DM), GA.alloc(DM)]
    stg = GA.alloc(512)
    fa = g["FAcol"]
    mcol2 = [fa.alloc(1, F32), fa.alloc(1, F32)]
    madc2 = [fa.alloc(1, F32), fa.alloc(1, F32)]
    ccol2 = [fa.alloc(1, F32), fa.alloc(1, F32)]
    rrc2 = [fa.alloc(1, F32), fa.alloc(1, F32)]
    wp, wp_t = load_wslabs(P, g, W("L1_attn_W_proj"), DM, DM, deps=[v_done])
    bpcol = fa.alloc(DC, F32)
    t_bp = P.dma("gpsimd", bpcol, W("L1_attn_b_proj").rearrange("one (oc p) -> p (one oc)", p=128),
                 allow_slow_non_contiguous=True)
    iters = [(tci, h) for tci in range(TC) for h in range(H)]
    state = {}

    def front(k):
        tci, h = iters[k]
        pp = k % 2
        SC = SC2[pp]
        mcol = mcol2[pp]
        qsl = QT[:, (h // 2) * T + tci * 128: (h // 2) * T + (tci + 1) * 128][(h % 2) * 64:(h % 2) * 64 + 64, :]
        bank_toks = []
        for bb in range(4):
            ps = psum[bb]
            ksl = KT[:, (h // 2) * T + bb * 512: (h // 2) * T + (bb + 1) * 512][(h % 2) * 64:(h % 2) * 64 + 64, :]
            gt = mmgroup(P, ps, [(qsl, ksl, None, None)], deps=[q_done, k_done])
            bank_toks.append((ps, gt))
        tci_, h_ = iters[k]
        s_end_f = (tci_ + 1) * 128
        nb_cf = (s_end_f + 511) // 512
        racc = fa_racc(g, fa, pp)
        ps5 = psum[7]
        tm = P.op("tensor", lambda t, l=qsl,
                  r=KSCB[(h % 2) * 64:(h % 2) * 64 + 64, (h // 2):(h // 2) + 1],
                  pf=ps5.ap[:, pp:pp + 1]: t.matmul(pf, l, r, start=True, stop=True),
                  deps=[ksb, ps5.consumed], sig="S_tensor")
        tmc = P.op("vector", lambda v, o=mcol, pf=ps5.ap[:, pp:pp + 1]:
                   v.tensor_scalar(o, pf, -1.0 / T, None, ALU.mult),
                   deps=[tm, g.get("_l1_cols%d" % pp)], sig="S_vector")
        ps5.consumed = tmc
        d_toks = []
        eb_guard = g.get("_l1_eb")
        for bb, (ps, gt) in enumerate(bank_toks):
            ta = P.op("scalar", lambda s, o=EB2[0][:, bb * 512:(bb + 1) * 512],
                      pf=ps.ap, m=mcol, a=racc[:, bb:bb + 1]:
                      s.activation(o, pf, AF.Abs, bias=m, accum_out=a),
                      deps=[gt, tmc, eb_guard, g.get("_l1_ra%d" % pp)], sig="S_scalar")
            if bb < nb_cf:
                tk = P.op("scalar", lambda s, o=SC[:, bb * 512:(bb + 1) * 512],
                          pf=ps.ap, m=mcol:
                          s.activation(o, pf, AF.Relu, bias=m),
                          deps=[ta, g.get("_l1_sc%d" % pp)], sig="S_scalar")
            else:
                tk = ta
            ps.consumed = tk
            d_toks.append(tk)
        state[k] = d_toks

    def middle(k):
        tci, h = iters[k]
        pp = k % 2
        s_end = (tci + 1) * 128
        SC, EB = SC2[pp], EB2[pp]
        madc, ccol, rrc = madc2[pp], ccol2[pp], rrc2[pp]
        racc = fa_racc(g, fa, pp)
        d_toks = state.pop(k)
        tmad = P.op("vector", lambda v, o=madc, i=racc[:, 0:4]:
                    v.tensor_reduce(o, i, AX, ALU.add), deps=d_toks, sig="S_vector")
        t1 = P.op("vector", lambda v, o=madc: v.tensor_scalar(o, o, 1.0 / T, 1e-6,
                                                              ALU.mult, ALU.add),
                  deps=[tmad], sig="S_vector")
        g["_l1_ra%d" % pp] = tmad
        t2 = P.op("vector", lambda v, o=ccol, i=madc: v.reciprocal(o, i),
                  deps=[t1], sig="S_vector")
        t3 = P.op("vector", lambda v, o=ccol: v.tensor_tensor(o, o, gcol, ALU.mult),
                  deps=[t2, g_t], sig="S_vector")
        nb_c = (s_end + 511) // 512
        u_toks = []
        for bb in range(nb_c):
            w = min(512, s_end - bb * 512)
            tk = P.op("vector", lambda v, o=SC[:, bb * 512: bb * 512 + w], c2=ccol:
                      v.tensor_scalar(o, o, c2, None, ALU.mult),
                      deps=[t3], sig="S_vector")
            u_toks.append(tk)
        tmk = P.op("vector", lambda v, o=SC[:, (tci * 128):(tci * 128) + 128]:
                   v.tensor_tensor(o, o, g["trilb_sb"], ALU.mult),
                   deps=u_toks, sig="S_vector")
        r_toks = []
        for bb in range(nb_c):
            w = min(512, s_end - bb * 512)
            te = P.op("vector", lambda v, o=EB[:, bb * 512: bb * 512 + w],
                      i=SC[:, bb * 512: bb * 512 + w]:
                      v.tensor_tensor(o, i, i, ALU.mult),
                      deps=[tmk, g.get("_l1_eb")], sig="S_vector")
            tr = P.op("vector", lambda v, o=SC[:, bb * 512: bb * 512 + w],
                      i=EB[:, bb * 512: bb * 512 + w], a=racc[:, 4 + bb:5 + bb]:
                      v.scalar_tensor_tensor(o, i, 1.0, i, ALU.mult, ALU.mult,
                                             accum_out=a),
                      deps=[te], sig="S_vector")
            r_toks.append(tr)
        tz = P.op("vector", lambda v, o=rrc, i=racc[:, 4:4 + nb_c]:
                  v.tensor_reduce(o, i, AX, ALU.add), deps=r_toks, sig="S_vector")
        t5 = P.op("vector", lambda v, o=rrc:
                  v.tensor_scalar(o, o, 1.0, 1e-6, ALU.mult, ALU.add),
                  deps=[tz], sig="S_vector")
        t5 = P.op("vector", lambda v, o=rrc, i=rrc: v.reciprocal(o, i),
                  deps=[t5], sig="S_vector")
        tn = P.op("vector", lambda v, o=SC[:, 0:s_end], r2=rrc:
                  v.tensor_scalar(o, o, r2, None, ALU.mult), deps=[t5], sig="S_vector")
        g["_l1_eb"] = r_toks[-1]
        g["_l1_cols%d" % pp] = t5
        state[("n", k)] = tn

    def back(k):
        tci, h = iters[k]
        pp = k % 2
        s_end = (tci + 1) * 128
        nsc = tci + 1
        yh_p = tci % 2
        YH = YH2[yh_p]
        SC, ATB = SC2[pp], ATB2[pp]
        tn = state.pop(("n", k))
        ps_t = psum[5]
        ps_y = psum[4]
        ev = None
        for scb in range(0, nsc, 4):
            nn = min(4, nsc - scb)
            tts = []
            for j in range(nn):
                tt = P.op("tensor", lambda t, i=SC[:, (scb + j) * 128:(scb + j + 1) * 128],
                          pf=ps_t.ap.bitcast(BF16)[:, pp * 512 + j * 128: pp * 512 + (j + 1) * 128]:
                          t.transpose(pf, i, g["eyeb_sb"]),
                          deps=[tn, ps_t.consumed if j == 0 else None], sig="S_tensor")
                tts.append(tt)
            ev = P.op("vector", lambda v, o=ATB[:, scb * 128: scb * 128 + nn * 128],
                      pf=ps_t.ap.bitcast(BF16)[:, pp * 512: pp * 512 + nn * 128]:
                      v.tensor_copy(o, pf),
                      deps=[tts[-1]], sig="S_vector")
            ps_t.consumed = ev
        g["_l1_sc%d" % pp] = ev
        mms = [(VTOK[:, sc * 1024 + h * 64: sc * 1024 + (h + 1) * 64],
                ATB[:, sc * 128:(sc + 1) * 128],
                ps_y.ap[0:64, 128 + pp * 128: 256 + pp * 128], ev) for sc in range(nsc)]
        gt = mmgroup(P, ps_y, mms, deps=[ev, v_done])
        yev = P.op("vector", lambda v, o=YH[(h % 2) * 64:(h % 2) * 64 + 64,
                                            (h // 2) * 128:(h // 2) * 128 + 128],
                   pf=ps_y.ap[0:64, 128 + pp * 128: 256 + pp * 128]:
                   v.tensor_copy(o, pf),
                   deps=[gt, g.get("_l1_yh%d" % yh_p)], sig="S_vector")
        ps_y.consumed = yev
        g["_yh%d" % h] = yev
        if h == H - 1:
            ydeps = [g["_yh%d" % hh] for hh in range(H)]
            for oc in range(DC):
                ps = psum[6]
                mms2 = [(wp[kc][:, oc * 128:(oc + 1) * 128],
                         YH[:, kc * 128:(kc + 1) * 128], None, wp_t[kc]) for kc in range(DC)]
                gt2 = mmgroup(P, ps, mms2, deps=ydeps, sub=ps.ap[:, 0:128])
                ev2 = P.op("vector", lambda v, o=stg[:, 0:128], pf=ps.ap[:, 0:128],
                           b=bpcol[:, oc:oc + 1]:
                           v.tensor_scalar(o, pf, b, None, ALU.add),
                           deps=[gt2, t_bp, g.get("_l1_stg")], sig="S_vector")
                dk = P.dma("sync", ydel[oc * 128:(oc + 1) * 128, tci * 128:(tci + 1) * 128],
                           stg[:, 0:128], deps=[ev2])
                ps.consumed = dk
                g["_l1_stg"] = dk
                g["_l1_yh%d" % yh_p] = gt2

    N = len(iters)
    for k in range(N + 1):
        if k < N:
            front(k)
        if k >= 1:
            back(k - 1)
        if k < N:
            middle(k)
    rs_toks = [P.dma("sync", g["XT"][:, c * T:(c + 1) * T], xres[c * 128:(c + 1) * 128, :],
                     deps=[g["_l1_stg"], q_done, k_done]) for c in range(DC)]
    DB = GA.alloc(2 * T)
    db_ld = [None, None]
    for c in range(DC):
        buf = DB[:, (c % 2) * T:(c % 2 + 1) * T]
        lt = P.dma("sync", buf, ydel[c * 128:(c + 1) * 128, :],
                   deps=[g["_l1_stg"], db_ld[c % 2]])
        tk = P.op("vector", lambda v, o=g["XT"][:, c * T:(c + 1) * T], i=buf:
                  v.tensor_tensor(o, o, i, ALU.add), deps=[lt, rs_toks[c]], sig="S_vector")
        db_ld[c % 2] = tk


def fa_racc(g, fa, pp):
    if "_l1_racc" not in g:
        g["_l1_racc"] = fa.alloc(16, F32)
    return g["_l1_racc"][:, pp * 8:(pp + 1) * 8]


def emit_rope(P, g, buf, nblk, tpos0, ready):
    psum = g["psum"]
    SQ = [g["SQ0"], g["SQ1"]]
    ncol = nblk // 4 * 512
    t0 = tpos0 or 0
    last = None
    for ch in range(4):
        for tb in range(ncol // 512):
            col = ch * ncol + tb * 512
            tcol = t0 + tb * 512
            ps = psum[2]
            t1 = P.op("tensor", lambda t, r=buf[:, col:col + 512], pf=ps.ap:
                      t.matmul(pf, g["roperot_sb"], r, start=True, stop=True),
                      deps=[ps.consumed, ready], sig="S_tensor")
            t2 = P.op("vector", lambda v, o=SQ[0], i=buf[:, col:col + 512],
                      cs=g["cos4_sb"][:, tcol:tcol + 512]:
                      v.tensor_tensor(o, i, cs, ALU.mult),
                      deps=[ready, last, g["_cs_tok"], g["_cs_tok2"]], sig="S_vector")
            t3 = P.op("vector", lambda v, o=SQ[1], pf=ps.ap,
                      sn=g["sin4_sb"][:, tcol:tcol + 512]:
                      v.tensor_tensor(o, pf, sn, ALU.mult), deps=[t1, last], sig="S_vector")
            last = P.op("vector", lambda v, o=buf[:, col:col + 512], a=SQ[0], b=SQ[1]:
                        v.tensor_tensor(o, a, b, ALU.add), deps=[t2, t3], sig="S_vector")
            ps.consumed = last
    return last


def emit_L3_ffn(P, g):
    emit_moe(P, g, "L3_ffn", 4096, router_L3, add_xn=False, TBW=256)


def emit_L0_ffn(P, g):
    emit_moe(P, g, "L0_ffn", 2048, router_L0, add_xn=True, TBW=256)


LAYER_EMITTERS = {}


def build_program(layers, weight_names):
    nc = bass.Bass("TRN2", target_bir_lowering=False)
    es = ExitStack()
    g = {}
    xin = nc.dram_tensor("x_in", [DM, T], F32, kind="ExternalInput").ap()
    xout = nc.dram_tensor("x_out", [DM, T], F32, kind="ExternalOutput").ap()
    wdram = {}

    def W(name):
        if name not in wdram:
            shape = list(WEIGHT_SHAPES[name])
            if len(shape) == 0:
                shape = [1]
            if len(shape) == 1:
                shape = [1] + shape
            wdram[name] = nc.dram_tensor(name, shape, F32, kind="ExternalInput").ap()
        return wdram[name]
    g["W"] = W
    scratch = {}

    def dram(name, shape, dtype=F32):
        if name not in scratch:
            scratch[name] = nc.dram_tensor("scr_" + name, list(shape), dtype,
                                           kind="Internal").ap()
        return scratch[name]
    g["dram"] = dram
    g["ones_sb"] = es.enter_context(nc.sbuf_tensor("ones_sb", [128, 128], BF16))[:]
    g["eye_sb"] = es.enter_context(nc.sbuf_tensor("eye_sb", [128, 128], F32))[:]
    g["eyeb_sb"] = es.enter_context(nc.sbuf_tensor("eyeb_sb", [128, 128], BF16))[:]
    g["ones1f_sb"] = es.enter_context(nc.sbuf_tensor("ones1f_sb", [1, 128], F32))[:]
    cones = nc.dram_tensor("c_ones", [128, 128], BF16, kind="ExternalInput").ap()
    ceye = nc.dram_tensor("c_eye", [128, 128], F32, kind="ExternalInput").ap()
    ceyeb = nc.dram_tensor("c_eyeb", [128, 128], BF16, kind="ExternalInput").ap()
    ccols = nc.dram_tensor("c_cols", [128, 16], F32, kind="ExternalInput").ap()
    chm = nc.dram_tensor("c_hmask", [16, 1024], BF16, kind="ExternalInput").ap()
    g["hmask_sb"] = es.enter_context(nc.sbuf_tensor("hmask_sb", [16, 1024], BF16))[:]
    cmf = nc.dram_tensor("c_maskf", [128, 128], F32, kind="ExternalInput").ap()
    g["maskf_sb"] = es.enter_context(nc.sbuf_tensor("maskf_sb", [128, 128], F32))[:]
    g["c_cos4"] = nc.dram_tensor("c_cos4", [128, T], BF16, kind="ExternalInput").ap()
    g["c_sin4"] = nc.dram_tensor("c_sin4", [128, T], BF16, kind="ExternalInput").ap()
    crot = nc.dram_tensor("c_roperot", [128, 128], BF16, kind="ExternalInput").ap()
    g["roperot_sb"] = es.enter_context(nc.sbuf_tensor("roperot_sb", [128, 128], BF16))[:]
    ctri = nc.dram_tensor("c_trilb", [128, 128], BF16, kind="ExternalInput").ap()
    g["trilb_sb"] = es.enter_context(nc.sbuf_tensor("trilb_sb", [128, 128], BF16))[:]
    ctri2 = nc.dram_tensor("c_triub", [128, 128], BF16, kind="ExternalInput").ap()
    g["triub_sb"] = es.enter_context(nc.sbuf_tensor("triub_sb", [128, 128], BF16))[:]
    cmfl = nc.dram_tensor("c_maskfl", [128, 128], F32, kind="ExternalInput").ap()
    g["maskfl_sb"] = es.enter_context(nc.sbuf_tensor("maskfl_sb", [128, 128], F32))[:]
    cpm = nc.dram_tensor("c_pmask", [128, 7 * 128], BF16, kind="ExternalInput").ap()
    g["pmask_sb"] = es.enter_context(nc.sbuf_tensor("pmask_sb", [128, 7 * 128], BF16))[:]
    g["cc_sb"] = es.enter_context(nc.sbuf_tensor("cc_sb", [128, 16], F32))[:]
    g["cc_vals"] = []

    def cc(val):
        val = float(val)
        if val not in g["cc_vals"]:
            assert len(g["cc_vals"]) < 16
            g["cc_vals"].append(val)
        i = g["cc_vals"].index(val)
        return g["cc_sb"][:, i:i + 1]
    g["cc"] = cc
    GAN = 98176 + 4096
    GA = Arena(es.enter_context(nc.sbuf_tensor("GA", [128, GAN], BF16))[:], GAN)
    g["GA"] = g["BA"] = g["FA"] = GA
    g["XT"] = GA.alloc(DC * T, F32)
    g["XTspace"] = GA.ap[:, 0:2 * DC * T]
    g["XN"] = GA.alloc(DC * T)
    g["XNspace"] = GA.ap[:, 2 * DC * T: 3 * DC * T]

    for nm in ("M", "MSQ", "VT", "SD", "R", "MR"):
        if nm == "MSQ":
            g["scr2k"] = GA.ap[:, GA.ptr:GA.ptr + 2048]
        g[nm] = GA.alloc(512, F32)
    g["SQ0"] = GA.alloc(512)
    g["SQ1"] = GA.alloc(512)
    g["colmax"] = GA.alloc(8, F32)
    g["cmax1"] = GA.alloc(1, F32)
    _r2 = GA.alloc(8, F32)
    g["row2"], g["row2b"] = _r2[0:1, 0:4], _r2[0:1, 4:8]
    GA.set_mark()
    g["FAcol"] = Arena(es.enter_context(nc.sbuf_tensor("FAcol", [128, 512], BF16))[:], 512)
    g["psum"] = [Psum(es.enter_context(nc.psum_tensor(f"P{i}", [128, 512], F32))[:])
                 for i in range(8)]
    g["nc"] = nc

    P = Prog()
    g["P"] = P
    ld = []
    for c in range(DC):
        ld.append(P.dma("sync", g["XT"][:, c * T:(c + 1) * T], xin[c * 128:(c + 1) * 128, :]))
    ld.append(P.dma("sync", g["ones_sb"], cones))
    ld.append(P.dma("sync", g["eye_sb"], ceye))
    ld.append(P.dma("sync", g["eyeb_sb"], ceyeb))
    ld.append(P.dma("sync", g["cc_sb"], ccols))
    ld.append(P.dma("sync", g["hmask_sb"], chm))
    ld.append(P.dma("sync", g["maskf_sb"], cmf))
    ld.append(P.dma("sync", g["roperot_sb"], crot))
    ld.append(P.dma("sync", g["trilb_sb"], ctri))
    ld.append(P.dma("sync", g["triub_sb"], ctri2))
    ld.append(P.dma("sync", g["maskfl_sb"], cmfl))
    ld.append(P.dma("sync", g["pmask_sb"], cpm))
    g["ones1_tok"] = P.op("vector", lambda v: v.memset(g["ones1f_sb"], 1.0), sig="S_vector")
    for tok in ld + [g["ones1_tok"]]:
        for e in ENGS:
            P.op(e, None, deps=[tok])

    for i, name in enumerate(layers):
        LAYER_EMITTERS[name](P, g)
        P.barrier()
        g["GA"].reset()
        g["FAcol"].reset()

    sts = []
    for c in range(DC):
        sts.append(P.dma("sync", xout[c * 128:(c + 1) * 128, :], g["XT"][:, c * T:(c + 1) * T]))
    for tok in sts:
        P.op("sync", None, deps=[tok])

    sems = {}
    with ExitStack() as es2:
        for s in sorted(P.sem_names):
            sems[s] = es2.enter_context(nc.semaphore(s))
        block = es2.enter_context(nc.Block())
        P.replay(block, sems)
    es.close()
    return nc, sorted(wdram.keys()), g


WEIGHT_SHAPES = {
    "L0_attn_Wz": (1024, 2048), "L0_attn_Win": (1024, 648),
    "L0_attn_Wzg": (2048, 512), "L0_attn_Wout": (512, 1024),
    "L0_attn_dt_bias": (8,), "L0_attn_A_log": (8,), "L0_attn_D_param": (8,),
    "L0_ffn_rw1": (1024, 1024), "L0_ffn_rb1": (1024,), "L0_ffn_rw2": (1024, 8),
    "L0_ffn_rb2": (8,), "L0_ffn_ew1": (8, 1024, 2048), "L0_ffn_eb1": (8, 2048),
    "L0_ffn_ew2": (8, 2048, 1024), "L0_ffn_eb2": (8, 1024),
    "L1_attn_W_attn": (1024, 3072), "L1_attn_b_attn": (3072,),
    "L1_attn_W_proj": (1024, 1024), "L1_attn_b_proj": (1024,),
    "L1_attn_score_gain": (),
    "L1_ffn_kernel": (1024, 1024), "L1_ffn_bias": (1024,),
    "L1_ffn_a": (1024, 16), "L1_ffn_b": (16, 1024),
    "L2_attn_Wq": (1024, 1024), "L2_attn_Wk": (1024, 1024),
    "L2_attn_Wv": (1024, 1024), "L2_attn_Wo": (1024, 1024),
    "L2_ffn_W_exp": (1024, 4096), "L2_ffn_b_exp": (4096,),
    "L2_ffn_up_weights": (12, 2048, 2, 2), "L2_ffn_up_bias": (4096,),
    "L2_ffn_down_weights": (12, 2048, 2, 2), "L2_ffn_W_con": (4096, 1024),
    "L2_ffn_b_con": (1024,), "L2_ffn_down_bias": (1024,),
    "L3_attn_W_kv": (1024, 256), "L3_attn_g_kv": (256,),
    "L3_attn_W_kdec": (256, 1024), "L3_attn_W_vdec": (256, 1024),
    "L3_attn_W_q": (1024, 512), "L3_attn_W_qdec": (512, 1024),
    "L3_attn_W_krope": (1024, 512), "L3_attn_W_qrope": (512, 512),
    "L3_attn_W_o": (1024, 1024),
    "L3_ffn_gate_w": (1024, 8), "L3_ffn_gate_b": (8,),
    "L3_ffn_W1": (8, 1024, 4096), "L3_ffn_b1": (8, 4096),
    "L3_ffn_W2": (8, 4096, 1024), "L3_ffn_b2": (8, 1024),
}

ALL_LAYERS = ["L0_attn", "L0_ffn", "L1_attn", "L1_ffn",
              "L2_attn", "L2_ffn", "L3_attn", "L3_ffn"]

LAYER_EMITTERS["L1_ffn"] = emit_L1_ffn
LAYER_EMITTERS["L3_ffn"] = emit_L3_ffn
LAYER_EMITTERS["L0_ffn"] = emit_L0_ffn
LAYER_EMITTERS["L2_attn"] = emit_L2_attn
LAYER_EMITTERS["L3_attn"] = emit_L3_attn
LAYER_EMITTERS["L1_attn"] = emit_L1_attn
LAYER_EMITTERS["L0_attn"] = emit_L0_attn
LAYER_EMITTERS["L2_ffn"] = emit_L2_ffn


def _pmasks():
    pm = np.zeros((128, 7 * 128), dtype=ml_dtypes.bfloat16)
    for st in range(7):
        for p in range(128):
            pm[p, st * 128 + (p ^ (1 << st))] = 1
    return pm


def _consts_np():
    hm = np.zeros((16, 1024), dtype=ml_dtypes.bfloat16)
    for c in range(8):
        for p in range(128):
            hm[2 * c + p // 64, c * 128 + p] = 1
    mf = np.triu(np.full((128, 128), -1e9, dtype=np.float32), 1)
    ROPE = 32
    inv_freq = 1.0 / 10000.0 ** (np.arange(0, ROPE, 2, dtype=np.float32) / ROPE)
    tt = np.arange(T, dtype=np.float32)
    emb = np.concatenate([np.outer(tt, inv_freq)] * 2, -1)
    cos1, sin1 = np.cos(emb).T, np.sin(emb).T
    cos4 = np.tile(cos1, (4, 1)).astype(ml_dtypes.bfloat16)
    sin4 = np.tile(sin1, (4, 1)).astype(ml_dtypes.bfloat16)
    rot = np.zeros((128, 128), dtype=np.float32)
    for hh in range(4):
        b = hh * 32
        for i in range(16):
            rot[b + i + 16, b + i] = -1.0
            rot[b + i, b + i + 16] = 1.0
    return {
        "c_ones": np.ones((128, 128), dtype=ml_dtypes.bfloat16),
        "c_eye": np.eye(128, dtype=np.float32),
        "c_eyeb": np.eye(128, dtype=ml_dtypes.bfloat16),
        "c_hmask": hm,
        "c_maskf": mf,
        "c_cos4": cos4,
        "c_sin4": sin4,
        "c_roperot": rot.astype(ml_dtypes.bfloat16),
        "c_trilb": np.tril(np.ones((128, 128))).astype(ml_dtypes.bfloat16),
        "c_triub": np.triu(np.ones((128, 128))).astype(ml_dtypes.bfloat16),
        "c_maskfl": np.tril(np.full((128, 128), -1e9, dtype=np.float32), -1),
        "c_pmask": _pmasks(),
    }


def run_layers(layers, x, inputs):
    nc, wnames, g = build_program(layers, None)
    consts = _consts_np()
    cvals = np.zeros((128, 16), dtype=np.float32)
    for i, v in enumerate(g["cc_vals"]):
        cvals[:, i] = v
    consts["c_cols"] = cvals
    in_maps = []
    for ci in range(NCORES):
        m = {"x_in": np.ascontiguousarray(x[ci].T)}
        for wn in wnames:
            a = np.asarray(inputs[wn], dtype=np.float32)
            if a.ndim == 0:
                a = a.reshape(1, 1)
            elif a.ndim == 1:
                a = a.reshape(1, -1)
            m[wn] = a
        m.update(consts)
        in_maps.append(m)
    res = run_bass_kernel_spmd(nc, in_maps, core_ids=list(range(NCORES)))
    out = np.stack([np.ascontiguousarray(r["x_out"].T) for r in res.results])
    return out


def kernel(**inputs):
    x = np.asarray(inputs["L0_attn_x"], dtype=np.float32)
    return run_layers(ALL_LAYERS, x, inputs)


# version 10
# speedup vs baseline: 1.1111x; 1.0084x over previous
import numpy as np
import ml_dtypes
from collections import defaultdict
from contextlib import ExitStack

import concourse.bass as bass
import concourse.mybir as mybir
from concourse.bass_utils import run_bass_kernel_spmd

F32 = mybir.dt.float32
BF16 = mybir.dt.bfloat16
AF = mybir.ActivationFunctionType
ALU = mybir.AluOpType
AX = mybir.AxisListType.X

T = 2048
NT = T // 512
TC = T // 128
DM = 1024
DC = DM // 128
NCORES = 8
MAGIC = float(np.float32(1.5 * 2 ** 23))

ENGS = ("tensor", "vector", "scalar", "gpsimd", "sync")


class Prog:

    NSLOT = 20

    def __init__(self):
        self.items = {e: [] for e in ENGS}
        self.counts = defaultdict(int)
        self.last_wait = {e: defaultdict(int) for e in ENGS}
        self.sem_names = set()
        self.dma_idx = defaultdict(int)

    def op(self, eng, fn, deps=(), sig=None, n=1):
        waits = []
        for d in deps:
            if d is None:
                continue
            s, v = d
            if self.last_wait[eng][s] >= v:
                continue
            self.last_wait[eng][s] = v
            self.sem_names.add(s)
            waits.append((s, v))
        incs = []
        tok = None
        if sig is not None:
            self.sem_names.add(sig)
            self.counts[sig] += n
            incs.append((sig, n))
            tok = (sig, self.counts[sig])
        self.items[eng].append((fn, waits, incs))
        return tok

    def dma(self, eng, out, in_, deps=(), **kw):
        i = self.dma_idx[eng]
        self.dma_idx[eng] += 1
        sem = f"D_{eng}{i % self.NSLOT}"
        prev = 16 * (i // self.NSLOT)
        deps = list(deps)
        if prev > 0:
            deps.append((sem, prev))
        return self.op(eng, lambda e: e.dma_start(out=out, in_=in_, **kw),
                       deps=deps, sig=sem, n=16)

    def barrier(self):
        toks = []
        for e in ENGS:
            if e in ("vector", "scalar", "gpsimd"):
                toks.append(self.op(e, lambda eng: eng.drain(), sig="B_" + e))
            else:
                toks.append(self.op(e, lambda eng: eng.nop(), sig="B_" + e))
        for e in ENGS:
            self.op(e, None, deps=[t for t in toks if t[0] != "B_" + e])

    def replay(self, block, sems):
        def runner(name):
            def run(engine):
                for fn, waits, incs in self.items[name]:
                    for s, v in waits:
                        engine.wait_ge(sems[s], v)
                    if fn is None:
                        continue
                    inst = fn(engine)
                    for s, nn in incs:
                        inst.then_inc(sems[s], nn)
            return run
        block.tensor(runner("tensor"))
        block.vector(runner("vector"))
        block.scalar(runner("scalar"))
        block.gpsimd(runner("gpsimd"))
        block.sync(runner("sync"))


class Psum:
    def __init__(self, ap):
        self.ap = ap
        self.consumed = None
        self.written = None


class Arena:

    def __init__(self, ap, n):
        self.ap, self.n, self.ptr = ap, n, 0
        self.hi = 0
        self.mark = 0

    def alloc(self, elems, dtype=BF16, align=2):
        nb = elems if dtype == BF16 else 2 * elems
        self.ptr = (self.ptr + align - 1) // align * align
        assert self.ptr + nb <= self.n, f"arena overflow {self.ptr}+{nb}>{self.n}"
        v = self.ap[:, self.ptr:self.ptr + nb]
        if dtype != BF16:
            v = v.bitcast(dtype)
        self.ptr += nb
        self.hi = max(self.hi, self.ptr)
        return v

    def set_mark(self):
        self.mark = self.ptr

    def reset(self):
        self.ptr = self.mark


def mmgroup(P, ps, mms, deps=(), sub=None):
    tok = None
    nmm = len(mms)
    for i, (l, r, o, d) in enumerate(mms):
        first, last = (i == 0), (i == nmm - 1)
        o = (ps.ap if sub is None else sub) if o is None else o
        dd = list(deps) + [d] + ([ps.consumed] if first else [])
        tok = P.op("tensor",
                   lambda t, oo=o, ll=l, rr=r, st=first, sp=last:
                   t.matmul(oo, ll, rr, start=st, stop=sp),
                   deps=dd, sig="S_tensor" if last else None)
    ps.written = tok
    return tok


def emit_layernorm(P, g, xdep=()):
    XT, XN = g["XT"], g["XN"]
    ones = g["ones_sb"]
    SQ = [g["SQ0"], g["SQ1"]]
    ps_s, ps_q = g["psum"][6], g["psum"][7]
    M, MSQ, VT, SD, R, MR = g["M"], g["MSQ"], g["VT"], g["SD"], g["R"], g["MR"]
    stats_prev = None
    done = []
    for tb in range(NT):
        cp_toks = []
        sq_mm = [None, None]
        t1 = t2 = None
        for c in range(DC):
            xt_c = XT[:, c * T + tb * 512: c * T + (tb + 1) * 512]
            xn_c = XN[:, c * T + tb * 512: c * T + (tb + 1) * 512]
            tcp = P.op("vector", lambda v, o=xn_c, i=xt_c: v.tensor_copy(o, i),
                       deps=list(xdep), sig="S_vector")
            cp_toks.append(tcp)
            sq = SQ[c % 2]
            tsq = P.op("scalar", lambda s, o=sq, i=xn_c: s.activation(o, i, AF.Square),
                       deps=[tcp, sq_mm[c % 2]], sig="S_scalar")
            t1 = P.op("tensor", lambda t, i=xn_c, st=(c == 0), sp=(c == DC - 1):
                      t.matmul(ps_s.ap, ones, i, start=st, stop=sp),
                      deps=[tcp] + ([ps_s.consumed, stats_prev] if c == 0 else []),
                      sig="S_tensor" if c == DC - 1 else None)
            t2 = P.op("tensor", lambda t, i=sq, st=(c == 0), sp=(c == DC - 1):
                      t.matmul(ps_q.ap, ones, i, start=st, stop=sp),
                      deps=[tsq] + ([ps_q.consumed] if c == 0 else []),
                      sig="S_tensor")
            sq_mm[c % 2] = t2
        tM = P.op("vector", lambda v: v.tensor_scalar(M, ps_s.ap, 1.0 / DM, None, ALU.mult),
                  deps=[t1, stats_prev], sig="S_vector")
        ps_s.consumed = tM
        tM2 = P.op("scalar", lambda s: s.activation(MSQ, M, AF.Square),
                   deps=[tM], sig="S_scalar")
        tV = P.op("vector", lambda v: v.scalar_tensor_tensor(
            VT, ps_q.ap, 1.0 / DM, MSQ, ALU.mult, ALU.subtract),
            deps=[t2, tM2], sig="S_vector")
        ps_q.consumed = tV
        tSD = P.op("scalar", lambda s: s.activation(SD, VT, AF.Sqrt, bias=g["cc"](1e-5)),
                   deps=[tV], sig="S_scalar")
        tR = P.op("vector", lambda v: v.reciprocal(R, SD), deps=[tSD], sig="S_vector")
        tMR = P.op("vector", lambda v: v.tensor_tensor(MR, M, R, ALU.mult),
                   deps=[tR], sig="S_vector")
        tk = None
        for c in range(DC):
            xn_c = XN[:, c * T + tb * 512: c * T + (tb + 1) * 512]
            eng = "vector" if c % 2 == 0 else "gpsimd"
            tn1 = P.op(eng, lambda v, o=xn_c: v.tensor_tensor(o, o, R, ALU.mult),
                       deps=[tR, t1, t2, cp_toks[c]], sig="S_" + eng)
            tk = P.op(eng, lambda v, o=xn_c: v.tensor_tensor(o, o, MR, ALU.subtract),
                      deps=[tMR, tn1], sig="S_" + eng)
            if tb == NT - 1 and c >= DC - 2:
                done.append(tk)
        stats_prev = tk
    return done


def load_wslabs(P, g, dram_ap, in_dim, out_dim, deps=()):
    KC = (in_dim + 127) // 128
    views, toks = [], []
    for kc in range(KC):
        kw = min(128, in_dim - kc * 128)
        v = g["BA"].alloc(out_dim)
        tk = P.dma("gpsimd", v[0:kw, :], dram_ap[kc * 128: kc * 128 + kw, :], deps=deps)
        views.append(v)
        toks.append(tk)
    return views, toks


def residual_evac(P, g, oc, t0, n, ps, deps, bias_col=None):
    xt = g["XT"][:, oc * T + t0: oc * T + t0 + n]
    if bias_col is not None:
        tk = P.op("vector", lambda v: v.scalar_tensor_tensor(
            xt, ps.ap[:, 0:n], bias_col, xt, ALU.add, ALU.add),
            deps=deps, sig="S_vector")
    else:
        tk = P.op("vector", lambda v: v.tensor_tensor(xt, xt, ps.ap[:, 0:n], ALU.add),
                  deps=deps, sig="S_vector")
    ps.consumed = tk
    return tk


def bcast_cols(P, g, row_ap, n, deps):
    pb = g["psum"][4]
    t1 = P.op("tensor", lambda t: t.matmul(pb.ap[:, 0:n], g["ones1f_sb"], row_ap,
                                           start=True, stop=True),
              deps=list(deps) + [pb.consumed], sig="S_tensor")
    out = g["FA"].alloc(n, F32)
    t2 = P.op("vector", lambda v: v.tensor_copy(out, pb.ap[:, 0:n]),
              deps=[t1], sig="S_vector")
    pb.consumed = t2
    return out, t2


def emit_L1_ffn(P, g):
    W = g["W"]
    xdeps = emit_layernorm(P, g)
    BA, FA = g["BA"], g["FA"]
    XN = g["XN"]
    psum = g["psum"]

    kslab = [FA.alloc(1024, F32) for _ in range(DC)]
    ktok = [P.dma("sync", kslab[kc], W("L1_ffn_kernel")[kc * 128:(kc + 1) * 128, :])
            for kc in range(DC)]
    CM = g["colmax"]
    cm_toks = []
    for kc in range(DC):
        tk = P.op("vector", lambda v, o=CM[:, kc:kc + 1], i=kslab[kc]:
                  v.tensor_reduce(o, i, AX, ALU.max, apply_absolute_value=True),
                  deps=[ktok[kc]], sig="S_vector")
        cm_toks.append(tk)
    t1 = P.op("vector", lambda v: v.tensor_reduce(g["cmax1"], CM, AX, ALU.max),
              deps=cm_toks, sig="S_vector")
    pt = psum[5]
    t2 = P.op("tensor", lambda t: t.transpose(pt.ap[0:1, 0:128], g["cmax1"], g["eye_sb"]),
              deps=[t1, pt.consumed], sig="S_tensor")
    t3 = P.op("vector", lambda v: v.tensor_reduce(
        g["row2"][0:1, 0:1], pt.ap[0:1, 0:128], AX, ALU.max), deps=[t2], sig="S_vector")
    pt.consumed = t3
    t4 = P.op("vector", lambda v: v.reciprocal(g["row2"][0:1, 1:2], g["row2"][0:1, 0:1]),
              deps=[t3], sig="S_vector")
    t5a = P.op("vector", lambda v: v.tensor_scalar(
        g["row2b"][0:1, 0:1], g["row2"][0:1, 1:2], 7.0, None, ALU.mult),
        deps=[t4], sig="S_vector")
    t5b = P.op("vector", lambda v: v.tensor_scalar(
        g["row2b"][0:1, 1:2], g["row2"][0:1, 0:1], 1.0 / 7.0, None, ALU.mult),
        deps=[t3], sig="S_vector")
    qs, tbc = bcast_cols(P, g, g["row2b"][0:1, 0:2], 2, [t5a, t5b])
    ic, sc = qs[:, 0:1], qs[:, 1:2]
    wq_views, wq_toks = [], []
    for kc in range(DC):
        wv = BA.alloc(1024)
        a = kslab[kc]
        tq1 = P.op("vector", lambda v, a=a: v.tensor_scalar(a, a, ic, MAGIC, ALU.mult, ALU.add),
                   deps=[tbc, cm_toks[kc]], sig="S_vector")
        tq2 = P.op("vector", lambda v, a=a: v.tensor_scalar(a, a, MAGIC, -8.0, ALU.subtract, ALU.max),
                   deps=[tq1], sig="S_vector")
        tq3 = P.op("vector", lambda v, a=a, o=wv: v.tensor_scalar(o, a, 7.0, sc, ALU.min, ALU.mult),
                   deps=[tq2], sig="S_vector")
        wq_views.append(wv)
        wq_toks.append(tq3)

    aslab, atoks = load_wslabs(P, g, W("L1_ffn_a"), DM, 16)
    bslab = BA.alloc(1024)
    btok = P.dma("gpsimd", bslab[0:16, :], W("L1_ffn_b"))
    bias_sb = g["FAcol"].alloc(8, F32)
    btok2 = P.dma("gpsimd", bias_sb,
                  W("L1_ffn_bias").rearrange("one (oc p) -> p (one oc)", p=128),
                  allow_slow_non_contiguous=True)

    XA = BA.alloc(T)
    ps_xa = psum[3]
    xa_toks = []
    for tb in range(NT):
        mms = [(aslab[kc][:, 0:16],
                XN[:, kc * T + tb * 512: kc * T + (tb + 1) * 512],
                ps_xa.ap[0:16, 0:512], atoks[kc]) for kc in range(DC)]
        gt = mmgroup(P, ps_xa, mms, deps=xdeps)
        ev = P.op("vector", lambda v, o=XA[0:16, tb * 512:(tb + 1) * 512]:
                  v.tensor_copy(o, ps_xa.ap[0:16, 0:512]), deps=[gt], sig="S_vector")
        ps_xa.consumed = ev
        xa_toks.append(ev)

    for oc in range(DC):
        for tb in range(NT):
            ps = psum[(oc * NT + tb) % 3]
            mms = [(wq_views[kc][:, oc * 128:(oc + 1) * 128],
                    XN[:, kc * T + tb * 512: kc * T + (tb + 1) * 512],
                    None, wq_toks[kc]) for kc in range(DC)]
            mms.append((bslab[0:16, oc * 128:(oc + 1) * 128],
                        XA[0:16, tb * 512:(tb + 1) * 512], None, btok))
            gt = mmgroup(P, ps, mms, deps=list(xdeps) + [xa_toks[tb], btok2])
            residual_evac(P, g, oc, tb * 512, 512, ps, [gt],
                          bias_col=bias_sb[:, oc:oc + 1])


def gating_topk2(P, g, LG8, lg_tok, GW, GWT, E=8):
    fa = g["FAcol"]
    scr = fa.alloc(E, F32)
    scr2 = fa.alloc(E, F32)
    E8 = fa.alloc(E, F32)
    SEL = fa.alloc(E, F32)
    m1 = fa.alloc(1, F32)
    nm1 = fa.alloc(1, F32)
    m2 = fa.alloc(1, F32)
    zz = fa.alloc(1, F32)
    rz = fa.alloc(1, F32)
    ps_g = g["psum"][5]
    gw_toks = []
    gwt_toks = []
    for tci in range(TC):
        tt = P.op("tensor", lambda t, i=LG8[0:8, tci * 128:(tci + 1) * 128]:
                  t.transpose(ps_g.ap[0:128, 0:8], i, g["eye_sb"][0:8, 0:8]),
                  deps=[lg_tok, ps_g.consumed], sig="S_tensor")
        t0 = P.op("vector", lambda v: v.tensor_copy(scr, ps_g.ap[0:128, 0:8]),
                  deps=[tt], sig="S_vector")
        ps_g.consumed = t0
        tm1 = P.op("vector", lambda v: v.tensor_reduce(m1, scr, AX, ALU.max),
                   deps=[t0], sig="S_vector")
        tn = P.op("vector", lambda v: v.tensor_scalar(nm1, m1, -1.0, None, ALU.mult),
                  deps=[tm1], sig="S_vector")
        te = P.op("scalar", lambda s: s.activation(E8, scr, AF.Exp, bias=nm1),
                  deps=[t0, tn], sig="S_scalar")
        teq = P.op("vector", lambda v: v.tensor_scalar(SEL, scr, m1, None, ALU.is_ge),
                   deps=[tm1], sig="S_vector")
        tl2 = P.op("vector", lambda v: v.scalar_tensor_tensor(
            scr2, SEL, g["cc"](-1e30), scr, ALU.mult, ALU.add),
            deps=[teq], sig="S_vector")
        tm2 = P.op("vector", lambda v: v.tensor_reduce(m2, scr2, AX, ALU.max),
                   deps=[tl2], sig="S_vector")
        tsel = P.op("vector", lambda v: v.tensor_scalar(SEL, scr, m2, None, ALU.is_ge),
                    deps=[tm2], sig="S_vector")
        tes = P.op("vector", lambda v: v.tensor_tensor(E8, E8, SEL, ALU.mult),
                   deps=[te, tsel], sig="S_vector")
        tz = P.op("vector", lambda v: v.tensor_reduce(zz, E8, AX, ALU.add),
                  deps=[tes], sig="S_vector")
        trz = P.op("vector", lambda v: v.reciprocal(rz, zz), deps=[tz], sig="S_vector")
        gwc = GW[:, tci * E:(tci + 1) * E]
        tw = P.op("vector", lambda v, o=gwc: v.tensor_scalar(o, E8, rz, None, ALU.mult),
                  deps=[trz], sig="S_vector")
        gw_toks.append(tw)
        tt2 = P.op("tensor", lambda t, i=gwc: t.transpose(
            ps_g.ap[0:8, 0:128], i, g["eye_sb"]),
            deps=[tw, ps_g.consumed], sig="S_tensor")
        tc2 = P.op("vector", lambda v, o=GWT[0:8, tci * 128:(tci + 1) * 128]:
                   v.tensor_copy(o, ps_g.ap[0:8, 0:128]), deps=[tt2], sig="S_vector")
        ps_g.consumed = tc2
        gwt_toks.append(tc2)
    return gw_toks, gwt_toks


def emit_moe(P, g, pref, HID, router_fn, add_xn, TBW):
    E = 8
    HC = HID // 128
    W, GA, XN, psum = g["W"], g["GA"], g["XN"], g["psum"]
    xdeps = emit_layernorm(P, g)
    xres = g["dram"](pref + "_xres", [DM, T])
    acc = g["dram"](pref + "_acc", [T, DM])
    sp_toks = [P.dma("sync", xres[c * 128:(c + 1) * 128, :], g["XT"][:, c * T:(c + 1) * T],
                     deps=list(xdeps)) for c in range(DC)]
    GW = GA.alloc(TC * E, F32)
    GWT = GA.alloc(T)
    ga_save = GA.ptr
    LG8, lg_tok = router_fn(P, g, xdeps)
    gw_toks, gwt_toks = gating_topk2(P, g, LG8, lg_tok, GW, GWT)
    GA.ptr = ga_save
    gate_done = [gw_toks[-1], gwt_toks[-1]]
    b2slab = GA.alloc(DM)
    b2tok = P.dma("gpsimd", b2slab[0:8, :],
                  W(pref + ("_b2" if pref == "L3_ffn" else "_eb2")),
                  deps=gate_done)
    b1col = g["FAcol"].alloc(HC, F32)
    b1name = pref + ("_b1" if pref == "L3_ffn" else "_eb1")
    w1name = pref + ("_W1" if pref == "L3_ffn" else "_ew1")
    w2name = pref + ("_W2" if pref == "L3_ffn" else "_ew2")
    ga_w2 = GA.ptr
    W2space = GA.alloc(HC * 1024)
    h_tb = GA.alloc(HC * TBW)
    stage = [g["M"], g["VT"]]

    NTB = T // TBW
    ntc2 = TBW // 128
    prev_acc = [None] * (2 * TC)
    prev_h_last = None
    prev_out_last = None
    w1_views = [g["XTspace"][:, kc * HID:(kc + 1) * HID] for kc in range(DC)]
    for e in range(E):
        w1_toks = []
        for kc in range(DC):
            tk = P.dma("gpsimd", w1_views[kc],
                       W(w1name)[e:e + 1, kc * 128:(kc + 1) * 128, :],
                       deps=(sp_toks + list(xdeps) if e == 0 else [prev_h_last]))
            w1_toks.append(tk)
        w2_toks = []
        for hc in range(HC):
            tk = P.dma("gpsimd", W2space[:, hc * 1024:(hc + 1) * 1024],
                       W(w2name)[e:e + 1, hc * 128:(hc + 1) * 128, :],
                       deps=(gate_done if e == 0 else [prev_out_last]))
            w2_toks.append(tk)
        tb1 = P.dma("gpsimd", b1col,
                    W(b1name)[e:e + 1, :].rearrange("one (hc p) -> p (one hc)", p=128),
                    deps=[prev_h_last] if e > 0 else [],
                    allow_slow_non_contiguous=True)
        h_toks = {}
        out_last = None
        for tw in range(NTB):
            for hc in range(HC):
                ps_h = psum[hc % 2]
                mms = [(w1_views[kc][:, hc * 128:(hc + 1) * 128],
                        XN[:, kc * T + tw * TBW: kc * T + (tw + 1) * TBW],
                        None, w1_toks[kc]) for kc in range(DC)]
                gt = mmgroup(P, ps_h, mms, deps=list(xdeps), sub=ps_h.ap[:, 0:TBW])
                hv = h_tb[:, hc * TBW:(hc + 1) * TBW]
                ev = P.op("scalar", lambda s, o=hv, i=ps_h.ap[:, 0:TBW], b=b1col[:, hc:hc + 1]:
                          s.activation(o, i, AF.Silu, bias=b), deps=[gt, tb1, out_last],
                          sig="S_scalar")
                ps_h.consumed = ev
                h_toks[hc] = ev
                prev_h_last = gt
            for tc2 in range(ntc2):
                tci = tw * ntc2 + tc2
                for half in range(2):
                    ps_o = psum[2 + half]
                    mms = [(h_tb[:, hc * TBW + tc2 * 128: hc * TBW + (tc2 + 1) * 128],
                            W2space[:, hc * 1024 + half * 512: hc * 1024 + (half + 1) * 512],
                            None, w2_toks[hc]) for hc in range(HC)]
                    gt = mmgroup(P, ps_o, mms,
                                 deps=[h_toks[hc] for hc in range(HC)][-1:] + [h_toks[0]])
                    out_last = gt
                    wcol = GW[:, tci * E + e: tci * E + e + 1]
                    ev = P.op("vector", lambda v, o=stage[half], i=ps_o.ap, w=wcol:
                              v.tensor_scalar(o, i, w, None, ALU.mult),
                              deps=[gt, gw_toks[tci],
                                    prev_acc[2 * tci + half]], sig="S_vector")
                    dk = P.dma("gpsimd",
                               acc[tci * 128:(tci + 1) * 128, half * 512:(half + 1) * 512],
                               stage[half],
                               deps=[ev] + ([prev_acc[2 * tci + half]] if prev_acc[2 * tci + half] else []),
                               **({} if e == 0 else {"accum_op": ALU.add}))
                    prev_acc[2 * tci + half] = dk
                    ps_o.consumed = dk
        prev_out_last = out_last
    rs_toks = [P.dma("sync", g["XT"][:, c * T:(c + 1) * T], xres[c * 128:(c + 1) * 128, :],
                     deps=[prev_h_last]) for c in range(DC)]
    GA.ptr = ga_w2
    ACCS = [GA.alloc(DM, F32), GA.alloc(DM, F32)]
    acc_ld = [None, None]
    for tci in range(TC):
        buf = ACCS[tci % 2]
        lt = P.dma("sync", buf, acc[tci * 128:(tci + 1) * 128, :],
                   deps=[prev_acc[2 * tci], prev_acc[2 * tci + 1], acc_ld[tci % 2],
                         prev_out_last])
        for fc in range(DC):
            ps_f = psum[4 + (fc % 2)]
            t1 = P.op("tensor", lambda t, i=buf[:, fc * 128:(fc + 1) * 128],
                      pf=ps_f.ap[:, 0:128]:
                      t.transpose(pf, i, g["eye_sb"]),
                      deps=[lt, ps_f.consumed], sig="S_tensor")
            t2 = P.op("tensor", lambda t, l=b2slab[0:8, fc * 128:(fc + 1) * 128],
                      r=GWT[0:8, tci * 128:(tci + 1) * 128], pf=ps_f.ap[:, 0:128]:
                      t.matmul(pf, l, r, start=False, stop=True,
                               skip_group_check=True),
                      deps=[t1, b2tok, gwt_toks[tci]], sig="S_tensor")
            xt = g["XT"][:, fc * T + tci * 128: fc * T + (tci + 1) * 128]
            t3 = P.op("vector", lambda v, o=xt, pf=ps_f.ap[:, 0:128]:
                      v.tensor_tensor(o, o, pf, ALU.add),
                      deps=[t2, rs_toks[fc]], sig="S_vector")
            ps_f.consumed = t3
            if add_xn:
                xnv = XN[:, fc * T + tci * 128: fc * T + (tci + 1) * 128]
                t3 = P.op("vector", lambda v, o=xt, i=xnv: v.tensor_tensor(o, o, i, ALU.add),
                          deps=[t3], sig="S_vector")
            acc_ld[tci % 2] = t3 if fc == DC - 1 else acc_ld[tci % 2]


def router_L3(P, g, xdeps):
    W, GA, XN, psum = g["W"], g["GA"], g["XN"], g["psum"]
    slabs, toks = load_wslabs(P, g, W("L3_ffn_gate_w"), DM, 8)
    gbcol = g["FAcol"].alloc(1, F32)
    gbt = P.dma("gpsimd", gbcol[0:8, :],
                W("L3_ffn_gate_b").rearrange("one e -> e one"),
                allow_slow_non_contiguous=True)
    LG8 = GA.alloc(T, F32)
    ps = psum[4]
    tk = None
    for tb in range(NT):
        mms = [(slabs[kc], XN[:, kc * T + tb * 512: kc * T + (tb + 1) * 512],
                ps.ap[0:8, 0:512], toks[kc]) for kc in range(DC)]
        gt = mmgroup(P, ps, mms, deps=list(xdeps))
        tk = P.op("vector", lambda v, o=LG8[0:8, tb * 512:(tb + 1) * 512]:
                  v.tensor_scalar(o, ps.ap[0:8, 0:512], gbcol[0:8, :], None, ALU.add),
                  deps=[gt, gbt], sig="S_vector")
        ps.consumed = tk
    return LG8, tk


def router_L0(P, g, xdeps):
    W, GA, XN, psum = g["W"], g["GA"], g["XN"], g["psum"]
    r1slabs, r1toks = load_wslabs(P, g, W("L0_ffn_rw1"), DM, DM)
    rb1col = g["FAcol"].alloc(DC, F32)
    t_rb1 = P.dma("gpsimd", rb1col,
                  W("L0_ffn_rb1").rearrange("one (oc p) -> p (one oc)", p=128),
                  allow_slow_non_contiguous=True)
    R1 = GA.alloc(DC * T)
    r1_done = []
    for oc in range(DC):
        for tb in range(NT):
            ps = psum[2 + (tb % 2)]
            mms = [(r1slabs[kc][:, oc * 128:(oc + 1) * 128],
                    XN[:, kc * T + tb * 512: kc * T + (tb + 1) * 512],
                    None, r1toks[kc]) for kc in range(DC)]
            gt = mmgroup(P, ps, mms, deps=list(xdeps))
            ev = P.op("scalar", lambda s, o=R1[:, oc * T + tb * 512: oc * T + (tb + 1) * 512],
                      i=ps.ap, b=rb1col[:, oc:oc + 1]:
                      s.activation(o, i, AF.Silu, bias=b), deps=[gt, t_rb1], sig="S_scalar")
            ps.consumed = ev
            r1_done.append(ev)
    r2slabs, r2toks = load_wslabs(P, g, W("L0_ffn_rw2"), DM, 8)
    rb2col = g["FAcol"].alloc(1, F32)
    t_rb2 = P.dma("gpsimd", rb2col[0:8, :],
                  W("L0_ffn_rb2").rearrange("one e -> e one"),
                  allow_slow_non_contiguous=True)
    LG8 = GA.alloc(T, F32)
    ps = psum[4]
    tk = None
    for tb in range(NT):
        mms = [(r2slabs[kc], R1[:, kc * T + tb * 512: kc * T + (tb + 1) * 512],
                ps.ap[0:8, 0:512], r2toks[kc]) for kc in range(DC)]
        gt = mmgroup(P, ps, mms, deps=r1_done[-1:] + [r1_done[tb]])
        tk = P.op("vector", lambda v, o=LG8[0:8, tb * 512:(tb + 1) * 512]:
                  v.tensor_scalar(o, ps.ap[0:8, 0:512], rb2col[0:8, :], None, ALU.add),
                  deps=[gt, t_rb2], sig="S_vector")
        ps.consumed = tk
    return LG8, tk


def transpose_weight(P, g, dram_ap, n_rows, n_cols, deps=()):
    GA, psum = g["GA"], g["psum"]
    RC, CC = n_rows // 128, n_cols // 128
    out = [GA.alloc(n_rows) for _ in range(CC)]
    stg = [GA.alloc(128, F32), GA.alloc(128, F32)]
    stok = [None, None]
    ps = psum[5]
    k = 0
    for dc in range(CC):
        for rc in range(RC):
            sb = stg[k % 2]
            lt = P.dma("sync", sb, dram_ap[rc * 128:(rc + 1) * 128, dc * 128:(dc + 1) * 128],
                       deps=[stok[k % 2]] + list(deps))
            off = (rc % 4) * 128
            tt = P.op("tensor", lambda t, i=sb, pf=ps.ap[:, off:off + 128]:
                      t.transpose(pf, i, g["eye_sb"]),
                      deps=[lt, ps.consumed if rc % 4 == 0 else None], sig="S_tensor")
            stok[k % 2] = tt
            k += 1
            if rc % 4 == 3 or rc == RC - 1:
                lo = (rc // 4) * 4 * 128
                w = (rc % 4 + 1) * 128
                ev = P.op("vector", lambda v, o=out[dc][:, lo:lo + w],
                          pf=ps.ap[:, 0:w]: v.tensor_copy(o, pf),
                          deps=[tt], sig="S_vector")
                ps.consumed = ev
    return out, ps.consumed


def elu1_evac(P, g, dst, ps, n, deps, scr):
    prev = g.get("_elu_prev")
    t1 = P.op("vector", lambda v, o=scr[0][:, 0:n], i=ps.ap[:, 0:n]:
              v.tensor_scalar(o, i, 0.0, None, ALU.min),
              deps=list(deps) + [prev], sig="S_vector")
    t2 = P.op("scalar", lambda s, o=scr[0][:, 0:n]: s.activation(o, o, AF.Exp),
              deps=[t1, prev], sig="S_scalar")
    t3 = P.op("vector", lambda v, o=scr[1][:, 0:n], i=ps.ap[:, 0:n]:
              v.tensor_scalar(o, i, 0.0, None, ALU.max), deps=deps, sig="S_vector")
    t4 = P.op("vector", lambda v, o=dst, a=scr[0][:, 0:n], b=scr[1][:, 0:n]:
              v.tensor_tensor(o, a, b, ALU.add), deps=[t2, t3], sig="S_vector")
    ps.consumed = t4
    g["_elu_prev"] = t4
    return t4


def emit_L2_attn(P, g):
    W, GA, XN, psum = g["W"], g["GA"], g["XN"], g["psum"]
    H, HD = 16, 64
    xdeps = emit_layernorm(P, g)
    xres = g["dram"]("l2a_xres", [DM, T])
    sp_toks = [P.dma("sync", xres[c * 128:(c + 1) * 128, :], g["XT"][:, c * T:(c + 1) * T],
                     deps=list(xdeps)) for c in range(DC)]
    KTOK = g["XTspace"][:, 0:TC * 1024]
    VTOK = g["XTspace"][:, TC * 1024:2 * TC * 1024]
    scr = [g["SQ0"], g["SQ1"]]
    OUTC = GA.alloc(DC * T)
    ga_post_outc = GA.ptr
    QF = GA.alloc(DC * T)
    ga_save = GA.ptr

    def proj_tokmajor(wname, dst, elu, first_deps):
        WT, wt_done = transpose_weight(P, g, W(wname), DM, DM, deps=first_deps)
        toks = []
        for tc in range(TC):
            for fb in range(2):
                ps = psum[fb]
                mms = [(XN[:, kc * T + tc * 128: kc * T + (tc + 1) * 128],
                        WT[kc][:, fb * 512:(fb + 1) * 512], None, wt_done)
                       for kc in range(DC)]
                gt = mmgroup(P, ps, mms, deps=list(xdeps) + (sp_toks if wname == "L2_attn_Wk" else []))
                dv = dst[:, tc * 1024 + fb * 512: tc * 1024 + (fb + 1) * 512]
                if elu:
                    tk = elu1_evac(P, g, dv, ps, 512, [gt], scr)
                else:
                    tk = P.op("vector", lambda v, o=dv, i=ps.ap: v.tensor_copy(o, i),
                              deps=[gt], sig="S_vector")
                    ps.consumed = tk
                toks.append(tk)
        return toks

    k_toks = proj_tokmajor("L2_attn_Wk", KTOK, True, [])
    GA.ptr = ga_save
    v_toks = proj_tokmajor("L2_attn_Wv", VTOK, False, [k_toks[-1]])
    GA.ptr = ga_save
    WTq, wtq_done = transpose_weight(P, g, W("L2_attn_Wq"), DM, DM, deps=[v_toks[-1]])
    q_toks = []
    for oc in range(DC):
        for tb in range(NT):
            ps = psum[2 + ((oc * NT + tb) % 2)]
            mms = [(WTq[kc][:, oc * 128:(oc + 1) * 128],
                    XN[:, kc * T + tb * 512: kc * T + (tb + 1) * 512],
                    None, wtq_done) for kc in range(DC)]
            gt = mmgroup(P, ps, mms, deps=list(xdeps))
            dv = QF[:, oc * T + tb * 512: oc * T + (tb + 1) * 512]
            q_toks.append(elu1_evac(P, g, dv, ps, 512, [gt], scr))
    GA.ptr = ga_save

    KVBD = GA.alloc(DC * 128)
    kdep = [k_toks[-1], v_toks[-1]]
    kv_toks = []
    zt = P.op("vector", lambda v: v.memset(KVBD, 0.0), deps=[], sig="S_vector")
    for c in range(DC):
        toks2 = []
        banks = [psum[c % 2], psum[2 + c % 2]]
        for par in range(2):
            h = 2 * c + par
            ps = banks[par]
            mms = [(KTOK[:, tc * 1024 + c * 128: tc * 1024 + (c + 1) * 128],
                    VTOK[:, tc * 1024 + h * 64: tc * 1024 + (h + 1) * 64],
                    ps.ap[:, 0:64], None) for tc in range(TC)]
            toks2.append(mmgroup(P, ps, mms, deps=kdep))
        c1 = P.op("vector", lambda v, o=KVBD[0:64, c * 128: c * 128 + 64],
                  pf=banks[0].ap[0:64, 0:64]: v.tensor_copy(o, pf),
                  deps=[toks2[0], zt], sig="S_vector")
        c2 = P.op("vector", lambda v, o=KVBD[64:128, c * 128 + 64: c * 128 + 128],
                  pf=banks[1].ap[64:128, 0:64]: v.tensor_copy(o, pf),
                  deps=[toks2[1]], sig="S_vector")
        banks[0].consumed = c1
        banks[1].consumed = c2
        kv_toks.append(c2)

    KS = GA.alloc(1024, F32)
    ks_toks = []
    for fb in range(2):
        ps = psum[2 + fb]
        mms = [(g["ones_sb"], KTOK[:, tc * 1024 + fb * 512: tc * 1024 + (fb + 1) * 512],
                None, None) for tc in range(TC)]
        gt = mmgroup(P, ps, mms, deps=kdep)
        ev = P.op("vector", lambda v, o=KS[:, fb * 512:(fb + 1) * 512], i=ps.ap:
                  v.tensor_copy(o, i), deps=[gt], sig="S_vector")
        ps.consumed = ev
        ks_toks.append(ev)
    DKL = GA.alloc(16 * DC)
    tz = P.op("vector", lambda v: v.memset(DKL, 0.0), deps=[], sig="S_vector")
    dkl_toks = []
    ps5 = psum[5]
    for c in range(DC):
        tt = P.op("tensor", lambda t, i=KS[:, c * 128:(c + 1) * 128],
                  pf=ps5.ap[:, 0:128]: t.transpose(pf, i, g["eye_sb"]),
                  deps=[ks_toks[-1], ps5.consumed], sig="S_tensor")
        h0, h1 = 2 * c, 2 * c + 1
        c1 = P.op("vector", lambda v, o=DKL[0:64, c * 16 + h0: c * 16 + h0 + 1],
                  pf=ps5.ap[0:64, 0:1]: v.tensor_copy(o, pf), deps=[tt, tz], sig="S_vector")
        c2 = P.op("vector", lambda v, o=DKL[64:128, c * 16 + h1: c * 16 + h1 + 1],
                  pf=ps5.ap[64:128, 0:1]: v.tensor_copy(o, pf), deps=[tt], sig="S_vector")
        ps5.consumed = c2
        dkl_toks.append(c2)
    DRCPB = GA.alloc(T)
    den_toks = []
    for tb in range(NT):
        ps = psum[4]
        mms = [(DKL[:, c * 16:(c + 1) * 16],
                QF[:, c * T + tb * 512: c * T + (tb + 1) * 512],
                ps.ap[0:16, 0:512], dkl_toks[c]) for c in range(DC)]
        gt = mmgroup(P, ps, mms, deps=[q_toks[-1]])
        t1 = P.op("vector", lambda v, pf=ps.ap[0:16, 0:512]:
                  v.tensor_scalar(pf, pf, 1e-6, None, ALU.add), deps=[gt], sig="S_vector")
        t2 = P.op("vector", lambda v, o=g["M"][0:16, 0:512], pf=ps.ap[0:16, 0:512]:
                  v.reciprocal(o, pf), deps=[t1], sig="S_vector")
        t3 = P.op("vector", lambda v, o=DRCPB[0:16, tb * 512:(tb + 1) * 512],
                  i=g["M"][0:16, 0:512]: v.tensor_copy(o, i), deps=[t2], sig="S_vector")
        ps.consumed = t3
        den_toks.append(t3)
    HM = g["hmask_sb"]
    RB = GA.alloc(512, F32)
    num_toks = []
    prev_ev = None
    for c in range(DC):
        for tb in range(NT):
            ps_rb = psum[4]
            tb1 = P.op("tensor", lambda t, l=HM[0:16, c * 128:(c + 1) * 128],
                       r=DRCPB[0:16, tb * 512:(tb + 1) * 512], pf=ps_rb.ap:
                       t.matmul(pf, l, r, start=True, stop=True),
                       deps=[den_toks[tb], ps_rb.consumed], sig="S_tensor")
            tb2 = P.op("vector", lambda v, o=RB, pf=ps_rb.ap: v.tensor_copy(o, pf),
                       deps=[tb1, prev_ev], sig="S_vector")
            ps_rb.consumed = tb2
            ps_n = psum[(c * NT + tb) % 2]
            t2 = P.op("tensor", lambda t, ll=KVBD[:, c * 128:(c + 1) * 128],
                      rr=QF[:, c * T + tb * 512: c * T + (tb + 1) * 512],
                      pf=ps_n.ap: t.matmul(pf, ll, rr, start=True, stop=True),
                      deps=[kv_toks[c], q_toks[-1], ps_n.consumed], sig="S_tensor")
            ev = P.op("vector", lambda v, o=OUTC[:, c * T + tb * 512: c * T + (tb + 1) * 512],
                      a=RB, pf=ps_n.ap: v.tensor_tensor(o, pf, a, ALU.mult),
                      deps=[t2, tb2], sig="S_vector")
            ps_n.consumed = ev
            prev_ev = ev
            num_toks.append(ev)
    rs_toks = [P.dma("sync", g["XT"][:, c * T:(c + 1) * T], xres[c * 128:(c + 1) * 128, :],
                     deps=[num_toks[-1]]) for c in range(DC)]
    GA.ptr = ga_post_outc
    WoT, wo_done = transpose_weight(P, g, W("L2_attn_Wo"), DM, DM,
                                    deps=[num_toks[-1]])
    for oc in range(DC):
        for tb in range(NT):
            ps = psum[2 + ((oc * NT + tb) % 2)]
            mms = [(WoT[kc][:, oc * 128:(oc + 1) * 128],
                    OUTC[:, kc * T + tb * 512: kc * T + (tb + 1) * 512],
                    None, wo_done) for kc in range(DC)]
            gt = mmgroup(P, ps, mms, deps=[num_toks[-1]])
            residual_evac(P, g, oc, tb * 512, 512, ps, [gt, rs_toks[oc]])


def emit_L3_attn(P, g):
    W, GA, XN, psum = g["W"], g["GA"], g["XN"], g["psum"]
    H = 16
    xdeps = emit_layernorm(P, g)
    xres = g["dram"]("l3a_xres", [DM, T])
    ydel = g["dram"]("l3a_delta", [DM, T], BF16)
    sp_toks = [P.dma("sync", xres[c * 128:(c + 1) * 128, :], g["XT"][:, c * T:(c + 1) * T],
                     deps=list(xdeps)) for c in range(DC)]
    KCOM = g["XTspace"]
    VTOK = g["XNspace"]
    kz = P.op("vector", lambda v: v.memset(KCOM, 0.0), deps=sp_toks + list(xdeps),
              sig="S_vector")
    HT = T // 2
    g["cos4_sb"] = GA.alloc(T)
    g["sin4_sb"] = GA.alloc(T)
    g["_cs_tok"] = P.dma("sync", g["cos4_sb"], g["c_cos4"])
    g["_cs_tok2"] = P.dma("sync", g["sin4_sb"], g["c_sin4"])
    QCOM = GA.alloc(H * HT)
    qz = P.op("gpsimd", lambda v: v.memset(QCOM, 0.0), deps=[], sig="S_gpsimd")
    ga0 = GA.ptr
    QC = GA.alloc(4 * T)
    ga_qc = GA.ptr
    KVN = GA.alloc(2 * T)
    ga1 = GA.ptr
    wkv, wkv_t = load_wslabs(P, g, W("L3_attn_W_kv"), DM, 256)
    gcol = g["FAcol"].alloc(2, F32)
    gct = P.dma("gpsimd", gcol, W("L3_attn_g_kv").rearrange("one (c p) -> p (one c)", p=128),
                allow_slow_non_contiguous=True)
    M, VT, SD, R = g["M"], g["VT"], g["SD"], g["R"]
    SQ = [g["SQ0"], g["SQ1"]]
    for tb in range(NT):
        sq_mm = None
        raw_toks = []
        for oc in range(2):
            ps = psum[oc]
            kvslot = KVN[:, oc * T + tb * 512: oc * T + (tb + 1) * 512]
            mms = [(wkv[kc][:, oc * 128:(oc + 1) * 128],
                    XN[:, kc * T + tb * 512: kc * T + (tb + 1) * 512],
                    None, wkv_t[kc]) for kc in range(DC)]
            gt = mmgroup(P, ps, mms, deps=list(xdeps))
            t1 = P.op("vector", lambda v, o=SQ[oc], pf=ps.ap: v.tensor_copy(o, pf),
                      deps=[gt, g.get("_kvn_prev")], sig="S_vector")
            ps.consumed = t1
            raw_toks.append(t1)
            t2 = P.op("scalar", lambda s, o=kvslot, i=SQ[oc]: s.activation(o, i, AF.Square),
                      deps=[t1], sig="S_scalar")
            sq_mm = P.op("tensor", lambda t, i=kvslot, st=(oc == 0), sp=(oc == 1):
                         t.matmul(psum[2].ap, g["ones_sb"], i, start=st, stop=sp),
                         deps=[t2] + ([psum[2].consumed] if oc == 0 else []),
                         sig="S_tensor")
        t4 = P.op("vector", lambda v: v.tensor_scalar(VT, psum[2].ap, 1.0 / 256, None, ALU.mult),
                  deps=[sq_mm], sig="S_vector")
        psum[2].consumed = t4
        t5 = P.op("scalar", lambda s: s.activation(SD, VT, AF.Sqrt, bias=g["cc"](1e-6)),
                  deps=[t4], sig="S_scalar")
        t6 = P.op("vector", lambda v: v.reciprocal(R, SD), deps=[t5], sig="S_vector")
        for oc in range(2):
            kvslot = KVN[:, oc * T + tb * 512: oc * T + (tb + 1) * 512]
            tk = P.op("vector", lambda v, o=kvslot, i=SQ[oc], gc=gcol[:, oc:oc + 1]:
                      v.scalar_tensor_tensor(o, i, gc, R, ALU.mult, ALU.mult),
                      deps=[t6, gct, sq_mm], sig="S_vector")
            g["_kvn_prev"] = tk
    kvn_done = g["_kvn_prev"]
    GA.ptr = ga1
    wq, wq_t = load_wslabs(P, g, W("L3_attn_W_q"), DM, 512, deps=[kvn_done])
    qc_done = None
    for oc in range(4):
        for tb in range(NT):
            ps = psum[3 + (oc % 2)]
            mms = [(wq[kc][:, oc * 128:(oc + 1) * 128],
                    XN[:, kc * T + tb * 512: kc * T + (tb + 1) * 512],
                    None, wq_t[kc]) for kc in range(DC)]
            gt = mmgroup(P, ps, mms, deps=list(xdeps))
            qc_done = P.op("vector", lambda v, o=QC[:, oc * T + tb * 512: oc * T + (tb + 1) * 512],
                           pf=ps.ap: v.tensor_copy(o, pf), deps=[gt], sig="S_vector")
            ps.consumed = qc_done
    GA.ptr = ga1
    KR = GA.alloc(4 * T)
    wkr, wkr_t = load_wslabs(P, g, W("L3_attn_W_krope"), DM, 512, deps=[qc_done])
    kr_done = None
    for oc in range(4):
        for tb in range(NT):
            ps = psum[3 + (oc % 2)]
            mms = [(wkr[kc][:, oc * 128:(oc + 1) * 128],
                    XN[:, kc * T + tb * 512: kc * T + (tb + 1) * 512],
                    None, wkr_t[kc]) for kc in range(DC)]
            gt = mmgroup(P, ps, mms, deps=list(xdeps))
            kr_done = P.op("vector", lambda v, o=KR[:, oc * T + tb * 512: oc * T + (tb + 1) * 512],
                           pf=ps.ap: v.tensor_copy(o, pf), deps=[gt], sig="S_vector")
            ps.consumed = kr_done
    kr_rope = emit_rope(P, g, KR, 4 * NT, None, kr_done)
    sc_done = []
    for h in range(H):
        src = KR[:, (h // 4) * T: (h // 4 + 1) * T][(h % 4) * 32:(h % 4) * 32 + 32, :]
        dst = KCOM[:, h * T:(h + 1) * T][64:96, :]
        sc_done.append(P.op("vector", lambda v, o=dst, i=src: v.tensor_copy(o, i),
                            deps=[kr_rope, kz], sig="S_vector"))
    GA.ptr = ga1
    wkd, wkd_t = load_wslabs(P, g, W("L3_attn_W_kdec"), 256, 1024,
                             deps=[sc_done[-1]])
    for oc in range(DC):
        for tb in range(NT):
            ps = psum[3 + (oc % 2)]
            mms = [(wkd[kc][:, oc * 128:(oc + 1) * 128],
                    KVN[:, kc * T + tb * 512: kc * T + (tb + 1) * 512],
                    None, wkd_t[kc]) for kc in range(2)]
            gt = mmgroup(P, ps, mms, deps=[kvn_done])
            for par in range(2):
                h = 2 * oc + par
                dst = KCOM[:, h * T:(h + 1) * T][0:64, tb * 512:(tb + 1) * 512]
                tk = P.op("vector", lambda v, o=dst, pf=ps.ap[par * 64:(par + 1) * 64, :]:
                          v.tensor_copy(o, pf), deps=[gt, kz], sig="S_vector")
            ps.consumed = tk
            sc_done.append(tk)
    wvd, wvd_t = load_wslabs(P, g, W("L3_attn_W_vdec"), 256, 1024,
                             deps=[sc_done[-1]])
    v_done = []
    for tc in range(TC):
        for fb in range(2):
            ps = psum[3 + fb]
            mms = [(KVN[:, kc * T + tc * 128: kc * T + (tc + 1) * 128],
                    wvd[kc][:, fb * 512:(fb + 1) * 512], None, wvd_t[kc])
                   for kc in range(2)]
            gt = mmgroup(P, ps, mms, deps=[kvn_done])
            tk = P.op("vector", lambda v, o=VTOK[:, tc * 1024 + fb * 512: tc * 1024 + (fb + 1) * 512],
                      pf=ps.ap: v.tensor_copy(o, pf), deps=[gt], sig="S_vector")
            ps.consumed = tk
            v_done.append(tk)
    GA.ptr = ga_qc
    QSCALE = 1.0 / np.sqrt(96.0)
    stg = GA.alloc(512)
    EXPB = GA.alloc(T)
    ATB = GA.alloc(T)
    YH = GA.alloc(DM)
    cols = GA.alloc(16, F32)
    g["l3racc"] = g["FAcol"].alloc(8, F32)
    ga2 = GA.ptr
    for half in range(2):
        hsl = slice(half * HT, (half + 1) * HT)
        GA.ptr = ga2
        wo_prev = [g.get("_wo_last"), v_done[-1], sc_done[-1]]
        wqd, wqd_t = load_wslabs(P, g, W("L3_attn_W_qdec"), 512, 1024, deps=wo_prev)
        wqr, wqr_t = load_wslabs(P, g, W("L3_attn_W_qrope"), 512, 512, deps=wo_prev)
        QR = GA.alloc(2 * T)
        qcd = []
        for oc in range(DC):
            for tb in range(2):
                ps = psum[3 + (tb % 2)]
                mms = [(wqd[kc][:, oc * 128:(oc + 1) * 128],
                        QC[:, kc * T + half * HT + tb * 512: kc * T + half * HT + (tb + 1) * 512],
                        None, wqd_t[kc]) for kc in range(4)]
                gt = mmgroup(P, ps, mms, deps=[qc_done])
                for par in range(2):
                    h = 2 * oc + par
                    dst = QCOM[:, h * HT:(h + 1) * HT][0:64, tb * 512:(tb + 1) * 512]
                    tk = P.op("vector", lambda v, o=dst, pf=ps.ap[par * 64:(par + 1) * 64, :]:
                              v.tensor_scalar(o, pf, QSCALE, None, ALU.mult),
                              deps=[gt, qz], sig="S_vector")
                ps.consumed = tk
                qcd.append(tk)
        qr_d = None
        for oc in range(4):
            for tb in range(2):
                ps = psum[3 + (tb % 2)]
                mms = [(wqr[kc][:, oc * 128:(oc + 1) * 128],
                        QC[:, kc * T + half * HT + tb * 512: kc * T + half * HT + (tb + 1) * 512],
                        None, wqr_t[kc]) for kc in range(4)]
                gt = mmgroup(P, ps, mms, deps=[qc_done])
                qr_d = P.op("vector", lambda v, o=QR[:, oc * HT + tb * 512: oc * HT + (tb + 1) * 512],
                            pf=ps.ap: v.tensor_copy(o, pf), deps=[gt], sig="S_vector")
                ps.consumed = qr_d
        qr_rope = emit_rope(P, g, QR, 4 * 2, half * HT, qr_d)
        qs_done = []
        for h in range(H):
            src = QR[:, (h // 4) * HT: (h // 4 + 1) * HT][(h % 4) * 32:(h % 4) * 32 + 32, :]
            dst = QCOM[:, h * HT:(h + 1) * HT][64:96, :]
            qs_done.append(P.op("vector", lambda v, o=dst, i=src:
                                v.tensor_scalar(o, i, QSCALE, None, ALU.mult),
                                deps=[qr_rope, qz], sig="S_vector"))
        qready = [qcd[-1], qs_done[-1]]
        GA.ptr = ga2
        wo, wo_t = load_wslabs(P, g, W("L3_attn_W_o"), DM, DM, deps=qready)
        EXPB2 = [EXPB, g["scr2k"]]
        itersA = [(tcl, h) for tcl in range(TC // 2) for h in range(H)]
        stA = {}

        def frontA(k):
            tcl, h = itersA[k]
            pp = k % 2
            tci = half * (TC // 2) + tcl
            s_end = (tci + 1) * 128
            NB = (s_end + 511) // 512
            bank_toks = []
            for bb in range(NB):
                w = min(512, s_end - bb * 512)
                ps = psum[bb % 4]
                mms = [(QCOM[:, h * HT + tcl * 128: h * HT + (tcl + 1) * 128],
                        KCOM[:, h * T + bb * 512: h * T + bb * 512 + w],
                        ps.ap[:, 0:w], None)]
                gt = mmgroup(P, ps, mms, deps=qready + [sc_done[-1], kz])
                if bb == NB - 1:
                    off = tci * 128 - bb * 512
                    gt = P.op("vector", lambda v, pf=ps.ap[:, off:off + 128]:
                              v.tensor_tensor(pf, pf, g["maskf_sb"], ALU.add),
                              deps=[gt], sig="S_vector")
                bank_toks.append((ps, gt, w))
            mx = []
            for bb, (ps, gt, w) in enumerate(bank_toks):
                tk = P.op("vector", lambda v, o=cols[:, pp * 8 + bb:pp * 8 + bb + 1],
                          pf=ps.ap[:, 0:w]:
                          v.tensor_reduce(o, pf, AX, ALU.max), deps=[gt], sig="S_vector")
                mx.append(tk)
            tm = P.op("vector", lambda v, o=cols[:, pp * 8 + 4:pp * 8 + 5],
                      i=cols[:, pp * 8:pp * 8 + NB]:
                      v.tensor_reduce(o, i, AX, ALU.max), deps=mx, sig="S_vector")
            tnm = P.op("vector", lambda v, o=cols[:, pp * 8 + 5:pp * 8 + 6],
                       i=cols[:, pp * 8 + 4:pp * 8 + 5]:
                       v.tensor_scalar(o, i, -1.0, None, ALU.mult), deps=[tm], sig="S_vector")
            stA[k] = (bank_toks, tnm)

        def middleA(k):
            tcl, h = itersA[k]
            pp = k % 2
            tci = half * (TC // 2) + tcl
            s_end = (tci + 1) * 128
            NB = (s_end + 511) // 512
            EXPB_ = EXPB2[pp]
            bank_toks, tnm = stA.pop(k)
            rs = []
            for bb, (ps, gt, w) in enumerate(bank_toks):
                tk = P.op("scalar", lambda s, o=EXPB_[:, bb * 512: bb * 512 + w],
                          pf=ps.ap[:, 0:w], b=cols[:, pp * 8 + 5:pp * 8 + 6],
                          a=g["l3racc"][:, pp * 4 + bb:pp * 4 + bb + 1]:
                          s.activation(o, pf, AF.Exp, bias=b, accum_out=a),
                          deps=[tnm, gt, g.get("_l3_eb%d" % pp)], sig="S_scalar")
                ps.consumed = tk
                rs.append(tk)
            tz2 = P.op("vector", lambda v, o=cols[:, pp * 8 + 6:pp * 8 + 7],
                       i=g["l3racc"][:, pp * 4:pp * 4 + NB]:
                       v.tensor_reduce(o, i, AX, ALU.add), deps=rs, sig="S_vector")
            trr = P.op("vector", lambda v, o=cols[:, pp * 8 + 7:pp * 8 + 8],
                       i=cols[:, pp * 8 + 6:pp * 8 + 7]:
                       v.reciprocal(o, i), deps=[tz2], sig="S_vector")
            tn = P.op("vector", lambda v, o=EXPB_[:, 0:s_end],
                      rr=cols[:, pp * 8 + 7:pp * 8 + 8]:
                      v.tensor_scalar(o, o, rr, None, ALU.mult), deps=[trr] + rs,
                      sig="S_vector")
            stA[("n", k)] = tn

        def backA(k):
            tcl, h = itersA[k]
            pp = k % 2
            tci = half * (TC // 2) + tcl
            s_end = (tci + 1) * 128
            EXPB_ = EXPB2[pp]
            tn = stA.pop(("n", k))
            ps_t = psum[4]
            ps_y = psum[5]
            nsc = (s_end + 127) // 128
            ev = None
            for scb in range(0, nsc, 4):
                nn = min(4, nsc - scb)
                tts = []
                for j in range(nn):
                    tt = P.op("tensor", lambda t, i=EXPB_[:, (scb + j) * 128:(scb + j + 1) * 128],
                              pf=ps_t.ap.bitcast(BF16)[:, j * 128:(j + 1) * 128]:
                              t.transpose(pf, i, g["eyeb_sb"]),
                              deps=[tn, ps_t.consumed if j == 0 else None], sig="S_tensor")
                    tts.append(tt)
                ev = P.op("scalar", lambda se, o=ATB[:, scb * 128: scb * 128 + nn * 128],
                          pf=ps_t.ap.bitcast(BF16)[:, 0:nn * 128]:
                          se.activation(o, pf, AF.Copy),
                          deps=[tts[-1]], sig="S_scalar")
                ps_t.consumed = ev
            g["_l3_eb%d" % pp] = ev
            mms = [(VTOK[:, sc * 1024 + h * 64: sc * 1024 + (h + 1) * 64],
                    ATB[:, sc * 128:(sc + 1) * 128],
                    ps_y.ap[0:64, 0:128], ev) for sc in range(nsc)]
            gt = mmgroup(P, ps_y, mms, deps=[ev, v_done[-1]])
            yev = P.op("vector", lambda v, o=YH[(h % 2) * 64:(h % 2) * 64 + 64,
                                                (h // 2) * 128:(h // 2) * 128 + 128],
                       pf=ps_y.ap[0:64, 0:128]: v.tensor_copy(o, pf),
                       deps=[gt, g.get("_l3_yh")], sig="S_vector")
            ps_y.consumed = yev
            g["_yh%d" % h] = yev
            if h == H - 1:
                wo_emit(tci, YH)

        def wo_emit(tci, YH):
            ydeps = [g["_yh%d" % hh] for hh in range(H)]
            for oc in range(DC):
                ps = psum[6]
                mms2 = [(wo[kc][:, oc * 128:(oc + 1) * 128],
                         YH[:, kc * 128:(kc + 1) * 128], None, wo_t[kc]) for kc in range(DC)]
                gt2 = mmgroup(P, ps, mms2, deps=ydeps, sub=ps.ap[:, 0:128])
                ev2 = P.op("vector", lambda v, o=stg[:, 0:128], pf=ps.ap[:, 0:128]:
                           v.tensor_copy(o, pf), deps=[gt2, g.get("_stg_guard")],
                           sig="S_vector")
                dk = P.dma("sync", ydel[oc * 128:(oc + 1) * 128, tci * 128:(tci + 1) * 128],
                           stg[:, 0:128], deps=[ev2])
                ps.consumed = dk
                g["_stg_guard"] = dk
                g["_wo_last"] = gt2
                g["_l3_yh"] = gt2

        NA = len(itersA)
        for k in range(NA + 1):
            if k < NA:
                frontA(k)
            if k >= 1:
                backA(k - 1)
            if k < NA:
                middleA(k)
    last_h = [g["_yh%d" % h] for h in range(H)]
    rs_toks = [P.dma("sync", g["XT"][:, c * T:(c + 1) * T], xres[c * 128:(c + 1) * 128, :],
                     deps=last_h + [g["_stg_guard"]]) for c in range(DC)]
    GA.ptr = ga2
    DB = GA.alloc(2 * T)
    db_ld = [None, None]
    for c in range(DC):
        buf = DB[:, (c % 2) * T:(c % 2 + 1) * T]
        lt = P.dma("sync", buf, ydel[c * 128:(c + 1) * 128, :],
                   deps=[g["_stg_guard"], db_ld[c % 2]])
        tk = P.op("vector", lambda v, o=g["XT"][:, c * T:(c + 1) * T], i=buf:
                  v.tensor_tensor(o, o, i, ALU.add), deps=[lt, rs_toks[c]], sig="S_vector")
        db_ld[c % 2] = tk


def _bfly_coeff(P, g, wap, stage, kind, dst, deps=()):
    s = 2 ** stage
    toks = []
    pitch = dst.tensor.shape[1]
    doff = dst.offset
    if stage <= 6:
        q_cnt = 128 // (2 * s)
        base = stage * 8192
        if kind == "diag":
            bdim, boff = [3, 2], 0
        elif kind == "part":
            bdim, boff = [1, 2], 1
        else:
            bdim, boff = [-1, 2], 2
        dims = ([[4 * s, q_cnt]] if q_cnt > 1 else []) + [bdim] + \
            ([[4, s]] if s > 1 else []) + [[256, 32]]
        odims = ([[pitch * 2 * s, q_cnt]] if q_cnt > 1 else []) + \
            [[pitch * s, 2]] + ([[pitch, s]] if s > 1 else []) + [[1, 32]]
        _peel_dma(P, toks, wap.tensor, base + boff, dims,
                  dst.tensor, doff, odims, deps)
    else:
        sb = stage - 7
        lo_cnt = 2 ** sb
        hi_cnt = 32 // (2 * lo_cnt)
        base = stage * 8192
        boff, bstep = {"diag": (0, 3), "part": (1, 1), "partT": (2, -1)}[kind]
        dims = ([[s * 4, hi_cnt]] if hi_cnt > 1 else []) + [[bstep, 2]] + \
            ([[512, lo_cnt]] if lo_cnt > 1 else [])
        odims = ([[2 * lo_cnt, hi_cnt]] if hi_cnt > 1 else []) + [[lo_cnt, 2]] + \
            ([[1, lo_cnt]] if lo_cnt > 1 else [])
        _peel_dma(P, toks, wap.tensor, base + boff, dims,
                  dst.tensor, doff, odims, deps, lead=([4, 128], [pitch, 128]))
    return toks


def _peel_dma(P, toks, it, ioff, idims, ot, ooff, odims, deps, lead=None):
    maxd = 1 if lead else 2
    if len(idims) <= maxd:
        ii = ([list(lead[0])] if lead else []) + [list(d) for d in idims]
        oo = ([list(lead[1])] if lead else []) + [list(d) for d in odims]
        eng = ("gpsimd", "sync", "scalar")[len(toks) % 3]
        toks.append(P.dma(eng, bass.AP(ot, ooff, oo), bass.AP(it, ioff, ii),
                          deps=deps, allow_slow_non_contiguous=True))
        return
    (istep, cnt), (ostep, cnt2) = idims[0], odims[0]
    assert cnt == cnt2
    for k in range(cnt):
        _peel_dma(P, toks, it, ioff + k * istep, idims[1:],
                  ot, ooff + k * ostep, odims[1:], deps, lead)


def emit_L2_ffn(P, g):
    W, GA, XN, psum = g["W"], g["GA"], g["XN"], g["psum"]
    xdeps = emit_layernorm(P, g)
    xres = g["dram"]("l2f_xres", [DM, T])
    wexp2 = g["dram"]("l2f_wexp2", [DM, 4096], BF16)
    wcon2 = g["dram"]("l2f_wcon2", [4096, DM], BF16)
    sp_toks = [P.dma("sync", xres[c * 128:(c + 1) * 128, :], g["XT"][:, c * T:(c + 1) * T],
                     deps=list(xdeps)) for c in range(DC)]
    upw, dww = W("L2_ffn_up_weights"), W("L2_ffn_down_weights")
    CO = {}
    co_t = {}
    for wap, tag in ((upw, "u"), (dww, "d")):
        for st in range(12):
            if st < 6:
                kinds = ("diag", "part")
            elif st == 6:
                kinds = ("diag", "part", "partT")
            elif tag == "d":
                kinds = ("diag", "partT")
            else:
                kinds = ("diag", "part")
            for kind in kinds:
                buf = GA.alloc(32, F32)
                co_t[(tag, st, kind)] = _bfly_coeff(P, g, wap, st, kind, buf)
                CO[(tag, st, kind)] = buf
    LT = {"u": GA.alloc(32 * 128), "d": GA.alloc(32 * 128)}
    becol = GA.alloc(32, F32)
    B1 = GA.alloc(32, F32)
    GB = GA.alloc(32, F32)
    bcol = GA.alloc(8, F32)
    bcol2 = GA.alloc(8, F32)
    ga_fw = GA.ptr
    St2 = [GA.alloc(128), GA.alloc(128)]
    Rt_ps2 = [psum[5], psum[6]]
    chain_done = {}
    prev_ev = None
    ccnt = 0
    for tag in ("u", "d"):
        for c in range(32):
            St = St2[ccnt % 2]
            Rt_ps = Rt_ps2[ccnt % 2]
            ccnt += 1
            rt = LT[tag][:, c * 128:(c + 1) * 128]
            t1 = P.op("scalar", lambda se, o=rt, a=CO[(tag, 6, "diag")][:, c:c + 1]:
                      se.activation(o, g["eyeb_sb"], AF.Copy, scale=a),
                      deps=co_t[(tag, 6, "diag")] + [prev_ev], sig="S_scalar")
            t2 = P.op("vector", lambda v, o=rt, pm=g["pmask_sb"][:, 6 * 128:7 * 128],
                      a=CO[(tag, 6, "partT")][:, c:c + 1]:
                      v.scalar_tensor_tensor(o, pm, a, o, ALU.mult, ALU.add),
                      deps=[t1] + co_t[(tag, 6, "partT")], sig="S_vector")
            prev = t2
            for k in range(5, -1, -1):
                t3 = P.op("scalar", lambda se, a=CO[(tag, k, "diag")][:, c:c + 1]:
                          se.activation(St, g["eyeb_sb"], AF.Copy, scale=a),
                          deps=co_t[(tag, k, "diag")] + [g.get("_l2_st%d" % (ccnt % 2))], sig="S_scalar")
                t4 = P.op("vector", lambda v, pm=g["pmask_sb"][:, k * 128:(k + 1) * 128],
                          a=CO[(tag, k, "part")][:, c:c + 1]:
                          v.scalar_tensor_tensor(St, pm, a, St, ALU.mult, ALU.add),
                          deps=[t3] + co_t[(tag, k, "part")], sig="S_vector")
                t5 = P.op("tensor", lambda t, r=rt, pf=Rt_ps.ap[:, 0:128]:
                          t.matmul(pf, St, r, start=True, stop=True),
                          deps=[t4, prev, Rt_ps.consumed], sig="S_tensor")
                g["_l2_st%d" % (ccnt % 2)] = t5
                prev = P.op("vector", lambda v, o=rt, pf=Rt_ps.ap[:, 0:128]:
                            v.tensor_copy(o, pf), deps=[t5], sig="S_vector")
                Rt_ps.consumed = prev
            chain_done[(tag, c)] = prev
            prev_ev = prev
    W1T = g["XTspace"]
    wstage2 = [GA.alloc(4096, F32), GA.alloc(4096, F32)]
    TMPW = GA.alloc(2048)
    TMP2W = GA.alloc(2048)
    tile2 = [GA.alloc(128), GA.alloc(128)]
    last_mm = None
    mm_tok = {}
    st_tok = [None, None]
    for kc in range(DC):
        wstage = wstage2[kc % 2]
        st_tok[kc % 2] = P.dma("sync", wstage, W("L2_ffn_W_exp")[kc * 128:(kc + 1) * 128, :],
                               deps=sp_toks + [st_tok[kc % 2], mm_tok.get(kc - 2)])
        for c in range(32):
            cp = c % 2
            tile = tile2[cp]
            ps_t = psum[0 + cp]
            tt = P.op("tensor", lambda t, i=wstage[:, c * 128:(c + 1) * 128],
                      pf=ps_t.ap[:, 0:128]: t.transpose(pf, i, g["eye_sb"]),
                      deps=[st_tok[kc % 2], ps_t.consumed], sig="S_tensor")
            te = P.op("vector", lambda v, o=tile, pf=ps_t.ap[:, 0:128]: v.tensor_copy(o, pf),
                      deps=[tt, g.get("_l2_tile%d" % cp)], sig="S_vector")
            ps_t.consumed = te
            ps_w = psum[2 + cp]
            tm = P.op("tensor", lambda t, l=LT["u"][:, c * 128:(c + 1) * 128], r=tile,
                      pf=ps_w.ap[:, 0:128]:
                      t.matmul(pf, l, r, start=True, stop=True),
                      deps=[te, chain_done[("u", c)], ps_w.consumed], sig="S_tensor")
            g["_l2_tile%d" % cp] = tm
            last_mm = P.op("vector", lambda v, o=W1T[:, c * 1024 + kc * 128: c * 1024 + (kc + 1) * 128],
                           pf=ps_w.ap[:, 0:128]: v.tensor_copy(o, pf),
                           deps=[tm], sig="S_vector")
            ps_w.consumed = last_mm
        mm_tok[kc] = last_mm
    def pair_fold(tag, stages, buf, width, deps0):
        last = deps0
        for stg in stages:
            m = 2 ** (stg - 7)
            done = []
            pi = 0
            for B in range(32):
                Pp = B ^ m
                if Pp < B:
                    continue
                pi += 1
                TA = TMPW if pi % 2 == 0 else TMP2W
                kpart = "part" if tag == "u" else "partT"
                da = CO[(tag, stg, "diag")][:, B:B + 1]
                oa = CO[(tag, stg, kpart)][:, B:B + 1]
                db = CO[(tag, stg, "diag")][:, Pp:Pp + 1]
                ob = CO[(tag, stg, kpart)][:, Pp:Pp + 1]
                a = buf[:, B * width:(B + 1) * width]
                b = buf[:, Pp * width:(Pp + 1) * width]
                d0 = list(last)
                x1 = P.op("scalar", lambda se, o=TA[:, 0:width], i=b, s=oa:
                          se.activation(o, i, AF.Copy, scale=s),
                          deps=d0 + co_t[(tag, stg, kpart)] + [g.get("_l2_pf%d" % (pi % 2))],
                          sig="S_scalar")
                x2 = P.op("vector", lambda v, o=TA[:, 0:width], i=a, s=da:
                          v.scalar_tensor_tensor(o, i, s, o, ALU.mult, ALU.add),
                          deps=[x1] + co_t[(tag, stg, "diag")], sig="S_vector")
                x3 = P.op("scalar", lambda se, o=TA[:, width:2 * width], i=a, s=ob:
                          se.activation(o, i, AF.Copy, scale=s),
                          deps=d0 + [g.get("_l2_pf%d" % (pi % 2)), x1], sig="S_scalar")
                x4 = P.op("vector", lambda v, o=b, i=b, s=db, t2=TA[:, width:2 * width]:
                          v.scalar_tensor_tensor(o, i, s, t2, ALU.mult, ALU.add),
                          deps=[x3, x2], sig="S_vector")
                x5 = P.op("vector", lambda v, o=a, i=TA[:, 0:width]:
                          v.tensor_copy(o, i), deps=[x2, x4], sig="S_vector")
                g["_l2_pf%d" % (pi % 2)] = x5
                done.append(x5)
                done.append(x4)
            last = done
        return last
    ufold_done = pair_fold("u", range(7, 12), W1T, 1024, [last_mm])
    wx_toks = []
    wslab = wstage2[0].bitcast(BF16)
    for kc in range(DC):
        evs = []
        for c in range(32):
            ps_t = psum[0]
            tt = P.op("tensor", lambda t, i=W1T[:, c * 1024 + kc * 128: c * 1024 + (kc + 1) * 128],
                      pf=ps_t.ap.bitcast(BF16)[:, 0:128]: t.transpose(pf, i, g["eyeb_sb"]),
                      deps=ufold_done + [ps_t.consumed] +
                      ([wx_toks[-1]] if wx_toks else []), sig="S_tensor")
            ev = P.op("vector", lambda v, o=wslab[:, c * 128:(c + 1) * 128],
                      pf=ps_t.ap.bitcast(BF16)[:, 0:128]: v.tensor_copy(o, pf),
                      deps=[tt], sig="S_vector")
            ps_t.consumed = ev
            evs.append(ev)
        wx_toks.append(P.dma("sync", wexp2[kc * 128:(kc + 1) * 128, :], wslab[:, 0:4096],
                             deps=evs))
    wx_done = wx_toks[-1]
    wc_l = []
    for fc in range(32):
        wc_l.append(P.dma("gpsimd", W1T[:, fc * 1024:(fc + 1) * 1024],
                          W("L2_ffn_W_con")[fc * 128:(fc + 1) * 128, :],
                          deps=[wx_done]))
    dfold_done = pair_fold("d", range(11, 6, -1), W1T, 1024, wc_l)
    wc_toks = []
    for fc in range(32):
        wc_toks.append(P.dma("sync", wcon2[fc * 128:(fc + 1) * 128, :],
                             W1T[:, fc * 1024:(fc + 1) * 1024],
                             deps=dfold_done + (wc_toks[-1:] if wc_toks else [])))
    wc_done = wc_toks[-1]
    t_be = P.dma("gpsimd", becol, W("L2_ffn_b_exp").rearrange("one (c p) -> p (one c)", p=128),
                 allow_slow_non_contiguous=True)
    for c in range(32):
        ps_b = psum[2]
        cb = P.op("vector", lambda v, o=St[:, 0:1], i=becol[:, c:c + 1]:
                  v.tensor_copy(o, i), deps=[t_be, g.get("_l2_bst")], sig="S_vector")
        tb1 = P.op("tensor", lambda t, l=LT["u"][:, c * 128:(c + 1) * 128],
                   pf=ps_b.ap[:, 0:1]:
                   t.matmul(pf, l, St[:, 0:1], start=True, stop=True),
                   deps=[cb, chain_done[("u", c)], ps_b.consumed], sig="S_tensor")
        tb2 = P.op("vector", lambda v, o=B1[:, c:c + 1], pf=ps_b.ap[:, 0:1]:
                   v.tensor_copy(o, pf), deps=[tb1], sig="S_vector")
        ps_b.consumed = tb2
        g["_l2_bst"] = tb2
    bfold = pair_fold("u", range(7, 12), B1, 1, [g["_l2_bst"]])
    t_ub = P.dma("gpsimd", GB, W("L2_ffn_up_bias").rearrange("one (c p) -> p (one c)", p=128),
                 allow_slow_non_contiguous=True)
    t_gb = P.op("vector", lambda v: v.tensor_tensor(GB, GB, B1, ALU.add),
                deps=bfold + [t_ub], sig="S_vector")
    t_b1 = P.dma("gpsimd", bcol, W("L2_ffn_b_con").rearrange("one (c p) -> p (one c)", p=128),
                 allow_slow_non_contiguous=True)
    t_b2 = P.dma("gpsimd", bcol2, W("L2_ffn_down_bias").rearrange("one (c p) -> p (one c)", p=128),
                 allow_slow_non_contiguous=True)
    t_bc = P.op("vector", lambda v: v.tensor_tensor(bcol, bcol, bcol2, ALU.add),
                deps=[t_b1, t_b2], sig="S_vector")
    GA.ptr = ga_fw
    wes = [W1T[:, kc * 4096:(kc + 1) * 4096] for kc in range(DC)]
    we_t = []
    for kc in range(DC):
        we_t.append(P.dma("gpsimd", wes[kc], wexp2[kc * 128:(kc + 1) * 128, :],
                          deps=[wc_done, wx_toks[kc], t_gb, g.get("_l2_bst")]))
    H = GA.alloc(32 * 512)
    wcs = [GA.alloc(1024), GA.alloc(1024)]
    stgd = GA.alloc(512)
    ydel = g["dram"]("l2f_delta", [DM, T], BF16)
    for tb in range(NT):
        h_toks = {}
        for cp in range(32):
            ps = psum[cp % 4]
            mms = [(wes[kc][:, cp * 128:(cp + 1) * 128],
                    XN[:, kc * T + tb * 512: kc * T + (tb + 1) * 512],
                    None, we_t[kc]) for kc in range(DC)]
            gt = mmgroup(P, ps, mms, deps=list(xdeps))
            ev = P.op("scalar", lambda s, o=H[:, cp * 512:(cp + 1) * 512], pf=ps.ap,
                      b=GB[:, cp:cp + 1]: s.activation(o, pf, AF.Gelu, bias=b),
                      deps=[gt, t_gb, g.get("_l2_h")], sig="S_scalar")
            ps.consumed = ev
            h_toks[cp] = ev
        h2_toks = {}
        for cp in range(32):
            ps = psum[4 + cp % 2]
            gt = P.op("tensor", lambda t, l=LT["d"][:, cp * 128:(cp + 1) * 128],
                      r=H[:, cp * 512:(cp + 1) * 512], pf=ps.ap:
                      t.matmul(pf, l, r, start=True, stop=True),
                      deps=[h_toks[cp], chain_done[("d", cp)], ps.consumed], sig="S_tensor")
            ev = P.op("vector", lambda v, o=H[:, cp * 512:(cp + 1) * 512], pf=ps.ap:
                      v.tensor_copy(o, pf), deps=[gt], sig="S_vector")
            ps.consumed = ev
            h2_toks[cp] = ev
        g["_l2_h"] = h2_toks[31]
        mm_last = {}
        wload = {}
        for kc in range(32):
            wload[kc] = P.dma("gpsimd", wcs[kc % 2], wcon2[kc * 128:(kc + 1) * 128, :],
                              deps=[wc_toks[kc], mm_last.get(kc - 2), t_gb,
                                    g.get("_l2_bst"), g.get("_l2_wc_prev")])
            tk = None
            for oc in range(DC):
                ps = psum[oc]
                tk = P.op("tensor", lambda t, l=wcs[kc % 2][:, oc * 128:(oc + 1) * 128],
                          r=H[:, kc * 512:(kc + 1) * 512], st=(kc == 0), sp=(kc == 31),
                          pf=ps.ap: t.matmul(pf, l, r, start=st, stop=sp),
                          deps=[wload[kc], h2_toks[kc]] +
                          ([ps.consumed] if kc == 0 else []),
                          sig="S_tensor" if oc == DC - 1 else None)
            mm_last[kc] = tk
        g["_l2_wc_prev"] = mm_last[31]
        for oc in range(DC):
            ps = psum[oc]
            ev = P.op("vector", lambda v, o=stgd, pf=ps.ap, b=bcol[:, oc:oc + 1]:
                      v.tensor_scalar(o, pf, b, None, ALU.add),
                      deps=[mm_last[31], t_bc, g.get("_l2_stg")], sig="S_vector")
            dk = P.dma("sync", ydel[oc * 128:(oc + 1) * 128, tb * 512:(tb + 1) * 512],
                       stgd, deps=[ev])
            ps.consumed = dk
            g["_l2_stg"] = dk
    rs_toks = [P.dma("sync", g["XT"][:, c * T:(c + 1) * T], xres[c * 128:(c + 1) * 128, :],
                     deps=[g["_l2_stg"], g["_l2_wc_prev"]]) for c in range(DC)]
    DB = H[:, 0:2 * T]
    db_ld = [None, None]
    for c in range(DC):
        buf = DB[:, (c % 2) * T:(c % 2 + 1) * T]
        lt = P.dma("sync", buf, ydel[c * 128:(c + 1) * 128, :],
                   deps=[g["_l2_stg"], db_ld[c % 2]])
        tk = P.op("vector", lambda v, o=g["XT"][:, c * T:(c + 1) * T], i=buf:
                  v.tensor_tensor(o, o, i, ALU.add), deps=[lt, rs_toks[c]], sig="S_vector")
        db_ld[c % 2] = tk


def emit_L0_attn(P, g):
    W, GA, XN, psum = g["W"], g["GA"], g["XN"], g["psum"]
    H, NST = 8, 64
    xdeps = emit_layernorm(P, g)
    xres = g["dram"]("l0a_xres", [DM, T])
    zdram = g["dram"]("l0a_z", [2048, T], BF16)
    w8d = g["dram"]("l0a_w8", [8 * T])
    u8d = g["dram"]("l0a_u8", [8 * T])
    cumd = g["dram"]("l0a_cum", [8 * T])
    eacd = g["dram"]("l0a_eac", [128])
    sp_toks = [P.dma("sync", xres[c * 128:(c + 1) * 128, :], g["XT"][:, c * T:(c + 1) * T],
                     deps=list(xdeps)) for c in range(DC)]
    XT8 = g["XTspace"][:, 0:4 * T]
    Xtok = g["XTspace"][:, 4 * T:12 * T]
    YT = g["XTspace"][:, 12 * T:16 * T]
    fa = g["FAcol"]
    ga_start = GA.ptr
    wz, wz_t = load_wslabs(P, g, W("L0_attn_Wz"), DM, 2048)
    zstg = GA.alloc(T)
    z_w = []
    zprev = None
    for zc in range(16):
        evs = []
        for tb in range(NT):
            ps = psum[tb % 2]
            mms = [(wz[kc][:, zc * 128:(zc + 1) * 128],
                    XN[:, kc * T + tb * 512: kc * T + (tb + 1) * 512],
                    None, wz_t[kc]) for kc in range(DC)]
            gt = mmgroup(P, ps, mms, deps=list(xdeps))
            ev = P.op("scalar", lambda s, o=zstg[:, tb * 512:(tb + 1) * 512], pf=ps.ap:
                      s.activation(o, pf, AF.Silu), deps=[gt, zprev], sig="S_scalar")
            ps.consumed = ev
            evs.append(ev)
        zprev = P.dma("sync", zdram[zc * 128:(zc + 1) * 128, :], zstg, deps=evs)
        z_w.append(zprev)
    GA.ptr = ga_start
    Bc = GA.alloc(T)
    Cc = GA.alloc(T)
    cv1 = GA.alloc(T, F32)
    cv2 = GA.alloc(T, F32)
    Btok = GA.alloc(TC * 64)
    S = GA.alloc(512, F32)
    SB16 = GA.alloc(512)
    ACB = GA.alloc(128, F32)
    EXPACB = GA.alloc(128, F32)
    DCB = GA.alloc(8, F32)
    WLT = GA.alloc(8, F32)
    DLT = GA.alloc(8, F32)
    W8R = GA.alloc(1024, F32)
    U8R = GA.alloc(1024, F32)
    CUMR = GA.alloc(1024, F32)
    g["_l0_UBC2"] = [GA.alloc(128), GA.alloc(128)]
    g["_l0_CUT2"] = [GA.alloc(128), GA.alloc(128)]
    g["_l0_GH2"] = [GA.alloc(128), GA.alloc(128)]
    g["_l0_GEXP2"] = [GA.alloc(128), GA.alloc(128)]
    g["_l0_CBMD2"] = [GA.alloc(128), GA.alloc(128)]
    g["_l0_W2S2"] = [fa.alloc(1, F32), fa.alloc(1, F32)]
    CBM = GA.alloc(128)
    ONE8 = GA.alloc(128, F32)
    BWB = [GA.alloc(64), GA.alloc(64)]
    zdone = [zprev, evs[-1]]
    t_one8 = P.op("vector", lambda v: v.memset(ONE8[0:8, :], 1.0), deps=zdone, sig="S_vector")
    DT8 = cv1[0:8, :]
    CUM = cv1[64:72, :]
    TMP8 = cv2[0:8, :]
    U8 = cv2[32:40, :]
    W8 = cv2[64:72, :]
    win, win_t = load_wslabs(P, g, W("L0_attn_Win"), DM, 648, deps=zdone)
    def fmproj(dst_ap, col0, ww, psl, deps0=()):
        done = None
        for tb in range(NT):
            ps = psum[psl + (tb % 2)]
            mms = [(win[kc][:, col0:col0 + ww],
                    XN[:, kc * T + tb * 512: kc * T + (tb + 1) * 512],
                    ps.ap[0:ww, 0:512], win_t[kc]) for kc in range(DC)]
            gt = mmgroup(P, ps, mms, deps=list(deps0))
            done = P.op("vector", lambda v, o=dst_ap[0:ww, tb * 512:(tb + 1) * 512],
                        pf=ps.ap[0:ww, 0:512]: v.tensor_copy(o, pf),
                        deps=[gt], sig="S_vector")
            ps.consumed = done
        return done
    b_done = fmproj(Bc, 0, 64, 0)
    c_done = fmproj(Cc, 64, 64, 2)
    dt_done = fmproj(DT8, 128, 8, 0)
    x8_done = None
    for xc in range(4):
        x8_done = fmproj(XT8[:, xc * T:(xc + 1) * T], 136 + xc * 128, 128, 2 * (xc % 2),
                         deps0=sp_toks)
    xt_done = None
    for tc in range(TC):
        ps = psum[4]
        mms = [(XN[:, kc * T + tc * 128: kc * T + (tc + 1) * 128],
                win[kc][:, 136:648], None, win_t[kc]) for kc in range(DC)]
        gt = mmgroup(P, ps, mms, deps=sp_toks)
        xt_done = P.op("vector", lambda v, o=Xtok[:, tc * 512:(tc + 1) * 512], pf=ps.ap:
                       v.tensor_copy(o, pf), deps=[gt], sig="S_vector")
        ps.consumed = xt_done
        ps = psum[5]
        mms = [(XN[:, kc * T + tc * 128: kc * T + (tc + 1) * 128],
                win[kc][:, 0:64], ps.ap[:, 0:64], win_t[kc]) for kc in range(DC)]
        gt = mmgroup(P, ps, mms, deps=[])
        bt_done = P.op("vector", lambda v, o=Btok[:, tc * 64:(tc + 1) * 64],
                       pf=ps.ap[:, 0:64]: v.tensor_copy(o, pf), deps=[gt], sig="S_vector")
        ps.consumed = bt_done
    dtb = fa.alloc(1, F32)
    t_dtb = P.dma("gpsimd", dtb[0:8, :], W("L0_attn_dt_bias").rearrange("one e -> e one"),
                  allow_slow_non_contiguous=True)
    alog = fa.alloc(1, F32)
    t_al = P.dma("gpsimd", alog[0:8, :], W("L0_attn_A_log").rearrange("one e -> e one"),
                 allow_slow_non_contiguous=True)
    nA = fa.alloc(1, F32)
    t_nA1 = P.op("scalar", lambda s: s.activation(nA[0:8, :], alog[0:8, :], AF.Exp),
                 deps=[t_al], sig="S_scalar")
    t_nA = P.op("vector", lambda v: v.tensor_scalar(nA[0:8, :], nA[0:8, :], -1.0, None, ALU.mult),
                deps=[t_nA1], sig="S_vector")
    s1 = P.op("vector", lambda v: v.tensor_scalar(DT8, DT8, dtb[0:8, :], None, ALU.add),
              deps=[dt_done, t_dtb], sig="S_vector")
    s2 = P.op("scalar", lambda s: s.activation(TMP8, DT8, AF.Abs), deps=[s1], sig="S_scalar")
    s3 = P.op("scalar", lambda s: s.activation(TMP8, TMP8, AF.Exp, scale=-1.0),
              deps=[s2], sig="S_scalar")
    s4 = P.op("scalar", lambda s: s.activation(TMP8, TMP8, AF.Ln, bias=g["cc"](1.0)[64:72, :]),
              deps=[s3], sig="S_scalar")
    s5 = P.op("vector", lambda v: v.tensor_scalar(DT8, DT8, 0.0, None, ALU.max),
              deps=[s4], sig="S_vector")
    s6 = P.op("vector", lambda v: v.tensor_tensor(DT8, DT8, TMP8, ALU.add),
              deps=[s5], sig="S_vector")
    s7 = P.op("vector", lambda v: v.tensor_scalar(TMP8, DT8, nA[0:8, :], None, ALU.mult),
              deps=[s6, t_nA], sig="S_vector")
    sc_t = []
    for c in range(TC):
        tk = P.op("vector", lambda v, o=CUM[:, c * 128:(c + 1) * 128],
                  d1=TMP8[:, c * 128:(c + 1) * 128]:
                  v.tensor_tensor_scan(o, ONE8[0:8, :], d1, 0.0, ALU.mult, ALU.add),
                  deps=[s7, t_one8], sig="S_vector")
        sc_t.append(tk)
    s9 = P.op("vector", lambda v: v.tensor_scalar(W8, CUM, -1.0, None, ALU.mult),
              deps=sc_t, sig="S_vector")
    s10 = P.op("scalar", lambda s: s.activation(U8, CUM, AF.Exp), deps=sc_t, sig="S_scalar")
    EACs = fa.alloc(16, F32)
    s11 = P.op("vector", lambda s: s.tensor_copy(EACs[0:8, 0:16], CUM[:, 127:2048:128]),
               deps=sc_t, sig="S_vector")
    d1 = P.dma("sync", w8d.rearrange("(c h s) -> h c s", h=8, c=16), W8, deps=[s9])
    d2 = P.dma("sync", u8d.rearrange("(c h s) -> h c s", h=8, c=16), U8, deps=[s10])
    d2b = P.dma("sync", cumd.rearrange("(c h s) -> h c s", h=8, c=16), CUM, deps=sc_t)
    d3 = P.dma("sync", eacd.rearrange("(h c) -> h c", h=8), EACs[0:8, 0:16], deps=[s11])
    EACR = fa.alloc(128, F32)
    d4 = P.dma("sync", EACR[0:1, 0:128], eacd.rearrange("(one x) -> one x", one=1), deps=[d3])
    pb = psum[4]
    t_e1 = P.op("tensor", lambda t: t.matmul(pb.ap[:, 0:128], g["ones1f_sb"], EACR[0:1, 0:128],
                                             start=True, stop=True),
                deps=[d4, pb.consumed], sig="S_tensor")
    t_ac = P.op("vector", lambda v: v.tensor_copy(ACB, pb.ap[:, 0:128]),
                deps=[t_e1], sig="S_vector")
    t_e2 = P.op("scalar", lambda s: s.activation(EXPACB, ACB, AF.Exp),
                deps=[t_ac], sig="S_scalar")
    pb.consumed = t_e2
    DR = fa.alloc(8, F32)
    d5 = P.dma("sync", DR[0:1, 0:8], W("L0_attn_D_param"), deps=[])
    t_d1 = P.op("tensor", lambda t: t.matmul(pb.ap[:, 0:8], g["ones1f_sb"], DR[0:1, 0:8],
                                             start=True, stop=True),
                deps=[d5, pb.consumed], sig="S_tensor")
    t_d2 = P.op("vector", lambda v: v.tensor_copy(DCB, pb.ap[:, 0:8]),
                deps=[t_d1], sig="S_vector")
    pb.consumed = t_d2
    t_s0 = P.op("vector", lambda v: v.memset(S, 0.0), sig="S_vector")
    wu_prev = None
    su_last = t_s0
    sb_tok = None
    for c in range(TC):
        csl = slice(c * 128, (c + 1) * 128)
        r1 = P.dma("sync", W8R[0:1, 0:1024],
                   w8d[c * 1024:(c + 1) * 1024].rearrange("(one x) -> one x", one=1),
                   deps=[d1, wu_prev])
        r2 = P.dma("sync", U8R[0:1, 0:1024],
                   u8d[c * 1024:(c + 1) * 1024].rearrange("(one x) -> one x", one=1),
                   deps=[d2, wu_prev])
        r3 = P.dma("sync", CUMR[0:1, 0:1024],
                   cumd[c * 1024:(c + 1) * 1024].rearrange("(one x) -> one x", one=1),
                   deps=[d2b, wu_prev])
        g["_l0_cumr"] = r3
        g["cumr_slice"] = lambda cc, hh: CUMR[0:1, hh * 128:(hh + 1) * 128]
        ps5 = psum[5]
        tw = P.op("tensor", lambda t, i=CUM[:, csl], pf=ps5.ap[:, 0:8]:
                  t.transpose(pf, i, g["eye_sb"][64:72, 64:72]),
                  deps=[sc_t[c], ps5.consumed], sig="S_tensor")
        twe = P.op("vector", lambda v, pf=ps5.ap[:, 0:8]: v.tensor_copy(WLT, pf),
                   deps=[tw, g.get("_l0_wlt")], sig="S_vector")
        ps5.consumed = twe
        td = P.op("tensor", lambda t, i=DT8[:, csl], pf=ps5.ap[:, 8:16]:
                  t.transpose(pf, i, g["eye_sb"][0:8, 0:8]),
                  deps=[s6, twe], sig="S_tensor")
        tde = P.op("vector", lambda v, pf=ps5.ap[:, 8:16]: v.tensor_copy(DLT, pf),
                   deps=[td, g.get("_l0_dlt")], sig="S_vector")
        ps5.consumed = tde
        sb_tok = P.op("vector", lambda v: v.tensor_copy(SB16[0:64, :], S[0:64, :]),
                      deps=[su_last, g.get("_l0_sb")], sig="S_vector")
        ps_cb = psum[6]
        tcb = P.op("tensor", lambda t, l=Bc[0:64, csl], r=Cc[0:64, csl], pf=ps_cb.ap[:, 0:128]:
                   t.matmul(pf, l, r, start=True, stop=True),
                   deps=[b_done, c_done, ps_cb.consumed], sig="S_tensor")
        tcbm = P.op("vector", lambda v, pf=ps_cb.ap[:, 0:128]:
                    v.tensor_tensor(CBM, pf, g["triub_sb"], ALU.mult),
                    deps=[tcb, g.get("_l0_cbm")], sig="S_vector")
        ps_cb.consumed = tcbm
        for h in range(H):
            hp = h % 2
            GHh = GH[:, hp * 64 if False else 0:128] if False else g["_l0_GH2"][hp]
            CBMDh = g["_l0_CBMD2"][hp]
            GEXPh = g["_l0_GEXP2"][hp]
            UBCh = g["_l0_UBC2"][hp]
            CUTh = g["_l0_CUT2"][hp]
            W2Sh = g["_l0_W2S2"][hp]
            ps_g = psum[7 - hp]
            tg0 = P.op("tensor", lambda t, l=W8R[0:1, h * 128:(h + 1) * 128],
                       pf=ps_g.ap[:, 0:128]:
                       t.matmul(pf, l, g["ones1f_sb"], start=True, stop=False),
                       deps=[r1, ps_g.consumed] + ([tcbm] if hp == 1 else []), sig=None)
            tg = P.op("tensor", lambda t, r=g["cumr_slice"](c, h),
                      pf=ps_g.ap[:, 0:128]:
                      t.matmul(pf, g["ones1f_sb"], r, start=False, stop=True),
                      deps=[g["_l0_cumr"]], sig="S_tensor")
            tmsk = P.op("vector", lambda v, pf=ps_g.ap[:, 0:128]:
                        v.tensor_tensor(pf, pf, g["maskfl_sb"], ALU.add),
                        deps=[tg], sig="S_vector")
            tex = P.op("scalar", lambda s, o=GEXPh, pf=ps_g.ap[:, 0:128]:
                       s.activation(o, pf, AF.Exp),
                       deps=[tmsk, g.get("_l0_gh%d" % hp)], sig="S_scalar")
            ps_g.consumed = tex
            tcbd = P.op("scalar", lambda s, o=CBMDh, dcol=DLT[:, h:h + 1]:
                        s.activation(o, CBM, AF.Copy, scale=dcol),
                        deps=[tcbm, tde, g.get("_l0_cbmd%d" % hp), tex], sig="S_scalar")
            tgh = P.op("vector", lambda v, o=GHh, a=CBMDh, b=GEXPh:
                       v.tensor_tensor(o, a, b, ALU.mult),
                       deps=[tex, tcbd, g.get("_l0_gh%d" % hp)], sig="S_vector")
            ps_u = psum[4 + hp]
            tu = P.op("tensor", lambda t, r=U8R[0:1, h * 128:(h + 1) * 128],
                      pf=ps_u.ap[0:64, 0:128]:
                      t.matmul(pf, g["ones1f_sb"][0:1, 0:64], r, start=True, stop=True),
                      deps=[r2, ps_u.consumed] + ([tde] if hp == 1 else []), sig="S_tensor")
            tue = P.op("vector", lambda v, o=UBCh[0:64, :], pf=ps_u.ap[0:64, 0:128]:
                       v.tensor_copy(o, pf), deps=[tu, g.get("_l0_ubc%d" % hp)],
                       sig="S_vector")
            ps_u.consumed = tue
            tcu = P.op("vector", lambda v, o=CUTh[0:64, :], i=Cc[0:64, csl], b=UBCh[0:64, :]:
                       v.tensor_tensor(o, i, b, ALU.mult),
                       deps=[tue, g.get("_l0_cut%d" % hp)], sig="S_vector")
            ps_y = psum[0 + hp]
            m1 = (Xtok[:, c * 512 + h * 64: c * 512 + (h + 1) * 64], GHh, ps_y.ap[0:64, 0:128], tgh)
            m2 = (SB16[0:64, h * 64:(h + 1) * 64], CUTh[0:64, :], ps_y.ap[0:64, 0:128], tcu)
            gy = mmgroup(P, ps_y, [m1, m2], deps=[xt_done, sb_tok])
            yv = YT[:, (h // 2) * T + c * 128:(h // 2) * T + (c + 1) * 128][(h % 2) * 64:(h % 2) * 64 + 64, :]
            xv = XT8[:, (h // 2) * T + c * 128:(h // 2) * T + (c + 1) * 128][(h % 2) * 64:(h % 2) * 64 + 64, :]
            ty = P.op("vector", lambda v, o=yv, i=xv, dc=DCB[(h % 2) * 64:(h % 2) * 64 + 64, h:h + 1],
                      pf=ps_y.ap[0:64, 0:128]:
                      v.scalar_tensor_tensor(o, i, dc, pf, ALU.mult, ALU.add),
                      deps=[gy, t_d2, x8_done], sig="S_vector")
            ps_y.consumed = ty
            g["_yh%d" % h] = ty
            if "_l0_w2c%d" % h not in g:
                g["_l0_w2c%d" % h] = fa.alloc(1, F32)
            w2c = g["_l0_w2c%d" % h]
            tw2a = P.op("vector", lambda v, o=W2Sh, a=ACB[:, h * 16 + c: h * 16 + c + 1],
                        b=WLT[:, h:h + 1]:
                        v.tensor_tensor(o, a, b, ALU.subtract),
                        deps=[twe, t_ac, g.get("_l0_w2s%d" % hp)], sig="S_vector")
            tw2b = P.op("scalar", lambda s, o=W2Sh: s.activation(o, o, AF.Exp),
                        deps=[tw2a], sig="S_scalar")
            tw2 = P.op("vector", lambda v, o=w2c, i=W2Sh, b=DLT[:, h:h + 1]:
                       v.tensor_tensor(o, i, b, ALU.mult),
                       deps=[tw2b, tde, g.get("_l0_bw%d" % h)], sig="S_vector")
            g["_l0_w2s%d" % hp] = tw2
            BW = BWB[h % 2]
            tbw = P.op("vector", lambda v, o=BW[:, 0:64], i=Btok[:, c * 64:(c + 1) * 64], w=w2c:
                       v.tensor_scalar(o, i, w, None, ALU.mult),
                       deps=[tw2, g.get("_l0_bwmm%d" % (h % 2))], sig="S_vector")
            g["_l0_bw%d" % h] = tbw
            ps_su = psum[2 + h % 2]
            tsu = P.op("tensor", lambda t, l=BW[:, 0:64],
                       r=Xtok[:, c * 512 + h * 64: c * 512 + (h + 1) * 64],
                       pf=ps_su.ap[0:64, 0:64]:
                       t.matmul(pf, l, r, start=True, stop=True),
                       deps=[tbw, ps_su.consumed], sig="S_tensor")
            g["_l0_bwmm%d" % (h % 2)] = tsu
            tss = P.op("vector", lambda v, o=S[0:64, h * 64:(h + 1) * 64],
                       e=EXPACB[0:64, h * 16 + c: h * 16 + c + 1],
                       pf=ps_su.ap[0:64, 0:64]:
                       v.scalar_tensor_tensor(o, o, e, pf, ALU.mult, ALU.add),
                       deps=[tsu, sb_tok], sig="S_vector")
            ps_su.consumed = tss
            su_last = tss
            g["_l0_gh%d" % hp] = gy
            g["_l0_cbmd%d" % hp] = tgh
            g["_l0_ubc%d" % hp] = tcu
            g["_l0_cut%d" % hp] = gy
        g["_l0_wlt"] = twe
        g["_l0_dlt"] = tde
        g["_l0_cbm"] = tcbm
        g["_l0_sb"] = sb_tok
        wu_prev = tsu
    ydel = g["dram"]("l0a_delta", [DM, T], BF16)
    ydeps = [g["_yh%d" % h] for h in range(H)] + [su_last]
    GA.ptr = ga_start
    wzg, wzg_t = load_wslabs(P, g, W("L0_attn_Wzg"), 2048, 512, deps=ydeps)
    wout, wout_t = load_wslabs(P, g, W("L0_attn_Wout"), 512, DM, deps=ydeps)
    zsl = [GA.alloc(512), GA.alloc(512)]
    stg2 = GA.alloc(512)
    yg_toks = []
    prev_tb_mm = None
    for tb in range(NT):
        zload = [None] * 16
        mm_t = {}
        for zc in range(16):
            zload[zc] = P.dma("gpsimd", zsl[zc % 2],
                              zdram[zc * 128:(zc + 1) * 128, tb * 512:(tb + 1) * 512],
                              deps=[z_w[zc], mm_t.get(zc - 2), prev_tb_mm]
                              + (ydeps if zc < 2 else []))
            tk = None
            for oc in range(4):
                ps = psum[oc]
                tk = P.op("tensor", lambda t, l=wzg[zc][:, oc * 128:(oc + 1) * 128],
                          r=zsl[zc % 2], st=(zc == 0), sp=(zc == 15), pf=ps.ap:
                          t.matmul(pf, l, r, start=st, stop=sp),
                          deps=[zload[zc], wzg_t[zc]] + ([ps.consumed] if zc == 0 else []),
                          sig="S_tensor" if oc == 3 else None)
            mm_t[zc] = tk
        prev_tb_mm = mm_t[15]
        for oc in range(4):
            ps = psum[oc]
            yg = P.op("vector", lambda v, o=YT[:, oc * T + tb * 512: oc * T + (tb + 1) * 512],
                      pf=ps.ap: v.tensor_tensor(o, o, pf, ALU.mult),
                      deps=[mm_t[15]] + ydeps, sig="S_vector")
            ps.consumed = yg
            yg_toks.append(yg)
    for oc in range(DC):
        for tb in range(NT):
            ps = psum[4 + ((oc * NT + tb) % 2)]
            mms = [(wout[kc][:, oc * 128:(oc + 1) * 128],
                    YT[:, kc * T + tb * 512: kc * T + (tb + 1) * 512],
                    None, wout_t[kc]) for kc in range(4)]
            gt = mmgroup(P, ps, mms, deps=[yg_toks[-1], yg_toks[0]])
            ev = P.op("vector", lambda v, o=stg2, pf=ps.ap:
                      v.tensor_copy(o, pf), deps=[gt, g.get("_l0_stg")], sig="S_vector")
            dk = P.dma("sync", ydel[oc * 128:(oc + 1) * 128, tb * 512:(tb + 1) * 512],
                       stg2, deps=[ev])
            ps.consumed = dk
            g["_l0_stg"] = dk
    rs_toks = [P.dma("sync", g["XT"][:, cc * T:(cc + 1) * T], xres[cc * 128:(cc + 1) * 128, :],
                     deps=[g["_l0_stg"]]) for cc in range(DC)]
    GA.ptr = ga_start
    DB = GA.alloc(2 * T)
    db_ld = [None, None]
    for cc in range(DC):
        buf = DB[:, (cc % 2) * T:(cc % 2 + 1) * T]
        lt = P.dma("sync", buf, ydel[cc * 128:(cc + 1) * 128, :],
                   deps=[g["_l0_stg"], db_ld[cc % 2], yg_toks[-1]])
        tk = P.op("vector", lambda v, o=g["XT"][:, cc * T:(cc + 1) * T], i=buf:
                  v.tensor_tensor(o, o, i, ALU.add), deps=[lt, rs_toks[cc]], sig="S_vector")
        db_ld[cc % 2] = tk


def emit_L1_attn(P, g):
    W, GA, XN, psum = g["W"], g["GA"], g["XN"], g["psum"]
    H = 16
    xdeps = emit_layernorm(P, g)
    xres = g["dram"]("l1a_xres", [DM, T])
    ydel = g["dram"]("l1a_delta", [DM, T], BF16)
    sp_toks = [P.dma("sync", xres[c * 128:(c + 1) * 128, :], g["XT"][:, c * T:(c + 1) * T],
                     deps=list(xdeps)) for c in range(DC)]
    QT = g["XTspace"][:, 0:DC * T]
    KT = g["XTspace"][:, DC * T:2 * DC * T]
    VTOK = GA.alloc(TC * 1024)
    onesr = GA.alloc(512)
    t_ones = P.op("vector", lambda v: v.memset(onesr[0:1, :], 1.0), sig="S_vector")
    brow = GA.alloc(3072)
    t_br = P.dma("gpsimd", brow[0:1, :], W("L1_attn_b_attn"))
    ga0 = GA.ptr

    def proj_fm(col0, dst, deps0):
        slabs, st = load_wslabs(P, g, W("L1_attn_W_attn")[:, col0:col0 + 1024],
                                DM, 1024, deps=deps0)
        done = None
        for oc in range(DC):
            for tb in range(NT):
                ps = psum[(oc * NT + tb) % 2]
                mms = [(slabs[kc][:, oc * 128:(oc + 1) * 128],
                        XN[:, kc * T + tb * 512: kc * T + (tb + 1) * 512],
                        None, st[kc]) for kc in range(DC)]
                mms.append((brow[0:1, col0 + oc * 128: col0 + (oc + 1) * 128],
                            onesr[0:1, 0:512], None, t_br))
                gt = mmgroup(P, ps, mms, deps=list(xdeps) + [t_ones] +
                             (sp_toks if col0 == 0 else []))
                done = P.op("vector", lambda v, o=dst[:, oc * T + tb * 512: oc * T + (tb + 1) * 512],
                            pf=ps.ap: v.tensor_copy(o, pf), deps=[gt], sig="S_vector")
                ps.consumed = done
        return done

    q_done = proj_fm(0, QT, [])
    GA.ptr = ga0
    k_done = proj_fm(1024, KT, [q_done])
    GA.ptr = ga0
    slabs, st = load_wslabs(P, g, W("L1_attn_W_attn")[:, 2048:3072], DM, 1024,
                            deps=[k_done])
    v_done = None
    for tc in range(TC):
        for fb in range(2):
            ps = psum[2 + fb]
            mms = [(XN[:, kc * T + tc * 128: kc * T + (tc + 1) * 128],
                    slabs[kc][:, fb * 512:(fb + 1) * 512], None, st[kc])
                   for kc in range(DC)]
            mms.append((onesr[0:1, 0:128],
                        brow[0:1, 2048 + fb * 512: 2048 + (fb + 1) * 512], None, t_br))
            gt = mmgroup(P, ps, mms, deps=list(xdeps) + [t_ones])
            v_done = P.op("vector", lambda v, o=VTOK[:, tc * 1024 + fb * 512: tc * 1024 + (fb + 1) * 512],
                          pf=ps.ap: v.tensor_copy(o, pf), deps=[gt], sig="S_vector")
            ps.consumed = v_done
    GA.ptr = ga0
    KSC = GA.alloc(8, F32)
    KSCB = GA.alloc(8)
    ks_t = []
    for c in range(DC):
        tk = P.op("vector", lambda v, o=KSC[:, c:c + 1], i=KT[:, c * T:(c + 1) * T]:
                  v.tensor_reduce(o, i, AX, ALU.add), deps=[k_done], sig="S_vector")
        ks_t.append(tk)
    ksb = P.op("vector", lambda v: v.tensor_copy(KSCB, KSC), deps=ks_t, sig="S_vector")
    sgr = g["FAcol"].alloc(1, F32)
    t_sg = P.dma("sync", sgr[0:1, 0:1], W("L1_attn_score_gain"))
    gcol, g_t = bcast_cols(P, g, sgr[0:1, 0:1], 1, [t_sg])
    SC2 = [GA.alloc(T), GA.alloc(T)]
    _EB = GA.alloc(T)
    EB2 = [_EB, _EB]
    ATB2 = [GA.alloc(T), GA.alloc(T)]
    YH2 = [GA.alloc(DM), GA.alloc(DM)]
    stg = GA.alloc(512)
    fa = g["FAcol"]
    mcol2 = [fa.alloc(1, F32), fa.alloc(1, F32)]
    madc2 = [fa.alloc(1, F32), fa.alloc(1, F32)]
    ccol2 = [fa.alloc(1, F32), fa.alloc(1, F32)]
    rrc2 = [fa.alloc(1, F32), fa.alloc(1, F32)]
    wp, wp_t = load_wslabs(P, g, W("L1_attn_W_proj"), DM, DM, deps=[v_done])
    bpcol = fa.alloc(DC, F32)
    t_bp = P.dma("gpsimd", bpcol, W("L1_attn_b_proj").rearrange("one (oc p) -> p (one oc)", p=128),
                 allow_slow_non_contiguous=True)
    iters = [(tci, h) for tci in range(TC) for h in range(H)]
    state = {}

    def front(k):
        tci, h = iters[k]
        pp = k % 2
        SC = SC2[pp]
        mcol = mcol2[pp]
        qsl = QT[:, (h // 2) * T + tci * 128: (h // 2) * T + (tci + 1) * 128][(h % 2) * 64:(h % 2) * 64 + 64, :]
        bank_toks = []
        for bb in range(4):
            ps = psum[bb]
            ksl = KT[:, (h // 2) * T + bb * 512: (h // 2) * T + (bb + 1) * 512][(h % 2) * 64:(h % 2) * 64 + 64, :]
            gt = mmgroup(P, ps, [(qsl, ksl, None, None)], deps=[q_done, k_done])
            bank_toks.append((ps, gt))
        tci_, h_ = iters[k]
        s_end_f = (tci_ + 1) * 128
        nb_cf = (s_end_f + 511) // 512
        racc = fa_racc(g, fa, pp)
        ps5 = psum[7]
        tm = P.op("tensor", lambda t, l=qsl,
                  r=KSCB[(h % 2) * 64:(h % 2) * 64 + 64, (h // 2):(h // 2) + 1],
                  pf=ps5.ap[:, pp:pp + 1]: t.matmul(pf, l, r, start=True, stop=True),
                  deps=[ksb, ps5.consumed], sig="S_tensor")
        tmc = P.op("vector", lambda v, o=mcol, pf=ps5.ap[:, pp:pp + 1]:
                   v.tensor_scalar(o, pf, -1.0 / T, None, ALU.mult),
                   deps=[tm, g.get("_l1_cols%d" % pp)], sig="S_vector")
        ps5.consumed = tmc
        d_toks = []
        eb_guard = g.get("_l1_eb")
        for bb, (ps, gt) in enumerate(bank_toks):
            ta = P.op("scalar", lambda s, o=EB2[0][:, bb * 512:(bb + 1) * 512],
                      pf=ps.ap, m=mcol, a=racc[:, bb:bb + 1]:
                      s.activation(o, pf, AF.Abs, bias=m, accum_out=a),
                      deps=[gt, tmc, eb_guard, g.get("_l1_ra%d" % pp)], sig="S_scalar")
            if bb < nb_cf:
                tk = P.op("scalar", lambda s, o=SC[:, bb * 512:(bb + 1) * 512],
                          pf=ps.ap, m=mcol:
                          s.activation(o, pf, AF.Relu, bias=m),
                          deps=[ta, g.get("_l1_sc%d" % pp)], sig="S_scalar")
            else:
                tk = ta
            ps.consumed = tk
            d_toks.append(tk)
        state[k] = d_toks

    def middle(k):
        tci, h = iters[k]
        pp = k % 2
        s_end = (tci + 1) * 128
        SC, EB = SC2[pp], EB2[pp]
        madc, ccol, rrc = madc2[pp], ccol2[pp], rrc2[pp]
        racc = fa_racc(g, fa, pp)
        d_toks = state.pop(k)
        tmad = P.op("vector", lambda v, o=madc, i=racc[:, 0:4]:
                    v.tensor_reduce(o, i, AX, ALU.add), deps=d_toks, sig="S_vector")
        t1 = P.op("vector", lambda v, o=madc: v.tensor_scalar(o, o, 1.0 / T, 1e-6,
                                                              ALU.mult, ALU.add),
                  deps=[tmad], sig="S_vector")
        g["_l1_ra%d" % pp] = tmad
        t2 = P.op("vector", lambda v, o=ccol, i=madc: v.reciprocal(o, i),
                  deps=[t1], sig="S_vector")
        t3 = P.op("vector", lambda v, o=ccol: v.tensor_tensor(o, o, gcol, ALU.mult),
                  deps=[t2, g_t], sig="S_vector")
        nb_c = (s_end + 511) // 512
        u_toks = []
        for bb in range(nb_c):
            w = min(512, s_end - bb * 512)
            tk = P.op("vector", lambda v, o=SC[:, bb * 512: bb * 512 + w], c2=ccol:
                      v.tensor_scalar(o, o, c2, None, ALU.mult),
                      deps=[t3], sig="S_vector")
            u_toks.append(tk)
        tmk = P.op("vector", lambda v, o=SC[:, (tci * 128):(tci * 128) + 128]:
                   v.tensor_tensor(o, o, g["trilb_sb"], ALU.mult),
                   deps=u_toks, sig="S_vector")
        r_toks = []
        for bb in range(nb_c):
            w = min(512, s_end - bb * 512)
            te = P.op("vector", lambda v, o=EB[:, bb * 512: bb * 512 + w],
                      i=SC[:, bb * 512: bb * 512 + w]:
                      v.tensor_tensor(o, i, i, ALU.mult),
                      deps=[tmk, g.get("_l1_eb")], sig="S_vector")
            tr = P.op("vector", lambda v, o=SC[:, bb * 512: bb * 512 + w],
                      i=EB[:, bb * 512: bb * 512 + w], a=racc[:, 4 + bb:5 + bb]:
                      v.scalar_tensor_tensor(o, i, 1.0, i, ALU.mult, ALU.mult,
                                             accum_out=a),
                      deps=[te], sig="S_vector")
            r_toks.append(tr)
        tz = P.op("vector", lambda v, o=rrc, i=racc[:, 4:4 + nb_c]:
                  v.tensor_reduce(o, i, AX, ALU.add), deps=r_toks, sig="S_vector")
        t5 = P.op("vector", lambda v, o=rrc:
                  v.tensor_scalar(o, o, 1.0, 1e-6, ALU.mult, ALU.add),
                  deps=[tz], sig="S_vector")
        t5 = P.op("vector", lambda v, o=rrc, i=rrc: v.reciprocal(o, i),
                  deps=[t5], sig="S_vector")
        tn = P.op("vector", lambda v, o=SC[:, 0:s_end], r2=rrc:
                  v.tensor_scalar(o, o, r2, None, ALU.mult), deps=[t5], sig="S_vector")
        g["_l1_eb"] = r_toks[-1]
        g["_l1_cols%d" % pp] = t5
        state[("n", k)] = tn

    def back(k):
        tci, h = iters[k]
        pp = k % 2
        s_end = (tci + 1) * 128
        nsc = tci + 1
        yh_p = tci % 2
        YH = YH2[yh_p]
        SC, ATB = SC2[pp], ATB2[pp]
        tn = state.pop(("n", k))
        ps_t = psum[5]
        ps_y = psum[4]
        ev = None
        for scb in range(0, nsc, 4):
            nn = min(4, nsc - scb)
            tts = []
            for j in range(nn):
                tt = P.op("tensor", lambda t, i=SC[:, (scb + j) * 128:(scb + j + 1) * 128],
                          pf=ps_t.ap.bitcast(BF16)[:, pp * 512 + j * 128: pp * 512 + (j + 1) * 128]:
                          t.transpose(pf, i, g["eyeb_sb"]),
                          deps=[tn, ps_t.consumed if j == 0 else None], sig="S_tensor")
                tts.append(tt)
            ev = P.op("vector", lambda v, o=ATB[:, scb * 128: scb * 128 + nn * 128],
                      pf=ps_t.ap.bitcast(BF16)[:, pp * 512: pp * 512 + nn * 128]:
                      v.tensor_copy(o, pf),
                      deps=[tts[-1]], sig="S_vector")
            ps_t.consumed = ev
        g["_l1_sc%d" % pp] = ev
        mms = [(VTOK[:, sc * 1024 + h * 64: sc * 1024 + (h + 1) * 64],
                ATB[:, sc * 128:(sc + 1) * 128],
                ps_y.ap[0:64, 128 + pp * 128: 256 + pp * 128], ev) for sc in range(nsc)]
        gt = mmgroup(P, ps_y, mms, deps=[ev, v_done])
        yev = P.op("vector", lambda v, o=YH[(h % 2) * 64:(h % 2) * 64 + 64,
                                            (h // 2) * 128:(h // 2) * 128 + 128],
                   pf=ps_y.ap[0:64, 128 + pp * 128: 256 + pp * 128]:
                   v.tensor_copy(o, pf),
                   deps=[gt, g.get("_l1_yh%d" % yh_p)], sig="S_vector")
        ps_y.consumed = yev
        g["_yh%d" % h] = yev
        if h == H - 1:
            ydeps = [g["_yh%d" % hh] for hh in range(H)]
            for oc in range(DC):
                ps = psum[6]
                mms2 = [(wp[kc][:, oc * 128:(oc + 1) * 128],
                         YH[:, kc * 128:(kc + 1) * 128], None, wp_t[kc]) for kc in range(DC)]
                gt2 = mmgroup(P, ps, mms2, deps=ydeps, sub=ps.ap[:, 0:128])
                ev2 = P.op("vector", lambda v, o=stg[:, 0:128], pf=ps.ap[:, 0:128],
                           b=bpcol[:, oc:oc + 1]:
                           v.tensor_scalar(o, pf, b, None, ALU.add),
                           deps=[gt2, t_bp, g.get("_l1_stg")], sig="S_vector")
                dk = P.dma("sync", ydel[oc * 128:(oc + 1) * 128, tci * 128:(tci + 1) * 128],
                           stg[:, 0:128], deps=[ev2])
                ps.consumed = dk
                g["_l1_stg"] = dk
                g["_l1_yh%d" % yh_p] = gt2

    N = len(iters)
    for k in range(N + 1):
        if k < N:
            front(k)
        if k >= 1:
            back(k - 1)
        if k < N:
            middle(k)
    rs_toks = [P.dma("sync", g["XT"][:, c * T:(c + 1) * T], xres[c * 128:(c + 1) * 128, :],
                     deps=[g["_l1_stg"], q_done, k_done]) for c in range(DC)]
    DB = GA.alloc(2 * T)
    db_ld = [None, None]
    for c in range(DC):
        buf = DB[:, (c % 2) * T:(c % 2 + 1) * T]
        lt = P.dma("sync", buf, ydel[c * 128:(c + 1) * 128, :],
                   deps=[g["_l1_stg"], db_ld[c % 2]])
        tk = P.op("vector", lambda v, o=g["XT"][:, c * T:(c + 1) * T], i=buf:
                  v.tensor_tensor(o, o, i, ALU.add), deps=[lt, rs_toks[c]], sig="S_vector")
        db_ld[c % 2] = tk


def fa_racc(g, fa, pp):
    if "_l1_racc" not in g:
        g["_l1_racc"] = fa.alloc(16, F32)
    return g["_l1_racc"][:, pp * 8:(pp + 1) * 8]


def emit_rope(P, g, buf, nblk, tpos0, ready):
    psum = g["psum"]
    SQ = [g["SQ0"], g["SQ1"]]
    ncol = nblk // 4 * 512
    t0 = tpos0 or 0
    last = None
    for ch in range(4):
        for tb in range(ncol // 512):
            col = ch * ncol + tb * 512
            tcol = t0 + tb * 512
            ps = psum[2]
            t1 = P.op("tensor", lambda t, r=buf[:, col:col + 512], pf=ps.ap:
                      t.matmul(pf, g["roperot_sb"], r, start=True, stop=True),
                      deps=[ps.consumed, ready], sig="S_tensor")
            t2 = P.op("vector", lambda v, o=SQ[0], i=buf[:, col:col + 512],
                      cs=g["cos4_sb"][:, tcol:tcol + 512]:
                      v.tensor_tensor(o, i, cs, ALU.mult),
                      deps=[ready, last, g["_cs_tok"], g["_cs_tok2"]], sig="S_vector")
            t3 = P.op("vector", lambda v, o=SQ[1], pf=ps.ap,
                      sn=g["sin4_sb"][:, tcol:tcol + 512]:
                      v.tensor_tensor(o, pf, sn, ALU.mult), deps=[t1, last], sig="S_vector")
            last = P.op("vector", lambda v, o=buf[:, col:col + 512], a=SQ[0], b=SQ[1]:
                        v.tensor_tensor(o, a, b, ALU.add), deps=[t2, t3], sig="S_vector")
            ps.consumed = last
    return last


def emit_L3_ffn(P, g):
    emit_moe(P, g, "L3_ffn", 4096, router_L3, add_xn=False, TBW=256)


def emit_L0_ffn(P, g):
    emit_moe(P, g, "L0_ffn", 2048, router_L0, add_xn=True, TBW=256)


LAYER_EMITTERS = {}


def build_program(layers, weight_names):
    nc = bass.Bass("TRN2", target_bir_lowering=False)
    es = ExitStack()
    g = {}
    xin = nc.dram_tensor("x_in", [DM, T], F32, kind="ExternalInput").ap()
    xout = nc.dram_tensor("x_out", [DM, T], F32, kind="ExternalOutput").ap()
    wdram = {}

    def W(name):
        if name not in wdram:
            shape = list(WEIGHT_SHAPES[name])
            if len(shape) == 0:
                shape = [1]
            if len(shape) == 1:
                shape = [1] + shape
            wdram[name] = nc.dram_tensor(name, shape, F32, kind="ExternalInput").ap()
        return wdram[name]
    g["W"] = W
    scratch = {}

    def dram(name, shape, dtype=F32):
        if name not in scratch:
            scratch[name] = nc.dram_tensor("scr_" + name, list(shape), dtype,
                                           kind="Internal").ap()
        return scratch[name]
    g["dram"] = dram
    g["ones_sb"] = es.enter_context(nc.sbuf_tensor("ones_sb", [128, 128], BF16))[:]
    g["eye_sb"] = es.enter_context(nc.sbuf_tensor("eye_sb", [128, 128], F32))[:]
    g["eyeb_sb"] = es.enter_context(nc.sbuf_tensor("eyeb_sb", [128, 128], BF16))[:]
    g["ones1f_sb"] = es.enter_context(nc.sbuf_tensor("ones1f_sb", [1, 128], F32))[:]
    cones = nc.dram_tensor("c_ones", [128, 128], BF16, kind="ExternalInput").ap()
    ceye = nc.dram_tensor("c_eye", [128, 128], F32, kind="ExternalInput").ap()
    ceyeb = nc.dram_tensor("c_eyeb", [128, 128], BF16, kind="ExternalInput").ap()
    ccols = nc.dram_tensor("c_cols", [128, 16], F32, kind="ExternalInput").ap()
    chm = nc.dram_tensor("c_hmask", [16, 1024], BF16, kind="ExternalInput").ap()
    g["hmask_sb"] = es.enter_context(nc.sbuf_tensor("hmask_sb", [16, 1024], BF16))[:]
    cmf = nc.dram_tensor("c_maskf", [128, 128], F32, kind="ExternalInput").ap()
    g["maskf_sb"] = es.enter_context(nc.sbuf_tensor("maskf_sb", [128, 128], F32))[:]
    g["c_cos4"] = nc.dram_tensor("c_cos4", [128, T], BF16, kind="ExternalInput").ap()
    g["c_sin4"] = nc.dram_tensor("c_sin4", [128, T], BF16, kind="ExternalInput").ap()
    crot = nc.dram_tensor("c_roperot", [128, 128], BF16, kind="ExternalInput").ap()
    g["roperot_sb"] = es.enter_context(nc.sbuf_tensor("roperot_sb", [128, 128], BF16))[:]
    ctri = nc.dram_tensor("c_trilb", [128, 128], BF16, kind="ExternalInput").ap()
    g["trilb_sb"] = es.enter_context(nc.sbuf_tensor("trilb_sb", [128, 128], BF16))[:]
    ctri2 = nc.dram_tensor("c_triub", [128, 128], BF16, kind="ExternalInput").ap()
    g["triub_sb"] = es.enter_context(nc.sbuf_tensor("triub_sb", [128, 128], BF16))[:]
    cmfl = nc.dram_tensor("c_maskfl", [128, 128], F32, kind="ExternalInput").ap()
    g["maskfl_sb"] = es.enter_context(nc.sbuf_tensor("maskfl_sb", [128, 128], F32))[:]
    cpm = nc.dram_tensor("c_pmask", [128, 7 * 128], BF16, kind="ExternalInput").ap()
    g["pmask_sb"] = es.enter_context(nc.sbuf_tensor("pmask_sb", [128, 7 * 128], BF16))[:]
    g["cc_sb"] = es.enter_context(nc.sbuf_tensor("cc_sb", [128, 16], F32))[:]
    g["cc_vals"] = []

    def cc(val):
        val = float(val)
        if val not in g["cc_vals"]:
            assert len(g["cc_vals"]) < 16
            g["cc_vals"].append(val)
        i = g["cc_vals"].index(val)
        return g["cc_sb"][:, i:i + 1]
    g["cc"] = cc
    GAN = 98176 + 4096
    GA = Arena(es.enter_context(nc.sbuf_tensor("GA", [128, GAN], BF16))[:], GAN)
    g["GA"] = g["BA"] = g["FA"] = GA
    g["XT"] = GA.alloc(DC * T, F32)
    g["XTspace"] = GA.ap[:, 0:2 * DC * T]
    g["XN"] = GA.alloc(DC * T)
    g["XNspace"] = GA.ap[:, 2 * DC * T: 3 * DC * T]

    for nm in ("M", "MSQ", "VT", "SD", "R", "MR"):
        if nm == "MSQ":
            g["scr2k"] = GA.ap[:, GA.ptr:GA.ptr + 2048]
        g[nm] = GA.alloc(512, F32)
    g["SQ0"] = GA.alloc(512)
    g["SQ1"] = GA.alloc(512)
    g["colmax"] = GA.alloc(8, F32)
    g["cmax1"] = GA.alloc(1, F32)
    _r2 = GA.alloc(8, F32)
    g["row2"], g["row2b"] = _r2[0:1, 0:4], _r2[0:1, 4:8]
    GA.set_mark()
    g["FAcol"] = Arena(es.enter_context(nc.sbuf_tensor("FAcol", [128, 512], BF16))[:], 512)
    g["psum"] = [Psum(es.enter_context(nc.psum_tensor(f"P{i}", [128, 512], F32))[:])
                 for i in range(8)]
    g["nc"] = nc

    P = Prog()
    g["P"] = P
    ld = []
    for c in range(DC):
        ld.append(P.dma("sync", g["XT"][:, c * T:(c + 1) * T], xin[c * 128:(c + 1) * 128, :]))
    ld.append(P.dma("sync", g["ones_sb"], cones))
    ld.append(P.dma("sync", g["eye_sb"], ceye))
    ld.append(P.dma("sync", g["eyeb_sb"], ceyeb))
    ld.append(P.dma("sync", g["cc_sb"], ccols))
    ld.append(P.dma("sync", g["hmask_sb"], chm))
    ld.append(P.dma("sync", g["maskf_sb"], cmf))
    ld.append(P.dma("sync", g["roperot_sb"], crot))
    ld.append(P.dma("sync", g["trilb_sb"], ctri))
    ld.append(P.dma("sync", g["triub_sb"], ctri2))
    ld.append(P.dma("sync", g["maskfl_sb"], cmfl))
    ld.append(P.dma("sync", g["pmask_sb"], cpm))
    g["ones1_tok"] = P.op("vector", lambda v: v.memset(g["ones1f_sb"], 1.0), sig="S_vector")
    for tok in ld + [g["ones1_tok"]]:
        for e in ENGS:
            P.op(e, None, deps=[tok])

    for i, name in enumerate(layers):
        LAYER_EMITTERS[name](P, g)
        P.barrier()
        g["GA"].reset()
        g["FAcol"].reset()

    sts = []
    for c in range(DC):
        sts.append(P.dma("sync", xout[c * 128:(c + 1) * 128, :], g["XT"][:, c * T:(c + 1) * T]))
    for tok in sts:
        P.op("sync", None, deps=[tok])

    sems = {}
    with ExitStack() as es2:
        for s in sorted(P.sem_names):
            sems[s] = es2.enter_context(nc.semaphore(s))
        block = es2.enter_context(nc.Block())
        P.replay(block, sems)
    es.close()
    return nc, sorted(wdram.keys()), g


WEIGHT_SHAPES = {
    "L0_attn_Wz": (1024, 2048), "L0_attn_Win": (1024, 648),
    "L0_attn_Wzg": (2048, 512), "L0_attn_Wout": (512, 1024),
    "L0_attn_dt_bias": (8,), "L0_attn_A_log": (8,), "L0_attn_D_param": (8,),
    "L0_ffn_rw1": (1024, 1024), "L0_ffn_rb1": (1024,), "L0_ffn_rw2": (1024, 8),
    "L0_ffn_rb2": (8,), "L0_ffn_ew1": (8, 1024, 2048), "L0_ffn_eb1": (8, 2048),
    "L0_ffn_ew2": (8, 2048, 1024), "L0_ffn_eb2": (8, 1024),
    "L1_attn_W_attn": (1024, 3072), "L1_attn_b_attn": (3072,),
    "L1_attn_W_proj": (1024, 1024), "L1_attn_b_proj": (1024,),
    "L1_attn_score_gain": (),
    "L1_ffn_kernel": (1024, 1024), "L1_ffn_bias": (1024,),
    "L1_ffn_a": (1024, 16), "L1_ffn_b": (16, 1024),
    "L2_attn_Wq": (1024, 1024), "L2_attn_Wk": (1024, 1024),
    "L2_attn_Wv": (1024, 1024), "L2_attn_Wo": (1024, 1024),
    "L2_ffn_W_exp": (1024, 4096), "L2_ffn_b_exp": (4096,),
    "L2_ffn_up_weights": (12, 2048, 2, 2), "L2_ffn_up_bias": (4096,),
    "L2_ffn_down_weights": (12, 2048, 2, 2), "L2_ffn_W_con": (4096, 1024),
    "L2_ffn_b_con": (1024,), "L2_ffn_down_bias": (1024,),
    "L3_attn_W_kv": (1024, 256), "L3_attn_g_kv": (256,),
    "L3_attn_W_kdec": (256, 1024), "L3_attn_W_vdec": (256, 1024),
    "L3_attn_W_q": (1024, 512), "L3_attn_W_qdec": (512, 1024),
    "L3_attn_W_krope": (1024, 512), "L3_attn_W_qrope": (512, 512),
    "L3_attn_W_o": (1024, 1024),
    "L3_ffn_gate_w": (1024, 8), "L3_ffn_gate_b": (8,),
    "L3_ffn_W1": (8, 1024, 4096), "L3_ffn_b1": (8, 4096),
    "L3_ffn_W2": (8, 4096, 1024), "L3_ffn_b2": (8, 1024),
}

ALL_LAYERS = ["L0_attn", "L0_ffn", "L1_attn", "L1_ffn",
              "L2_attn", "L2_ffn", "L3_attn", "L3_ffn"]

LAYER_EMITTERS["L1_ffn"] = emit_L1_ffn
LAYER_EMITTERS["L3_ffn"] = emit_L3_ffn
LAYER_EMITTERS["L0_ffn"] = emit_L0_ffn
LAYER_EMITTERS["L2_attn"] = emit_L2_attn
LAYER_EMITTERS["L3_attn"] = emit_L3_attn
LAYER_EMITTERS["L1_attn"] = emit_L1_attn
LAYER_EMITTERS["L0_attn"] = emit_L0_attn
LAYER_EMITTERS["L2_ffn"] = emit_L2_ffn


def _pmasks():
    pm = np.zeros((128, 7 * 128), dtype=ml_dtypes.bfloat16)
    for st in range(7):
        for p in range(128):
            pm[p, st * 128 + (p ^ (1 << st))] = 1
    return pm


def _consts_np():
    hm = np.zeros((16, 1024), dtype=ml_dtypes.bfloat16)
    for c in range(8):
        for p in range(128):
            hm[2 * c + p // 64, c * 128 + p] = 1
    mf = np.triu(np.full((128, 128), -1e9, dtype=np.float32), 1)
    ROPE = 32
    inv_freq = 1.0 / 10000.0 ** (np.arange(0, ROPE, 2, dtype=np.float32) / ROPE)
    tt = np.arange(T, dtype=np.float32)
    emb = np.concatenate([np.outer(tt, inv_freq)] * 2, -1)
    cos1, sin1 = np.cos(emb).T, np.sin(emb).T
    cos4 = np.tile(cos1, (4, 1)).astype(ml_dtypes.bfloat16)
    sin4 = np.tile(sin1, (4, 1)).astype(ml_dtypes.bfloat16)
    rot = np.zeros((128, 128), dtype=np.float32)
    for hh in range(4):
        b = hh * 32
        for i in range(16):
            rot[b + i + 16, b + i] = -1.0
            rot[b + i, b + i + 16] = 1.0
    return {
        "c_ones": np.ones((128, 128), dtype=ml_dtypes.bfloat16),
        "c_eye": np.eye(128, dtype=np.float32),
        "c_eyeb": np.eye(128, dtype=ml_dtypes.bfloat16),
        "c_hmask": hm,
        "c_maskf": mf,
        "c_cos4": cos4,
        "c_sin4": sin4,
        "c_roperot": rot.astype(ml_dtypes.bfloat16),
        "c_trilb": np.tril(np.ones((128, 128))).astype(ml_dtypes.bfloat16),
        "c_triub": np.triu(np.ones((128, 128))).astype(ml_dtypes.bfloat16),
        "c_maskfl": np.tril(np.full((128, 128), -1e9, dtype=np.float32), -1),
        "c_pmask": _pmasks(),
    }


def run_layers(layers, x, inputs):
    nc, wnames, g = build_program(layers, None)
    consts = _consts_np()
    cvals = np.zeros((128, 16), dtype=np.float32)
    for i, v in enumerate(g["cc_vals"]):
        cvals[:, i] = v
    consts["c_cols"] = cvals
    in_maps = []
    for ci in range(NCORES):
        m = {"x_in": np.ascontiguousarray(x[ci].T)}
        for wn in wnames:
            a = np.asarray(inputs[wn], dtype=np.float32)
            if a.ndim == 0:
                a = a.reshape(1, 1)
            elif a.ndim == 1:
                a = a.reshape(1, -1)
            m[wn] = a
        m.update(consts)
        in_maps.append(m)
    res = run_bass_kernel_spmd(nc, in_maps, core_ids=list(range(NCORES)))
    out = np.stack([np.ascontiguousarray(r["x_out"].T) for r in res.results])
    return out


def kernel(**inputs):
    x = np.asarray(inputs["L0_attn_x"], dtype=np.float32)
    return run_layers(ALL_LAYERS, x, inputs)
